# version 45
# baseline (speedup 1.0000x reference)
import contextlib
import numpy as np
import ml_dtypes

import concourse.bass as bass
import concourse.mybir as mybir
from concourse.bass_utils import run_bass_kernel_spmd

F32 = mybir.dt.float32
BF16 = mybir.dt.bfloat16
I32 = mybir.dt.int32
AF = mybir.ActivationFunctionType
ALU = mybir.AluOpType
AX = mybir.AxisListType

BF = ml_dtypes.bfloat16

N_CORES = 8
SEQ = 16384
DM = 1024
TCORE = SEQ // N_CORES
HALO = 8
TLOC = TCORE + HALO
SUBW = 512
SUBS = [(0, SUBW), (SUBW, SUBW), (2 * SUBW, SUBW), (3 * SUBW, SUBW), (4 * SUBW, HALO)]

ENGINES = ("tensor", "vector", "scalar", "gpsimd", "sync")


class Sched:

    def __init__(self):
        self.ops = []
        self.eops = {e: [] for e in ENGINES}
        self.acc = {}
        self._fence = []

    def fence(self):
        lasts = {}
        qd = {}
        for op in self.ops:
            if op["dma"]:
                qd.setdefault(op["eng"], []).append(op["id"])
            else:
                lasts[op["eng"]] = op["id"]
        ids = set(lasts.values())
        for lst in qd.values():
            ids.update(lst[-8:])
        self._fence = sorted(ids)
        self.acc = {}

    def op(self, eng, fn, r=(), w=(), dma=False, deps=(), tag=None):
        opid = len(self.ops)
        dep = set(deps)
        dep.update(self._fence)
        for (b, lo, hi) in r:
            for (k, l2, h2, oid) in self.acc.get(b, ()):
                if k == "w" and l2 < hi and lo < h2:
                    dep.add(oid)
        for (b, lo, hi) in w:
            for (k, l2, h2, oid) in self.acc.get(b, ()):
                if l2 < hi and lo < h2:
                    dep.add(oid)
        for (b, lo, hi) in r:
            self.acc.setdefault(b, []).append(("r", lo, hi, opid))
        for (b, lo, hi) in w:
            recs = self.acc.setdefault(b, [])
            recs[:] = [rec for rec in recs if not (lo <= rec[1] and rec[2] <= hi)]
            recs.append(("w", lo, hi, opid))
        op = dict(id=opid, eng=eng, fn=fn, dma=dma, deps=sorted(dep),
                  inc=False, tag=tag)
        self.ops.append(op)
        self.eops[eng].append(op)
        return opid

    def mark_inc(self, opids):
        for oid in opids:
            self.ops[oid]["inc"] = True

    def build(self, nc, stack):
        for op in self.ops:
            if op["dma"]:
                op["inc"] = True
            for d in op["deps"]:
                dop = self.ops[d]
                if dop["eng"] == op["eng"] and not dop["dma"] and op["eng"] == "tensor":
                    continue
                dop["inc"] = True
        NQ = 8
        cum = {}
        qcnt = {}
        domains = set()
        for op in self.ops:
            if op["dma"]:
                qi = qcnt.get(op["eng"], 0)
                qcnt[op["eng"]] = qi + 1
                dom = "q_%s_%d" % (op["eng"], qi % NQ)
            else:
                dom = op["eng"]
            op["dom"] = dom
            domains.add(dom)
            if op["inc"]:
                cum[dom] = cum.get(dom, 0) + (16 if op["dma"] else 1)
            op["cum"] = cum.get(dom, 0)
        sems = {d: stack.enter_context(nc.semaphore(f"s_{d}")) for d in sorted(domains)}

        know = {}
        ek = {e: {} for e in ENGINES}
        waitplan = {}
        nwaits = 0
        for op in self.ops:
            e = op["eng"]
            needs = {}
            for d in op["deps"]:
                dop = self.ops[d]
                dom, val = dop["dom"], dop["cum"]
                if dom == e and not dop["dma"] and e == "tensor":
                    continue
                if ek[e].get(dom, 0) < val:
                    needs[dom] = max(needs.get(dom, 0), val)
            if op["dma"] and op["inc"]:
                prev = op["cum"] - 16
                if prev > 0 and ek[e].get(op["dom"], 0) < prev:
                    needs[op["dom"]] = max(needs.get(op["dom"], 0), prev)
            for dom, val in needs.items():
                ek[e][dom] = max(ek[e].get(dom, 0), val)
            for d in op["deps"]:
                for dom2, v2 in know.get(d, {}).items():
                    if ek[e].get(dom2, 0) < v2:
                        ek[e][dom2] = v2
            waitplan[op["id"]] = sorted(needs.items())
            nwaits += len(needs)
            k = dict(ek[e])
            k[op["dom"]] = max(k.get(op["dom"], 0), op["cum"])
            know[op["id"]] = k
        self.stats = dict(nops=len(self.ops), nwaits=nwaits,
                          nincs=sum(1 for o in self.ops if o["inc"]))

        block = stack.enter_context(nc.Block())

        def mk(e):
            ops = self.eops[e]

            def body(eng_obj):
                for op in ops:
                    for dom, val in waitplan[op["id"]]:
                        eng_obj.wait_ge(sems[dom], val)
                    inst = op["fn"](eng_obj)
                    if op["inc"]:
                        inst.then_inc(sems[op["dom"]], 16 if op["dma"] else 1)
            return body

        for e in ENGINES:
            if self.eops[e]:
                getattr(block, e)(mk(e))
        return sems


class TBuf:
    def __init__(self, stack, nc, name, shape, dtype, psum=False):
        mk = nc.psum_tensor if psum else nc.sbuf_tensor
        self.h = stack.enter_context(mk(name, list(shape), dtype))
        self.name = name
        self.shape = shape

    def __getitem__(self, sl):
        return self.h[sl]

    def t(self, lo, hi):
        return (self.name, lo, hi)

    @property
    def full(self):
        return (self.name, 0, self.shape[-1])


class _ViewH:

    def __init__(self, typed, col0, ncols):
        self._typed, self._c0, self._n = typed, col0, ncols

    def ap(self):
        return self._typed[0:128, self._c0:self._c0 + self._n]


class TView:

    def __init__(self, typed, col0, name, shape):
        self.name = name
        self.shape = shape
        self._typed = typed
        self._c0 = col0
        self.h = _ViewH(typed, col0, shape[-1])

    def __getitem__(self, sl):
        p, c = sl
        p0 = p.start or 0
        p1 = self.shape[0] if p.stop is None else p.stop
        c0 = (c.start or 0) + self._c0
        c1 = (self.shape[-1] if c.stop is None else c.stop) + self._c0
        return self._typed[p0:p1, c0:c1]

    def t(self, lo, hi):
        return (self.name, lo, hi)

    @property
    def full(self):
        return (self.name, 0, self.shape[-1])


class Arena:

    def __init__(self, stack, nc, nbytes, name="ARENA"):
        self.nbytes = nbytes
        self.h8 = stack.enter_context(nc.sbuf_tensor(name, [128, nbytes], mybir.dt.uint8))
        self._typed = {}
        self.off = 0
        self.top = nbytes

    def reset(self):
        self.off = 0

    def carve_top(self, name, shape, dtype):
        esz = mybir.dt.size(dtype)
        per_part = 1
        for s in shape[1:]:
            per_part *= s
        per_part *= esz
        off = (self.top - per_part) & ~31
        assert off >= self.off, f"arena OOM (top) at {name}"
        self.top = off
        return TView(self.typed(dtype), off // esz, name, shape)

    def typed(self, dtype):
        if dtype not in self._typed:
            self._typed[dtype] = self.h8.bitcast(dtype)
        return self._typed[dtype]

    def carve(self, name, shape, dtype):
        esz = mybir.dt.size(dtype)
        per_part = 1
        for s in shape[1:]:
            per_part *= s
        per_part *= esz
        off = (self.off + 31) & ~31
        assert off + per_part <= self.top, \
            f"arena OOM at {name}: need {per_part}, have {self.top - off}"
        tv = TView(self.typed(dtype), off // esz, name, shape)
        self.off = off + per_part
        return tv


DEBUG_L0 = False


L0_HI_SPECS = [
        ("toff", [1, 1], F32), ("iotaw0", [128, SUBW], F32),
        ("wzt", [DM, 2048], F32), ("wx", [96, 2048], F32), ("wdtt", [64, 2048], F32),
        ("convw", [128, 64], F32), ("convb", [128, 16], F32),
        ("bdt", [128, 16], F32), ("alog", [2048, 16], F32), ("dskip", [128, 16], F32),
        ("wot", [2048, DM], F32), ("sigr", [64, DM], F32), ("knots", [1, 8], F32),
        ("temp", [1, 1], F32), ("cw1", [DM, 256], F32), ("cb1", [128, 2], F32),
        ("cw2", [256, 2], F32), ("cb2", [2, 1], F32), ("sw1", [64, 32], F32),
        ("sb1", [64, 16], F32), ("sw2", [64, 16], F32), ("sb2", [64, 1], F32),
        ("dirs", [64, DM], F32), ("gam", [128, 8], F32), ("bet", [128, 8], F32),
        ("osc", [1, 1], F32), ("idf", [128, 128], F32), ("idb", [128, 128], BF16),
        ("sel01", [2, 64], BF16),
]


def emit_l0_inner(S, nc, B, x_hbm, x1_hbm, HI):

    def T(name, lo, hi):
        return (name, lo, hi)

    def dma(dst_ap, src_ap, r=(), w=(), tag=None):
        return S.op("sync", lambda E, d=dst_ap, s=src_ap: E.dma_start(d, s),
                    r=r, w=w, dma=True, tag=tag)

    def gdma(dst_ap, src_ap, r=(), w=(), tag=None):
        return S.op("gpsimd", lambda E, d=dst_ap, s=src_ap: E.dma_start(d, s),
                    r=r, w=w, dma=True, tag=tag)

    stage = B["stage"]
    dbc = B["dbc"]
    dma(stage[0:96, 0:2048], HI["wx"][:, :], w=[stage.t(0, 2048)])
    S.op("vector", lambda E: E.reduce_sum(dbc[0:96, :], stage[0:96, 0:2048], axis=AX.X),
         r=[stage.t(0, 2048)], w=[dbc.full])
    gdma(B["xn"][0:64, 0:2048], HI["wdtt"][:, :], w=[("xn", 0, 2048)])
    for c in range(8):
        gdma(B["wzt_b"][:, c * 2048:(c + 1) * 2048], HI["wzt"][c * 128:(c + 1) * 128, :],
             w=[T("wzt_b", c * 2048, (c + 1) * 2048)])
    for c in range(16):
        gdma(B["wot_b"][:, c * 1024:(c + 1) * 1024], HI["wot"][c * 128:(c + 1) * 128, :],
             w=[T("wot_b", c * 1024, (c + 1) * 1024)])
    for c in range(8):
        gdma(B["cw1_b"][:, c * 256:(c + 1) * 256], HI["cw1"][c * 128:(c + 1) * 128, :],
             w=[T("cw1_b", c * 256, (c + 1) * 256)])
    for c in range(2):
        gdma(B["cw2_b"][:, c * 2:(c + 1) * 2], HI["cw2"][c * 128:(c + 1) * 128, :],
             w=[T("cw2_b", c * 2, (c + 1) * 2)])

    simple = [("bdt_s", "bdt"),  ("dskip_s", "dskip"),
              ("knots_s", "knots"), ("itmp", "temp"), ("gam_s", "gam"), ("bet_s", "bet"),
              ("idf_s", "idf"), ("idb_s", "idb"), ("osc_s", "osc"), ("toff_s", "toff"),
              ("sel01_s", "sel01"), ("iotaw", "iotaw0"),
              ("cb1_sb", "cb1"), ("cb2_s", "cb2")]
    for dst, src in simple:
        d = B[dst]
        dma(d[:, :], HI[src][:, :], w=[d.full])
    x_sub = B["x_sub"]
    dma(x_sub[0:64, 0:DM], HI["sigr"][:, :], w=[("x_sub", 0, DM)])
    dma(x_sub[:, 1280:1344], HI["convw"][:, :], w=[("x_sub", 1280, 1344)])
    dma(x_sub[:, 1344:1360], HI["convb"][:, :], w=[("x_sub", 1344, 1360)])
    for c in range(16):
        dma(x_sub[:, DM + c * 16:DM + (c + 1) * 16], HI["alog"][c * 128:(c + 1) * 128, :],
            w=[("x_sub", DM + c * 16, DM + (c + 1) * 16)])
    for nm_src, nm_dst, wdt in [("sw1", "sw1_b", 32), ("sb1", "sb1_b", 16),
                                ("sw2", "sw2_b", 16), ("sb2", "sb2_b", 1),
                                ("dirs", "dirs_b", DM)]:
        dstb = B[nm_dst]
        gdma(dstb[0:64, 0:wdt], HI[nm_src][:, :], w=[dstb.full])

    for nm, val in [("onesb", 1.0), ("ones1f", 1.0), ("ones1b", 1.0), ("ones16b", 1.0),
                    ("epsc", 1e-5), ("mhalf", -0.5), ("mone", -1.0),
                    ("rsq2", 0.7071067811865476), ("onef", 1.0), ("c256", 256.0)]:
        S.op("vector", lambda E, nm=nm, val=val: E.memset(B[nm][:, :], val), w=[B[nm].full])
    S.op("gpsimd", lambda E: E.iota(B["iota8"][:, :], [[1, 8]], base=0, channel_multiplier=0,
                                    allow_small_or_imprecise_dtypes=True), w=[B["iota8"].full])
    S.op("gpsimd", lambda E: E.iota(B["iota64"][:, :], [[1, 1]], base=0, channel_multiplier=1,
                                    allow_small_or_imprecise_dtypes=True), w=[B["iota64"].full])

    S.op("vector", lambda E: E.tensor_scalar(B["iota8n"][:, :], B["iota8"][:, :], 1.0, -1000.0,
                                             ALU.mult, ALU.add),
         r=[B["iota8"].full], w=[B["iota8n"].full])

    PTf, PTb = B["PTf"], B["PTb"]

    def bcast_row(dst_ap, dst_w, src_ap, src_tag, width):
        S.op("tensor", lambda E, d=dst_ap, s=src_ap, width=width: E.matmul(
            PTf[:, 0:width], B["ones1f"][0:1, :], s, start=True, stop=True),
            r=[B["ones1f"].full, src_tag], w=[T("PTf", 0, width)])
        S.op("vector", lambda E, d=dst_ap, width=width: E.tensor_copy(d, PTf[:, 0:width]),
             r=[T("PTf", 0, width)], w=[dst_w])

    knots, dk_row = B["knots_s"], B["dk_row"]
    S.op("vector", lambda E: E.tensor_tensor(dk_row[0:1, :], knots[0:1, 1:8], knots[0:1, 0:7],
                                             ALU.subtract),
         r=[knots.full], w=[dk_row.full])
    bcast_row(B["kn_b"][:, :], B["kn_b"].full, knots[0:1, :], knots.full, 8)
    bcast_row(B["dk_b"][:, :], B["dk_b"].full, dk_row[0:1, :], dk_row.full, 7)
    itmp, it2 = B["itmp"], B["it2"]
    S.op("vector", lambda E: E.tensor_scalar(itmp[0:1, :], itmp[0:1, :], 1e-6, None, ALU.add),
         r=[itmp.full], w=[itmp.full])
    S.op("vector", lambda E: E.reciprocal(it2[0:1, 0:1], itmp[0:1, :]), r=[itmp.full], w=[it2.t(0, 1)])
    S.op("vector", lambda E: E.tensor_tensor(it2[0:1, 1:2], itmp[0:1, 0:1], it2[0:1, 0:1], ALU.mult),
         r=[itmp.full, it2.t(0, 1)], w=[it2.t(1, 2)])
    S.op("vector", lambda E: E.tensor_scalar(it2[0:1, 1:2], it2[0:1, 1:2], -1.0, 2.0, ALU.mult, ALU.add),
         r=[it2.t(1, 2)], w=[it2.t(1, 2)])
    S.op("vector", lambda E: E.tensor_tensor(itmp[0:1, 0:1], it2[0:1, 0:1], it2[0:1, 1:2], ALU.mult),
         r=[it2.full], w=[itmp.full])
    bcast_row(B["it_b"][:, :], B["it_b"].full, itmp[0:1, :], itmp.full, 1)
    bcast_row(B["toff_b"][:, :], B["toff_b"].full, B["toff_s"][0:1, :], B["toff_s"].full, 1)
    bcast_row(B["osc_b"][:, :], B["osc_b"].full, B["osc_s"][0:1, :], B["osc_s"].full, 1)

    S.op("tensor", lambda E: E.transpose(PTf[0:1, 0:32], dbc[64:96, 0:1], B["idf_s"][64:96, 64:96]),
         r=[dbc.full, T("idf_s", 0, 128)], w=[T("PTf", 0, 32)])
    S.op("vector", lambda E: E.tensor_copy(B["bcrow"][0:1, :], PTf[0:1, 0:32]),
         r=[T("PTf", 0, 32)], w=[B["bcrow"].full])
    bcast_row(B["bb"][:, :], B["bb"].full, B["bcrow"][0:1, 0:16], B["bcrow"].t(0, 16), 16)
    bcast_row(B["cb"][:, :], B["cb"].full, B["bcrow"][0:1, 16:32], B["bcrow"].t(16, 32), 16)

    S.op("scalar", lambda E: E.copy(B["dltb"][0:64, 0:1], dbc[0:64, 0:1]),
         r=[dbc.full], w=[B["dltb"].full])
    P4 = B["P4"]
    for c in range(16):
        S.op("tensor", lambda E, c=c: E.matmul(P4[:, c:c + 1], B["xn"][0:64, c * 128:(c + 1) * 128],
                                               B["dltb"][0:64, 0:1], start=True, stop=True),
             r=[T("xn", c * 128, (c + 1) * 128), B["dltb"].full], w=[T("P4", c, c + 1)])
    tmp16, tmp16c, dcol = B["tmp16"], B["tmp16c"], B["dcol"]
    S.op("vector", lambda E: E.tensor_tensor(tmp16[:, :], P4[:, 0:16], B["bdt_s"][:, :], ALU.add),
         r=[T("P4", 0, 16), B["bdt_s"].full], w=[tmp16.full])
    S.op("scalar", lambda E: E.activation(tmp16c[:, :], tmp16[:, :], AF.Abs),
         r=[tmp16.full], w=[tmp16c.full])
    S.op("scalar", lambda E: E.activation(tmp16c[:, :], tmp16c[:, :], AF.Exp, scale=B["mone"][:, 0:1]),
         r=[tmp16c.full, B["mone"].full], w=[tmp16c.full])
    S.op("scalar", lambda E: E.activation(tmp16c[:, :], tmp16c[:, :], AF.Ln, bias=B["onef"][:, 0:1]),
         r=[tmp16c.full, B["onef"].full], w=[tmp16c.full])
    S.op("vector", lambda E: E.tensor_scalar(tmp16[:, :], tmp16[:, :], 0.0, None, ALU.max),
         r=[tmp16.full], w=[tmp16.full])
    S.op("vector", lambda E: E.tensor_tensor(dcol[:, :], tmp16[:, :], tmp16c[:, :], ALU.add),
         r=[tmp16.full, tmp16c.full], w=[dcol.full])

    a_all, cw_all, base_all = B["a_all"], B["cw_all"], B["base_all"]
    for c in range(16):
        asl = T("a_all", c * 16, (c + 1) * 16)
        S.op("scalar", lambda E, c=c: E.activation(a_all[:, c * 16:(c + 1) * 16],
                                                   x_sub[:, DM + c * 16:DM + (c + 1) * 16], AF.Exp),
             r=[T("x_sub", DM + c * 16, DM + (c + 1) * 16)], w=[asl])
        S.op("vector", lambda E, c=c: E.tensor_scalar(a_all[:, c * 16:(c + 1) * 16],
                                                      a_all[:, c * 16:(c + 1) * 16],
                                                      dcol[:, c:c + 1], -1.0, ALU.mult, ALU.mult),
             r=[asl, dcol.full], w=[asl])
        S.op("scalar", lambda E, c=c: E.activation(tmp16[:, :], a_all[:, c * 16:(c + 1) * 16], AF.Exp),
             r=[asl], w=[tmp16.full])
        S.op("vector", lambda E: E.tensor_scalar(tmp16[:, :], tmp16[:, :], -1.0, 1.0, ALU.mult, ALU.add),
             r=[tmp16.full], w=[tmp16.full])
        S.op("vector", lambda E: E.reciprocal(tmp16[:, :], tmp16[:, :]), r=[tmp16.full], w=[tmp16.full])
        S.op("vector", lambda E, c=c: E.tensor_scalar(tmp16c[:, :], B["bb"][:, :], dcol[:, c:c + 1],
                                                      None, ALU.mult),
             r=[B["bb"].full, dcol.full], w=[tmp16c.full])
        S.op("vector", lambda E: E.tensor_tensor(tmp16c[:, :], tmp16c[:, :], tmp16[:, :], ALU.mult),
             r=[tmp16c.full, tmp16.full], w=[tmp16c.full])
        csl = T("cw_all", c * 16, (c + 1) * 16)
        S.op("vector", lambda E, c=c: E.tensor_tensor(cw_all[:, c * 16:(c + 1) * 16], tmp16c[:, :],
                                                      B["cb"][:, :], ALU.mult),
             r=[tmp16c.full, B["cb"].full], w=[csl])
        S.op("vector", lambda E, c=c: E.reduce_sum(base_all[:, c:c + 1],
                                                   cw_all[:, c * 16:(c + 1) * 16], axis=AX.X),
             r=[csl], w=[T("base_all", c, c + 1)])
        S.op("vector", lambda E, c=c: E.tensor_tensor(base_all[:, c:c + 1], base_all[:, c:c + 1],
                                                      B["dskip_s"][:, c:c + 1], ALU.add),
             r=[T("base_all", c, c + 1), B["dskip_s"].full], w=[T("base_all", c, c + 1)])
    S.op("scalar", lambda E: E.activation(B["sgn"][:, 0:256], cw_all[:, 0:256], AF.Abs),
         r=[T("cw_all", 0, 256)], w=[B["sgn"].full])
    S.op("scalar", lambda E: E.activation(B["lnK"][:, 0:256], B["sgn"][:, 0:256], AF.Ln),
         r=[B["sgn"].full], w=[B["lnK"].full])
    S.op("vector", lambda E: E.tensor_scalar(B["sgn"][:, 0:256], cw_all[:, 0:256], 0.0, None, ALU.is_gt),
         r=[T("cw_all", 0, 256)], w=[B["sgn"].full])
    S.op("vector", lambda E: E.tensor_scalar(B["sgn"][:, 0:256], B["sgn"][:, 0:256], 2.0, -1.0,
                                             ALU.mult, ALU.add),
         r=[B["sgn"].full], w=[B["sgn"].full])
    S.op("scalar", lambda E: E.activation(B["dA256"][:, 0:256], a_all[:, 0:256], AF.Exp,
                                          scale=B["c256"][:, 0:1]),
         r=[T("a_all", 0, 256), B["c256"].full], w=[B["dA256"].full])
    S.op("vector", lambda E: E.tensor_tensor(B["sgd"][:, 0:256], B["sgn"][:, 0:256],
                                             B["dA256"][:, 0:256], ALU.mult),
         r=[B["sgn"].full, B["dA256"].full], w=[B["sgd"].full])
    for k8 in range(4):
        S.op("gpsimd", lambda E, k8=k8: E.tensor_copy(B["eye8"][:, k8 * 128:(k8 + 1) * 128],
                                                      B["idb_s"][:, :]),
             r=[T("idb_s", 0, 128)], w=[T("eye8", k8 * 128, (k8 + 1) * 128)])

    sigr, sigq_b = B["x_sub"], B["sigq_b"]
    grpwk, grpwk2 = B["grpwk"], B["grpwk2"]
    S.op("vector", lambda E: E.tensor_scalar(B["h1x"][0:64, 0:DM], sigr[0:64, 0:DM], 0.3, None, ALU.is_gt),
         r=[("x_sub", 0, DM)], w=[T("h1x", 0, DM)])
    S.op("vector", lambda E: E.tensor_scalar(B["h1e"][0:64, 0:SUBW], sigr[0:64, 0:SUBW], -0.3, None, ALU.is_lt),
         r=[("x_sub", 0, DM)], w=[T("h1e", 0, SUBW)])
    S.op("vector", lambda E: E.tensor_scalar(B["ybf"][0:64, 0:SUBW], sigr[0:64, SUBW:DM], -0.3, None, ALU.is_lt),
         r=[("x_sub", 0, DM)], w=[T("ybf", 0, SUBW)])
    S.op("vector", lambda E: E.tensor_tensor(sigq_b[0:64, 0:SUBW], B["h1x"][0:64, 0:SUBW],
                                             B["h1e"][0:64, 0:SUBW], ALU.subtract),
         r=[T("h1x", 0, SUBW), T("h1e", 0, SUBW)], w=[sigq_b.t(0, SUBW)])
    S.op("vector", lambda E: E.tensor_tensor(sigq_b[0:64, SUBW:DM], B["h1x"][0:64, SUBW:DM],
                                             B["ybf"][0:64, 0:SUBW], ALU.subtract),
         r=[T("h1x", SUBW, DM), T("ybf", 0, SUBW)], w=[sigq_b.t(SUBW, DM)])
    for c in range(8):
        S.op("tensor", lambda E, c=c: E.transpose(PTb[:, 0:64], sigq_b[0:64, c * 128:(c + 1) * 128],
                                                  B["idb_s"][0:64, 0:64]),
             r=[sigq_b.full, T("idb_s", 0, 128)], w=[T("PTb", 0, 64)])
        S.op("vector", lambda E, c=c: E.tensor_copy(B["sigsT"][:, c * 64:(c + 1) * 64], PTb[:, 0:64]),
             r=[T("PTb", 0, 64)], w=[T("sigsT", c * 64, (c + 1) * 64)])
    S.op("gpsimd", lambda E: E.iota(grpwk[0:64, 0:8], [[-8, 8]], base=0, channel_multiplier=1,
                                    allow_small_or_imprecise_dtypes=True), w=[grpwk.full])
    S.op("vector", lambda E: E.tensor_scalar(grpwk2[0:64, :], grpwk[0:64, :], -0.5, None, ALU.is_ge),
         r=[grpwk.full], w=[grpwk2.full])
    S.op("vector", lambda E: E.tensor_scalar(grpwk[0:64, :], grpwk[0:64, :], 7.5, None, ALU.is_le),
         r=[grpwk.full], w=[grpwk.full])
    S.op("vector", lambda E: E.tensor_tensor(grpwk[0:64, :], grpwk[0:64, :], grpwk2[0:64, :], ALU.mult),
         r=[grpwk.full, grpwk2.full], w=[grpwk.full])
    S.op("scalar", lambda E: E.copy(B["grp_b"][0:64, :], grpwk[0:64, :]),
         r=[grpwk.full], w=[B["grp_b"].full])
    csig_b = B["csig_b"]
    P5 = B["P5"]
    for half in range(2):
        lo = half * 512
        S.op("tensor", lambda E, lo=lo: E.matmul(P5[0:8, 0:512], B["grp_b"][0:64, 0:8],
                                                 sigq_b[0:64, lo:lo + 512], start=True, stop=True),
             r=[B["grp_b"].full, sigq_b.full], w=[T("P5", 0, 512)])
        S.op("vector", lambda E, lo=lo: E.tensor_scalar(B["h1x"][0:8, 0:512], P5[0:8, 0:512],
                                                        0.0, None, ALU.is_gt),
             r=[T("P5", 0, 512)], w=[T("h1x", 0, 512)])
        S.op("vector", lambda E, lo=lo: E.tensor_scalar(B["h1e"][0:8, 0:512], P5[0:8, 0:512],
                                                        0.0, None, ALU.is_lt),
             r=[T("P5", 0, 512)], w=[T("h1e", 0, 512)])
        S.op("vector", lambda E, lo=lo: E.tensor_tensor(csig_b[0:8, lo:lo + 512], B["h1x"][0:8, 0:512],
                                                        B["h1e"][0:8, 0:512], ALU.subtract),
             r=[T("h1x", 0, 512), T("h1e", 0, 512)], w=[csig_b.t(lo, lo + 512)])
    for c in range(8):
        S.op("tensor", lambda E, c=c: E.transpose(PTb[:, 0:8], csig_b[0:8, c * 128:(c + 1) * 128],
                                                  B["idb_s"][0:8, 0:8]),
             r=[csig_b.full, T("idb_s", 0, 128)], w=[T("PTb", 0, 8)])
        S.op("vector", lambda E, c=c: E.tensor_copy(B["csigT"][:, c * 8:(c + 1) * 8], PTb[:, 0:8]),
             r=[T("PTb", 0, 8)], w=[T("csigT", c * 8, (c + 1) * 8)])

    DB = {}
    if DEBUG_L0:
        DB["xn"] = nc.dram_tensor("xn_dbg", [DM, TLOC], BF16, kind="ExternalOutput")
        DB["y"] = nc.dram_tensor("y_dbg", [2048, TLOC], BF16, kind="ExternalOutput")
        DB["s"] = nc.dram_tensor("s_dbg", [DM, TLOC], BF16, kind="ExternalOutput")
        DB["tidx"] = nc.dram_tensor("tidx_dbg", [1, TLOC], BF16, kind="ExternalOutput")
        DB["gy"] = nc.dram_tensor("gy_dbg", [DM, TLOC], F32, kind="ExternalOutput")
        DB["gates"] = nc.dram_tensor("gates_dbg", [72, TLOC], F32, kind="ExternalOutput")
    for si, (ls, w) in enumerate(SUBS):
        emit_l0_sub(S, nc, B, x_hbm, x1_hbm, si, ls, w, DB)
        if si == 0:
            S.op("gpsimd", lambda E: E.iota(B["iotaw"][:, :], [[1, SUBW]], base=0,
                                            channel_multiplier=0,
                                            allow_small_or_imprecise_dtypes=True),
                 w=[B["iotaw"].full])




def emit_l0_sub(S, nc, B, x_hbm, x1_hbm, si, ls, w, DB=None):
    DB = DB or {}

    def T(name, lo, hi):
        return (name, lo, hi)

    def dbg_chunked(key, buf, nch):
        if key not in DB:
            return
        d = DB[key].rearrange("(c p) t -> p c t", p=128)[:, :, ls:ls + w]
        s = buf.h.ap().rearrange("p (c t) -> p c t", c=nch)[:, :, 0:w]
        S.op("sync", lambda E, d=d, s=s: E.dma_start(d, s), r=[buf.full], dma=True, tag="out_dma")

    def dbg_flat(key, src_ap, r_tag, rows):
        if key not in DB:
            return
        d = DB[key][0:rows, ls:ls + w]
        S.op("sync", lambda E, d=d, s=src_ap: E.dma_start(d, s), r=[r_tag], dma=True, tag="out_dma")

    P0, P1, P2, P3, P4, P5 = (B[k] for k in ["P0", "P1", "P2", "P3", "P4", "P5"])
    PTf = B["PTf"]
    x_sub, xbf, xn, yf = B["x_sub"], B["xbf"], B["xn"], B["yf"]
    pt = B["pt"]
    row_m = B["row_m"]
    row_v = B["dirsel"]
    row_r = B["dirsel"]
    row_rb, row_mb = B["row_rb"], B["row_mb"]
    iotaw = B["iotaw"]
    iotaw_tag = B["iotaw"].full

    for hf in range(4):
        srch = x_hbm.rearrange("(c p) t -> p c t", p=128)[:, hf * 2:(hf + 1) * 2, ls:ls + w]
        dsth = x_sub.h.ap().rearrange("p (c t) -> p c t", c=8)[:, hf * 2:(hf + 1) * 2, 0:w]
        S.op("sync", lambda E, d=dsth, s=srch: E.dma_start(d, s),
             w=[("x_sub", hf * 2 * SUBW, (hf + 1) * 2 * SUBW)], dma=True)

    a_all, cw_all, base_all = B["a_all"], B["cw_all"], B["base_all"]
    ball2, tv, y_all = B["ball2"], B["tv"], B["y_all"]
    S.op("vector", lambda E, ls=ls: E.tensor_scalar(tv[:, :], B["toff_b"][:, :], float(ls + 1),
                                                    None, ALU.add),
         r=[B["toff_b"].full], w=[tv.full])
    ew = min(w, 264)
    lnK, sgn, sgd, diagb, eye8 = B["lnK"], B["sgn"], B["sgd"], B["diagb"], B["eye8"]
    S.op("vector", lambda E, ls=ls: E.tensor_scalar(tv[:, :], B["toff_b"][:, :], float(ls + 1),
                                                    None, ALU.add),
         r=[B["toff_b"].full], w=[tv.full])
    bdeps = B.setdefault("_mamba_bdep", {0: [], 1: []})
    ball_all = B["cw_all"]
    S.op("vector", lambda E: E.tensor_scalar(ball_all[:, 0:256], a_all[:, 0:256],
                                             tv[:, 0:1], None, ALU.mult),
         r=[T("a_all", 0, 256), tv.full], w=[T("cw_all", 0, 256)])
    S.op("vector", lambda E: E.tensor_tensor(ball_all[:, 0:256], ball_all[:, 0:256],
                                             lnK[:, 0:256], ALU.add),
         r=[T("cw_all", 0, 256), lnK.full], w=[T("cw_all", 0, 256)])

    def emit_mamba(c0, c1):
        for c in range(c0, c1):
            par = c % 2
            bs = par * 16
            PSY = B["P4"] if par == 0 else B["P5"]
            psy = "P4" if par == 0 else "P5"
            S.op("vector", lambda E, c=c, bs=bs: E.tensor_scalar(ball2[:, bs:bs + 16],
                                                                 a_all[:, c * 16:(c + 1) * 16],
                                                                 tv[:, 0:1], None, ALU.mult),
                 r=[T("a_all", c * 16, (c + 1) * 16), tv.full], w=[T("ball2", bs, bs + 16)])
            S.op("vector", lambda E, c=c, bs=bs: E.tensor_tensor(ball2[:, bs:bs + 16],
                                                                 ball2[:, bs:bs + 16],
                                                                 lnK[:, c * 16:(c + 1) * 16], ALU.add),
                 r=[T("ball2", bs, bs + 16), lnK.full], w=[T("ball2", bs, bs + 16)])
            lastmm = None
            for hh in (0, 1):
                n0 = hh * 8
                hdep = list(bdeps[par])
                eids = []
                for n in range(n0, n0 + 8):
                    col = c * 16 + n
                    sl = (par * 16 + n) * 264
                    eid = S.op("scalar", lambda E, col=col, sl=sl: E.activation(
                        pt[:, sl:sl + ew], iotaw[:, 0:ew], AF.Exp,
                        scale=a_all[:, col:col + 1], bias=ball_all[:, col:col + 1]),
                        r=[iotaw_tag, T("a_all", col, col + 1), T("cw_all", col, col + 1)],
                        w=[T("pt", sl, sl + ew)], deps=hdep)
                    eids.append(eid)
                edep = [eids[-1]]
                do = hh * 1024
                for q4 in range(2):
                    S.op("vector", lambda E, c=c, n0=n0, do=do, q4=q4: E.tensor_tensor(
                        diagb[:, do + q4 * 512:do + (q4 + 1) * 512].rearrange("p (n k) -> p n k", n=4),
                        eye8[:, 0:512].rearrange("p (n k) -> p n k", n=4),
                        sgn[:, c * 16 + n0 + q4 * 4:c * 16 + n0 + q4 * 4 + 4]
                           .rearrange("p (n one) -> p n one", one=1)
                           .broadcast_to([128, 4, 128]),
                        ALU.mult),
                        r=[T("eye8", 0, 512), sgn.full], w=[T("diagb", do + q4 * 512, do + (q4 + 1) * 512)])
                for n in range(n0, n0 + 8):
                    sl = (par * 16 + n) * 264
                    dgo = do + (n % 8) * 128
                    lastmm = S.op("tensor", lambda E, sl=sl, dgo=dgo, n=n, PSY=PSY: E.matmul(
                        PSY[:, 0:ew], diagb[:, dgo:dgo + 128], pt[:, sl:sl + ew],
                        start=(n == 0), stop=(n == 15)),
                        r=[T("diagb", dgo, dgo + 128), T("pt", sl, sl + ew)],
                        w=[T(psy, 0, ew)], deps=edep)
            if w > 264:
                for hh in (0, 1):
                    n0 = hh * 8
                    do = hh * 1024
                    for q4 in range(2):
                        S.op("vector", lambda E, c=c, n0=n0, do=do, q4=q4: E.tensor_tensor(
                            diagb[:, do + q4 * 512:do + (q4 + 1) * 512].rearrange("p (n k) -> p n k", n=4),
                            eye8[:, 0:512].rearrange("p (n k) -> p n k", n=4),
                            sgd[:, c * 16 + n0 + q4 * 4:c * 16 + n0 + q4 * 4 + 4]
                               .rearrange("p (n one) -> p n one", one=1)
                               .broadcast_to([128, 4, 128]),
                            ALU.mult),
                            r=[T("eye8", 0, 512), sgd.full], w=[T("diagb", do + q4 * 512, do + (q4 + 1) * 512)])
                    for n in range(n0, n0 + 8):
                        sl = (par * 16 + n) * 264
                        dgo = do + (n % 8) * 128
                        lastmm = S.op("tensor", lambda E, sl=sl, dgo=dgo, n=n, PSY=PSY: E.matmul(
                            PSY[:, 264:w], diagb[:, dgo:dgo + 128], pt[:, sl + 8:sl + 256],
                            start=(n == 0), stop=(n == 15)),
                            r=[T("diagb", dgo, dgo + 128), T("pt", sl + 8, sl + 256)],
                            w=[T(psy, 264, w)])
            S.op("vector", lambda E, c=c, PSY=PSY, psy=psy: E.tensor_scalar(
                y_all[:, c * SUBW:c * SUBW + w], PSY[:, 0:w],
                -1.0, base_all[:, c:c + 1], ALU.mult, ALU.add),
                r=[T(psy, 0, w), T("base_all", c, c + 1)],
                w=[T("y_all", c * SUBW, c * SUBW + w)])
            bdeps[par] = [lastmm]

    emit_mamba(0, 8)

    for c in range(8):
        sl = (c % 2) * SUBW
        S.op("scalar", lambda E, c=c, sl=sl: E.copy(xbf[:, sl:sl + w], x_sub[:, c * SUBW:c * SUBW + w]),
             r=[T("x_sub", c * SUBW, c * SUBW + w)], w=[T("xbf", sl, sl + w)])
        S.op("gpsimd", lambda E, sl=sl: E.tensor_tensor(pt[:, sl:sl + w], xbf[:, sl:sl + w],
                                                        xbf[:, sl:sl + w], ALU.mult),
             r=[T("xbf", sl, sl + w)], w=[T("pt", sl, sl + w)])
        S.op("tensor", lambda E, c=c, sl=sl: E.matmul(P4[0:1, 0:w], B["onesb"][:, :], xbf[:, sl:sl + w],
                                                      start=(c == 0), stop=(c == 7)),
             r=[T("xbf", sl, sl + w), B["onesb"].full], w=[T("P4", 0, w)])
        S.op("tensor", lambda E, c=c, sl=sl: E.matmul(P5[0:1, 0:w], B["onesb"][:, :], pt[:, sl:sl + w],
                                                      start=(c == 0), stop=(c == 7)),
             r=[T("pt", sl, sl + w), B["onesb"].full], w=[T("P5", 0, w)])

    def ln_rows():
        S.op("vector", lambda E: E.tensor_scalar(row_m[0:1, 0:w], P4[0:1, 0:w], 1.0 / DM, None, ALU.mult),
             r=[T("P4", 0, w)], w=[T("row_m", 0, w)])
        S.op("vector", lambda E: E.tensor_scalar(row_v[0:1, SUBW:SUBW + w], P5[0:1, 0:w], 1.0 / DM, None, ALU.mult),
             r=[T("P5", 0, w)], w=[T("dirsel", SUBW, SUBW + w)])
        S.op("vector", lambda E: E.tensor_tensor(row_r[0:1, 0:w], row_m[0:1, 0:w], row_m[0:1, 0:w],
                                                 ALU.mult),
             r=[T("row_m", 0, w)], w=[T("dirsel", 0, w)])
        S.op("vector", lambda E: E.tensor_tensor(row_v[0:1, SUBW:SUBW + w], row_v[0:1, SUBW:SUBW + w], row_r[0:1, 0:w],
                                                 ALU.subtract),
             r=[T("dirsel", SUBW, SUBW + w), T("dirsel", 0, w)], w=[T("dirsel", SUBW, SUBW + w)])
        S.op("scalar", lambda E: E.activation(row_r[0:1, 0:w], row_v[0:1, SUBW:SUBW + w], AF.Ln,
                                              bias=B["epsc"][0:1, 0:1]),
             r=[T("dirsel", SUBW, SUBW + w), B["epsc"].full], w=[T("dirsel", 0, w)])
        S.op("scalar", lambda E: E.activation(row_r[0:1, 0:w], row_r[0:1, 0:w], AF.Exp,
                                              scale=B["mhalf"][0:1, 0:1]),
             r=[T("dirsel", 0, w), B["mhalf"].full], w=[T("dirsel", 0, w)])
        S.op("tensor", lambda E: E.matmul(P4[:, 0:w], B["ones1f"][0:1, :], row_m[0:1, 0:w],
                                          start=True, stop=True),
             r=[T("row_m", 0, w), B["ones1f"].full], w=[T("P4", 0, w)])
        S.op("tensor", lambda E: E.matmul(P5[:, 0:w], B["ones1f"][0:1, :], row_r[0:1, 0:w],
                                          start=True, stop=True),
             r=[T("dirsel", 0, w), B["ones1f"].full], w=[T("P5", 0, w)])

    ln_rows()
    xnf = B["xnf"]
    for c in range(8):
        S.op("vector", lambda E, c=c: E.tensor_tensor(xnf[:, c * SUBW:c * SUBW + w],
                                                      x_sub[:, c * SUBW:c * SUBW + w],
                                                      P4[:, 0:w], ALU.subtract),
             r=[T("x_sub", c * SUBW, c * SUBW + w), T("P4", 0, w)], w=[T("stage", c * SUBW, c * SUBW + w)])
        S.op("vector", lambda E, c=c: E.tensor_tensor(xnf[:, c * SUBW:c * SUBW + w],
                                                      xnf[:, c * SUBW:c * SUBW + w],
                                                      P5[:, 0:w], ALU.mult),
             r=[T("stage", c * SUBW, c * SUBW + w), T("P5", 0, w)], w=[T("stage", c * SUBW, c * SUBW + w)])
        S.op("gpsimd", lambda E, c=c: E.tensor_copy(xn[:, c * SUBW:c * SUBW + w],
                                                    xnf[:, c * SUBW:c * SUBW + w]),
             r=[T("stage", c * SUBW, c * SUBW + w)], w=[T("xn", c * SUBW, c * SUBW + w)])
    dbg_chunked("xn", xn, 8)

    emit_mamba(8, 16)
    dbg_chunked("y", y_all, 16)

    zs2, u_all = B["zs2"], B["y_all"]
    for c in range(16):
        pz = ("P0", "P1")[c % 2]
        PZ = B[pz]
        for k in range(8):
            S.op("tensor", lambda E, c=c, k=k, PZ=PZ: E.matmul(
                PZ[:, 0:w], B["wzt_b"][:, k * 2048 + c * 128:k * 2048 + (c + 1) * 128],
                xn[:, k * SUBW:k * SUBW + w], start=(k == 0), stop=(k == 7)),
                r=[T("wzt_b", k * 2048 + c * 128, k * 2048 + (c + 1) * 128),
                   T("xn", k * SUBW, k * SUBW + w)],
                w=[T(pz, 0, w)])
        ssl = (c % 2) * SUBW
        S.op("scalar", lambda E, pz=pz, PZ=PZ, ssl=ssl: E.activation(zs2[:, ssl:ssl + w],
                                                                      PZ[:, 0:w], AF.Silu),
             r=[T(pz, 0, w)], w=[T("zs2", ssl, ssl + w)])
        S.op("gpsimd", lambda E, c=c, ssl=ssl: E.tensor_tensor(u_all[:, c * SUBW:c * SUBW + w],
                                                               y_all[:, c * SUBW:c * SUBW + w],
                                                               zs2[:, ssl:ssl + w], ALU.mult),
             r=[T("y_all", c * SUBW, c * SUBW + w), T("zs2", ssl, ssl + w)],
             w=[T("y_all", c * SUBW, c * SUBW + w)])

    s_all = B["s_all"]
    for dc in range(8):
        pz = ("P2", "P3")[dc % 2]
        PZ = B[pz]
        for ec in range(16):
            S.op("tensor", lambda E, dc=dc, ec=ec, PZ=PZ: E.matmul(
                PZ[:, 0:w], B["wot_b"][:, ec * DM + dc * 128:ec * DM + (dc + 1) * 128],
                u_all[:, ec * SUBW:ec * SUBW + w], start=(ec == 0), stop=(ec == 15)),
                r=[T("wot_b", ec * DM + dc * 128, ec * DM + (dc + 1) * 128),
                   T("y_all", ec * SUBW, ec * SUBW + w)],
                w=[T(pz, 0, w)])
        S.op("scalar", lambda E, dc=dc, pz=pz, PZ=PZ: E.copy(s_all[:, dc * SUBW:dc * SUBW + w],
                                                             PZ[:, 0:w]),
             r=[T(pz, 0, w)], w=[T("s_all", dc * SUBW, dc * SUBW + w)])
    dbg_chunked("s", s_all, 8)

    u7, seg, gates = B["dirsel"], B["dirsel"], B["gates"]
    for k in range(8):
        S.op("tensor", lambda E, k=k: E.matmul(
            P2[0:64, 0:w], B["sigsT"][:, k * 64:(k + 1) * 64],
            xnf[:, k * SUBW:k * SUBW + w],
            start=(k == 0), stop=(k == 7)),
            r=[T("sigsT", k * 64, (k + 1) * 64), T("stage", k * SUBW, k * SUBW + w)],
            w=[T("P2", 0, w)])
    for k in range(8):
        S.op("tensor", lambda E, k=k: E.matmul(
            P2[64:72, 0:w], B["csigT"][:, k * 8:(k + 1) * 8],
            xnf[:, k * SUBW:k * SUBW + w],
            start=(k == 0), stop=(k == 7)),
            r=[T("csigT", k * 8, (k + 1) * 8), T("stage", k * SUBW, k * SUBW + w)],
            w=[T("P2", 0, w)])
    S.op("scalar", lambda E: E.activation(u7[0:72, SUBW:SUBW + w], P2[0:72, 0:w], AF.Sigmoid,
                                          scale=B["it_b"][0:72, 0:1]),
         r=[T("P2", 0, w), B["it_b"].full], w=[T("dirsel", SUBW, SUBW + w)])
    S.op("vector", lambda E: E.tensor_scalar(u7[0:72, SUBW:SUBW + w], u7[0:72, SUBW:SUBW + w], 7.0, None, ALU.mult),
         r=[T("dirsel", SUBW, SUBW + w)], w=[T("dirsel", SUBW, SUBW + w)])
    for k in range(7):
        if k == 0:
            S.op("vector", lambda E: E.tensor_scalar(seg[0:72, 0:w], u7[0:72, SUBW:SUBW + w], 0.0, 1.0,
                                                     ALU.max, ALU.min),
                 r=[T("dirsel", SUBW, SUBW + w)], w=[T("dirsel", 0, w)])
            S.op("vector", lambda E: E.tensor_scalar(gates[0:72, 0:w], seg[0:72, 0:w],
                                                     B["dk_b"][0:72, 0:1], B["kn_b"][0:72, 0:1],
                                                     ALU.mult, ALU.add),
                 r=[T("dirsel", 0, w), B["dk_b"].full, B["kn_b"].full], w=[T("gates", 0, w)])
        else:
            S.op("vector", lambda E, k=k: E.tensor_scalar(seg[0:72, 0:w], u7[0:72, SUBW:SUBW + w],
                                                          float(-k), None, ALU.add),
                 r=[T("dirsel", SUBW, SUBW + w)], w=[T("dirsel", 0, w)])
            S.op("vector", lambda E: E.tensor_scalar(seg[0:72, 0:w], seg[0:72, 0:w], 0.0, 1.0,
                                                     ALU.max, ALU.min),
                 r=[T("dirsel", 0, w)], w=[T("dirsel", 0, w)])
            S.op("vector", lambda E, k=k: E.scalar_tensor_tensor(gates[0:72, 0:w], seg[0:72, 0:w],
                                                                 B["dk_b"][0:72, k:k + 1],
                                                                 gates[0:72, 0:w], ALU.mult, ALU.add),
                 r=[T("dirsel", 0, w), T("gates", 0, w), B["dk_b"].full], w=[T("gates", 0, w)])
    dbg_flat("gates", gates[0:72, 0:w], ("gates", 0, w), 72)

    nblk = (w + 127) // 128
    gcT, tgT, rmax, eq8, midx = B["gcT"], B["tgT"], B["rmax"], B["eq8"], B["midx"]
    cidx, ohc, selg, lidx, tidx = B["cidx"], B["ohc"], B["selg"], B["lidx"], B["tidx"]
    trowb = B["trowb"]
    if nblk == 4:
        for b in range(4):
            S.op("tensor", lambda E, b=b: E.transpose(PTf[0:128, 0:8],
                                                      gates[64:72, b * 128:(b + 1) * 128],
                                                      B["idf_s"][64:72, 64:72]),
                 r=[T("gates", 0, w), T("idf_s", 0, 128)], w=[T("PTf", 0, 8)])
            S.op("vector", lambda E, b=b: E.tensor_copy(gcT[0:128, b * 8:(b + 1) * 8], PTf[0:128, 0:8]),
                 r=[T("PTf", 0, 8)], w=[T("gcT", b * 8, (b + 1) * 8)])
            S.op("tensor", lambda E, b=b: E.transpose(PTf[0:128, 0:64],
                                                      gates[0:64, b * 128:(b + 1) * 128],
                                                      B["idf_s"][0:64, 0:64]),
                 r=[T("gates", 0, w), T("idf_s", 0, 128)], w=[T("PTf", 0, 64)])
            S.op("vector", lambda E, b=b: E.tensor_copy(tgT[0:128, b * 64:(b + 1) * 64],
                                                        PTf[0:128, 0:64]),
                 r=[T("PTf", 0, 64)], w=[T("tgT", b * 64, (b + 1) * 64)])
            S.op("vector", lambda E, b=b: E.tensor_copy(yf[:, 388 + b * 8:388 + (b + 1) * 8],
                                                        B["iota8"][0:128, 0:8]),
                 r=[B["iota8"].full], w=[T("stage", 388 + b * 8, 388 + (b + 1) * 8)])
            S.op("vector", lambda E, b=b: E.tensor_copy(yf[:, 420 + b * 8:420 + (b + 1) * 8],
                                                        B["iota8n"][0:128, 0:8]),
                 r=[B["iota8n"].full], w=[T("stage", 420 + b * 8, 420 + (b + 1) * 8)])
        r3 = lambda ap, b=4, g=8: ap.rearrange("p (b g) -> p b g", b=b)
        c3 = lambda ap: ap.rearrange("p (b one) -> p b one", one=1)
        S.op("vector", lambda E: E.reduce_max(c3(rmax[0:128, 0:4]), r3(gcT[0:128, 0:32]), axis=AX.X),
             r=[T("gcT", 0, 32)], w=[T("rmax", 0, 4)])
        S.op("vector", lambda E: E.tensor_tensor(
            r3(yf[:, 288:320]), r3(gcT[0:128, 0:32]),
            c3(rmax[0:128, 0:4]).broadcast_to([128, 4, 8]), ALU.is_ge),
            r=[T("gcT", 0, 32), T("rmax", 0, 4)], w=[T("stage", 288, 320)])
        S.op("vector", lambda E: E.tensor_tensor(r3(yf[:, 320:352]), r3(yf[:, 288:320]),
                                                 r3(yf[:, 420:452]), ALU.mult),
             r=[T("stage", 288, 320), T("stage", 420, 452)], w=[T("stage", 320, 352)])
        S.op("vector", lambda E: E.tensor_scalar(yf[:, 320:352], yf[:, 320:352], 1000.0, None, ALU.add),
             r=[T("stage", 320, 352)], w=[T("stage", 320, 352)])
        S.op("vector", lambda E: E.tensor_reduce(c3(cidx[0:128, 0:4]), r3(yf[:, 320:352]),
                                                 axis=AX.X, op=ALU.min),
             r=[T("stage", 320, 352)], w=[T("cidx", 0, 4)])
        S.op("vector", lambda E: E.tensor_tensor(
            r3(yf[:, 288:320]), r3(yf[:, 388:420]),
            c3(cidx[0:128, 0:4]).broadcast_to([128, 4, 8]), ALU.is_equal),
            r=[T("stage", 388, 420), T("cidx", 0, 4)], w=[T("stage", 288, 320)])
        S.op("vector", lambda E: E.tensor_tensor(
            yf[:, 0:256].rearrange("p (b g j) -> p b g j", b=4, g=8),
            tgT[0:128, 0:256].rearrange("p (b g j) -> p b g j", b=4, g=8),
            yf[:, 288:320].rearrange("p (b g one) -> p b g one", b=4, g=8)
              .broadcast_to([128, 4, 8, 8]),
            ALU.mult),
            r=[T("tgT", 0, 256), T("stage", 288, 320)], w=[T("stage", 0, 256)])
        S.op("vector", lambda E: E.tensor_reduce(
            yf[:, 352:384].rearrange("p (b j one) -> p b j one", b=4, j=8),
            yf[:, 0:256].rearrange("p (b g j) -> p b j g", b=4, g=8),
            axis=AX.X, op=ALU.add),
            r=[T("stage", 0, 256)], w=[T("stage", 352, 384)])
        S.op("vector", lambda E: E.reduce_max(c3(yf[:, 384:388]), r3(yf[:, 352:384]), axis=AX.X),
             r=[T("stage", 352, 384)], w=[T("stage", 384, 388)])
        S.op("vector", lambda E: E.tensor_tensor(
            r3(yf[:, 288:320]), r3(yf[:, 352:384]),
            c3(yf[:, 384:388]).broadcast_to([128, 4, 8]), ALU.is_ge),
            r=[T("stage", 352, 388)], w=[T("stage", 288, 320)])
        S.op("vector", lambda E: E.tensor_tensor(r3(yf[:, 320:352]), r3(yf[:, 288:320]),
                                                 r3(yf[:, 420:452]), ALU.mult),
             r=[T("stage", 288, 320), T("stage", 420, 452)], w=[T("stage", 320, 352)])
        S.op("vector", lambda E: E.tensor_scalar(yf[:, 320:352], yf[:, 320:352], 1000.0, None, ALU.add),
             r=[T("stage", 320, 352)], w=[T("stage", 320, 352)])
        S.op("vector", lambda E: E.tensor_reduce(c3(yf[:, 384:388]), r3(yf[:, 320:352]),
                                                 axis=AX.X, op=ALU.min),
             r=[T("stage", 320, 352)], w=[T("stage", 384, 388)])
        S.op("vector", lambda E: E.scalar_tensor_tensor(tidx[0:128, 0:4], cidx[0:128, 0:4], 8.0,
                                                        yf[:, 384:388], ALU.mult, ALU.add),
             r=[T("cidx", 0, 4), T("stage", 384, 388)], w=[T("tidx", 0, 4)])
        for b in range(4):
            S.op("tensor", lambda E, b=b: E.transpose(PTf[0:1, 0:128], tidx[0:128, b:b + 1],
                                                      B["idf_s"][0:128, 0:128]),
                 r=[T("tidx", 0, 4), T("idf_s", 0, 128)], w=[T("PTf", 0, 128)])
            S.op("vector", lambda E, b=b: E.tensor_copy(trowb[0:1, b * 128:(b + 1) * 128],
                                                        PTf[0:1, 0:128]),
                 r=[T("PTf", 0, 128)], w=[T("trowb", b * 128, (b + 1) * 128)])
    else:
     for b in range(nblk):
        bw = min(128, w - b * 128)
        S.op("tensor", lambda E, b=b, bw=bw: E.transpose(PTf[0:bw, 0:8],
                                                         gates[64:72, b * 128:b * 128 + bw],
                                                         B["idf_s"][64:72, 64:72]),
             r=[T("gates", 0, w), T("idf_s", 0, 128)], w=[T("PTf", 0, 8)])
        S.op("vector", lambda E, b=b, bw=bw: E.tensor_copy(gcT[0:bw, b * 8:(b + 1) * 8], PTf[0:bw, 0:8]),
             r=[T("PTf", 0, 8)], w=[T("gcT", b * 8, (b + 1) * 8)])
        S.op("tensor", lambda E, b=b, bw=bw: E.transpose(PTf[0:bw, 0:64],
                                                         gates[0:64, b * 128:b * 128 + bw],
                                                         B["idf_s"][0:64, 0:64]),
             r=[T("gates", 0, w), T("idf_s", 0, 128)], w=[T("PTf", 0, 64)])
        S.op("vector", lambda E, b=b, bw=bw: E.tensor_copy(tgT[0:bw, b * 64:(b + 1) * 64],
                                                           PTf[0:bw, 0:64]),
             r=[T("PTf", 0, 64)], w=[T("tgT", b * 64, (b + 1) * 64)])
        S.op("vector", lambda E, b=b, bw=bw: E.reduce_max(rmax[0:bw, b:b + 1],
                                                          gcT[0:bw, b * 8:(b + 1) * 8], axis=AX.X),
             r=[T("gcT", b * 8, (b + 1) * 8)], w=[T("rmax", b, b + 1)])
        S.op("vector", lambda E, b=b, bw=bw: E.tensor_scalar(eq8[0:bw, :],
                                                             gcT[0:bw, b * 8:(b + 1) * 8],
                                                             rmax[0:bw, b:b + 1], None, ALU.is_ge),
             r=[T("gcT", b * 8, (b + 1) * 8), T("rmax", b, b + 1)], w=[eq8.full])
        S.op("vector", lambda E, bw=bw: E.tensor_tensor(midx[0:bw, :], eq8[0:bw, :],
                                                        B["iota8n"][0:bw, :], ALU.mult),
             r=[eq8.full, B["iota8n"].full], w=[midx.full])
        S.op("vector", lambda E, bw=bw: E.tensor_scalar(midx[0:bw, :], midx[0:bw, :], 1000.0, None, ALU.add),
             r=[midx.full], w=[midx.full])
        S.op("vector", lambda E, b=b, bw=bw: E.tensor_reduce(cidx[0:bw, b:b + 1], midx[0:bw, :],
                                                             axis=AX.X, op=ALU.min),
             r=[midx.full], w=[T("cidx", b, b + 1)])
        S.op("vector", lambda E, b=b, bw=bw: E.tensor_scalar(ohc[0:bw, :], B["iota8"][0:bw, :],
                                                             cidx[0:bw, b:b + 1], None, ALU.is_equal),
             r=[B["iota8"].full, T("cidx", b, b + 1)], w=[ohc.full])
        for gi in range(8):
            if gi == 0:
                S.op("vector", lambda E, b=b, bw=bw, gi=gi: E.tensor_scalar(
                    selg[0:bw, :], tgT[0:bw, b * 64 + gi * 8:b * 64 + (gi + 1) * 8],
                    ohc[0:bw, gi:gi + 1], None, ALU.mult),
                    r=[T("tgT", b * 64, (b + 1) * 64), ohc.full], w=[selg.full])
            else:
                S.op("vector", lambda E, b=b, bw=bw, gi=gi: E.scalar_tensor_tensor(
                    selg[0:bw, :], tgT[0:bw, b * 64 + gi * 8:b * 64 + (gi + 1) * 8],
                    ohc[0:bw, gi:gi + 1], selg[0:bw, :], ALU.mult, ALU.add),
                    r=[T("tgT", b * 64, (b + 1) * 64), ohc.full, selg.full], w=[selg.full])
        S.op("vector", lambda E, bw=bw: E.reduce_max(lidx[0:bw, 0:1], selg[0:bw, :], axis=AX.X),
             r=[selg.full], w=[lidx.full])
        S.op("vector", lambda E, bw=bw: E.tensor_scalar(eq8[0:bw, :], selg[0:bw, :],
                                                        lidx[0:bw, 0:1], None, ALU.is_ge),
             r=[selg.full, lidx.full], w=[eq8.full])
        S.op("vector", lambda E, bw=bw: E.tensor_tensor(midx[0:bw, :], eq8[0:bw, :],
                                                        B["iota8n"][0:bw, :], ALU.mult),
             r=[eq8.full, B["iota8n"].full], w=[midx.full])
        S.op("vector", lambda E, bw=bw: E.tensor_scalar(midx[0:bw, :], midx[0:bw, :], 1000.0, None, ALU.add),
             r=[midx.full], w=[midx.full])
        S.op("vector", lambda E, bw=bw: E.tensor_reduce(lidx[0:bw, 0:1], midx[0:bw, :],
                                                        axis=AX.X, op=ALU.min),
             r=[midx.full], w=[lidx.full])
        S.op("vector", lambda E, b=b, bw=bw: E.scalar_tensor_tensor(tidx[0:bw, b:b + 1],
                                                                    cidx[0:bw, b:b + 1], 8.0,
                                                                    lidx[0:bw, 0:1], ALU.mult, ALU.add),
             r=[T("cidx", b, b + 1), lidx.full], w=[T("tidx", b, b + 1)])
        S.op("tensor", lambda E, b=b, bw=bw: E.transpose(PTf[0:1, 0:bw], tidx[0:bw, b:b + 1],
                                                         B["idf_s"][0:bw, 0:bw]),
             r=[T("tidx", b, b + 1), T("idf_s", 0, 128)], w=[T("PTf", 0, 128)])
        S.op("vector", lambda E, b=b, bw=bw: E.tensor_copy(trowb[0:1, b * 128:b * 128 + bw],
                                                           PTf[0:1, 0:bw]),
             r=[T("PTf", 0, 128)], w=[T("trowb", b * 128, b * 128 + bw)])
    dbg_flat("tidx", trowb[0:1, 0:w], ("trowb", 0, w), 1)

    ohT = B["ohT"]
    S.op("tensor", lambda E: E.matmul(P3[0:64, 0:w], B["ones1b"][0:1, 0:64], trowb[0:1, 0:w],
                                      start=True, stop=True),
         r=[T("trowb", 0, w), B["ones1b"].full], w=[T("P3", 0, w)])
    S.op("vector", lambda E: E.tensor_scalar(ohT[0:64, 0:w], P3[0:64, 0:w], B["iota64"][0:64, 0:1],
                                             None, ALU.is_equal),
         r=[T("P3", 0, w), B["iota64"].full], w=[T("ohT", 0, w)])

    w1s, b1s, w2s, b2s = B["w1s"], B["b1s"], B["w2s"], B["b2s"]
    for (lhs_ap, lhs_tag, dstb, dlo, rows) in [
        (B["sw1_b"][0:64, 0:16], B["sw1_b"].full, w1s, 0, 16),
        (B["sw1_b"][0:64, 16:32], B["sw1_b"].full, w1s, SUBW, 16),
        (B["sb1_b"][0:64, 0:16], B["sb1_b"].full, b1s, 0, 16),
        (B["sw2_b"][0:64, 0:16], B["sw2_b"].full, w2s, 0, 16),
        (B["sb2_b"][0:64, 0:1], B["sb2_b"].full, b2s, 0, 1),
    ]:
        S.op("tensor", lambda E, lhs=lhs_ap, rows=rows: E.matmul(P3[0:rows, 0:w], lhs,
                                                                 ohT[0:64, 0:w], start=True, stop=True),
             r=[lhs_tag, T("ohT", 0, w)], w=[T("P3", 0, w)])
        if dstb in (w2s, b2s):
            S.op("vector", lambda E, dstb=dstb, dlo=dlo, rows=rows: E.tensor_scalar(
                dstb[0:rows, dlo:dlo + w], P3[0:rows, 0:w], B["osc_b"][0:rows, 0:1], None, ALU.mult),
                r=[T("P3", 0, w), B["osc_b"].full], w=[dstb.t(dlo, dlo + w)])
        else:
            S.op("scalar", lambda E, dstb=dstb, dlo=dlo, rows=rows: E.copy(dstb[0:rows, dlo:dlo + w],
                                                                          P3[0:rows, 0:w]),
                 r=[T("P3", 0, w)], w=[dstb.t(dlo, dlo + w)])

    h1x, h1e = B["h1x"], B["h1e"]
    h1b = h1x
    for hc in range(2):
        for k in range(8):
            S.op("tensor", lambda E, hc=hc, k=k: E.matmul(
                P3[:, 0:w], B["cw1_b"][:, k * 256 + hc * 128:k * 256 + (hc + 1) * 128],
                xn[:, k * SUBW:k * SUBW + w], start=(k == 0), stop=(k == 7)),
                r=[T("cw1_b", k * 256 + hc * 128, k * 256 + (hc + 1) * 128),
                   T("xn", k * SUBW, k * SUBW + w)], w=[T("P3", 0, w)])
        S.op("vector", lambda E, hc=hc: E.tensor_scalar(h1x[:, hc * SUBW:hc * SUBW + w], P3[:, 0:w],
                                                        1.0, B["cb1_sb"][:, hc:hc + 1],
                                                        ALU.mult, ALU.add),
             r=[T("P3", 0, w), B["cb1_sb"].full], w=[T("h1x", hc * SUBW, hc * SUBW + w)])
        S.op("scalar", lambda E, hc=hc: E.activation(h1e[:, 0:w], h1x[:, hc * SUBW:hc * SUBW + w],
                                                     AF.Erf, scale=B["rsq2"][:, 0:1]),
             r=[T("h1x", hc * SUBW, hc * SUBW + w), B["rsq2"].full], w=[h1e.full])
        S.op("vector", lambda E: E.tensor_scalar(h1e[:, 0:w], h1e[:, 0:w], 1.0, 0.5, ALU.add, ALU.mult),
             r=[h1e.full], w=[h1e.full])
        S.op("vector", lambda E, hc=hc: E.tensor_tensor(h1b[:, hc * SUBW:hc * SUBW + w],
                                                        h1x[:, hc * SUBW:hc * SUBW + w],
                                                        h1e[:, 0:w], ALU.mult),
             r=[T("h1x", hc * SUBW, hc * SUBW + w), h1e.full], w=[T("h1x", hc * SUBW, hc * SUBW + w)])
    for hc in range(2):
        S.op("tensor", lambda E, hc=hc: E.matmul(P3[0:2, 0:w], B["cw2_b"][:, hc * 2:(hc + 1) * 2],
                                                 h1b[:, hc * SUBW:hc * SUBW + w],
                                                 start=(hc == 0), stop=(hc == 1)),
             r=[T("cw2_b", hc * 2, (hc + 1) * 2), T("h1x", hc * SUBW, hc * SUBW + w)],
             w=[T("P3", 0, w)])
    comp2b = B["comp2b"]
    S.op("scalar", lambda E: E.activation(comp2b[0:2, 0:w], P3[0:2, 0:w], AF.Tanh,
                                          bias=B["cb2_s"][0:2, 0:1]),
         r=[T("P3", 0, w), B["cb2_s"].full], w=[T("comp2b", 0, w)])
    S.op("tensor", lambda E: E.matmul(P0[0:16, 0:w], B["sel01_s"][0:2, 0:16], comp2b[0:2, 0:w],
                                      start=True, stop=True),
         r=[B["sel01_s"].full, T("comp2b", 0, w)], w=[T("P0", 0, w)])
    S.op("tensor", lambda E: E.matmul(P1[0:16, 0:w], B["sel01_s"][0:2, 32:48], comp2b[0:2, 0:w],
                                      start=True, stop=True),
         r=[B["sel01_s"].full, T("comp2b", 0, w)], w=[T("P1", 0, w)])
    hh1, hh2, hhp = B["hh1"], B["hh2"], B["hhp"]
    S.op("vector", lambda E: E.tensor_tensor(hh1[0:16, 0:w], P0[0:16, 0:w], w1s[0:16, 0:w], ALU.mult),
         r=[T("P0", 0, w), T("w1s", 0, w)], w=[T("hh1", 0, w)])
    S.op("vector", lambda E: E.tensor_tensor(hh2[0:16, 0:w], P1[0:16, 0:w],
                                             w1s[0:16, SUBW:SUBW + w], ALU.mult),
         r=[T("P1", 0, w), T("w1s", SUBW, SUBW + w)], w=[T("hh2", 0, w)])
    S.op("vector", lambda E: E.tensor_tensor(hh1[0:16, 0:w], hh1[0:16, 0:w], hh2[0:16, 0:w], ALU.add),
         r=[T("hh1", 0, w), T("hh2", 0, w)], w=[T("hh1", 0, w)])
    S.op("vector", lambda E: E.tensor_tensor(hh1[0:16, 0:w], hh1[0:16, 0:w], b1s[0:16, 0:w], ALU.add),
         r=[T("hh1", 0, w), T("b1s", 0, w)], w=[T("hh1", 0, w)])
    S.op("vector", lambda E: E.tensor_scalar(hh1[0:16, 0:w], hh1[0:16, 0:w], 0.0, None, ALU.max),
         r=[T("hh1", 0, w)], w=[T("hh1", 0, w)])
    S.op("vector", lambda E: E.tensor_tensor(hhp[0:16, 0:w], hh1[0:16, 0:w], w2s[0:16, 0:w], ALU.mult),
         r=[T("hh1", 0, w), T("w2s", 0, w)], w=[T("hhp", 0, w)])
    S.op("tensor", lambda E: E.matmul(P3[0:1, 0:w], B["ones16b"][0:16, 0:1], hhp[0:16, 0:w],
                                      start=True, stop=True),
         r=[T("hhp", 0, w), B["ones16b"].full], w=[T("P3", 0, w)])
    magb = B["magb"]
    S.op("vector", lambda E: E.tensor_tensor(magb[0:1, 0:w], P3[0:1, 0:w], b2s[0:1, 0:w], ALU.add),
         r=[T("P3", 0, w), T("b2s", 0, w)], w=[T("magb", 0, w)])
    S.op("tensor", lambda E: E.matmul(P3[:, 0:w], B["ones1b"][0:1, :], magb[0:1, 0:w],
                                      start=True, stop=True),
         r=[T("magb", 0, w), B["ones1b"].full], w=[T("P3", 0, w)])

    dirsel, ybf = B["dirsel"], B["ybf"]
    for dc in range(8):
        sl = (dc % 2) * SUBW
        S.op("tensor", lambda E, dc=dc: E.matmul(P2[:, 0:w], B["dirs_b"][0:64, dc * 128:(dc + 1) * 128],
                                                 ohT[0:64, 0:w], start=True, stop=True),
             r=[B["dirs_b"].full, T("ohT", 0, w)], w=[T("P2", 0, w)])
        S.op("scalar", lambda E, sl=sl: E.copy(dirsel[:, sl:sl + w], P2[:, 0:w]),
             r=[T("P2", 0, w)], w=[T("dirsel", sl, sl + w)])
        S.op("vector", lambda E, dc=dc, sl=sl: E.tensor_tensor(yf[:, dc * SUBW:dc * SUBW + w],
                                                               dirsel[:, sl:sl + w],
                                                               P3[:, 0:w], ALU.mult),
             r=[T("dirsel", sl, sl + w), T("P3", 0, w)], w=[T("stage", dc * SUBW, dc * SUBW + w)])
        S.op("vector", lambda E, dc=dc: E.tensor_tensor(yf[:, dc * SUBW:dc * SUBW + w],
                                                        yf[:, dc * SUBW:dc * SUBW + w],
                                                        xn[:, dc * SUBW:dc * SUBW + w], ALU.add),
             r=[T("stage", dc * SUBW, dc * SUBW + w), T("xn", dc * SUBW, dc * SUBW + w)],
             w=[T("stage", dc * SUBW, dc * SUBW + w)])
        S.op("scalar", lambda E, dc=dc, sl=sl: E.copy(ybf[:, sl:sl + w],
                                                      yf[:, dc * SUBW:dc * SUBW + w]),
             r=[T("stage", dc * SUBW, dc * SUBW + w)], w=[T("ybf", sl, sl + w)])
        S.op("vector", lambda E, sl=sl: E.tensor_tensor(pt[:, sl:sl + w], ybf[:, sl:sl + w],
                                                        ybf[:, sl:sl + w], ALU.mult),
             r=[T("ybf", sl, sl + w)], w=[T("pt", sl, sl + w)])
        S.op("tensor", lambda E, dc=dc, sl=sl: E.matmul(P4[0:1, 0:w], B["onesb"][:, :],
                                                        ybf[:, sl:sl + w],
                                                        start=(dc == 0), stop=(dc == 7)),
             r=[T("ybf", sl, sl + w), B["onesb"].full], w=[T("P4", 0, w)])
        S.op("tensor", lambda E, dc=dc, sl=sl: E.matmul(P5[0:1, 0:w], B["onesb"][:, :],
                                                        pt[:, sl:sl + w],
                                                        start=(dc == 0), stop=(dc == 7)),
             r=[T("pt", sl, sl + w), B["onesb"].full], w=[T("P5", 0, w)])
    ln_rows()

    gam, bet, gsil, s_all = B["gam_s"], B["bet_s"], B["gsil"], B["s_all"]
    for dc in range(8):
        gl = (dc % 2) * SUBW
        S.op("vector", lambda E, dc=dc: E.tensor_tensor(yf[:, dc * SUBW:dc * SUBW + w],
                                                        yf[:, dc * SUBW:dc * SUBW + w],
                                                        P4[:, 0:w], ALU.subtract),
             r=[T("stage", dc * SUBW, dc * SUBW + w), T("P4", 0, w)],
             w=[T("stage", dc * SUBW, dc * SUBW + w)])
        S.op("vector", lambda E, dc=dc: E.tensor_tensor(yf[:, dc * SUBW:dc * SUBW + w],
                                                        yf[:, dc * SUBW:dc * SUBW + w],
                                                        P5[:, 0:w], ALU.mult),
             r=[T("stage", dc * SUBW, dc * SUBW + w), T("P5", 0, w)],
             w=[T("stage", dc * SUBW, dc * SUBW + w)])
        S.op("vector", lambda E, dc=dc: E.tensor_scalar(yf[:, dc * SUBW:dc * SUBW + w],
                                                        yf[:, dc * SUBW:dc * SUBW + w],
                                                        gam[:, dc:dc + 1], bet[:, dc:dc + 1],
                                                        ALU.mult, ALU.add),
             r=[T("stage", dc * SUBW, dc * SUBW + w), gam.full, bet.full],
             w=[T("stage", dc * SUBW, dc * SUBW + w)])
        S.op("scalar", lambda E, dc=dc, gl=gl: E.activation(gsil[:, gl:gl + w],
                                                            yf[:, dc * SUBW:dc * SUBW + w], AF.Silu),
             r=[T("stage", dc * SUBW, dc * SUBW + w)], w=[T("gsil", gl, gl + w)])
        S.op("vector", lambda E, dc=dc, gl=gl: E.tensor_tensor(gsil[:, gl:gl + w], gsil[:, gl:gl + w],
                                                               s_all[:, dc * SUBW:dc * SUBW + w],
                                                               ALU.mult),
             r=[T("gsil", gl, gl + w), T("s_all", dc * SUBW, dc * SUBW + w)], w=[T("gsil", gl, gl + w)])
        S.op("vector", lambda E, dc=dc, gl=gl: E.tensor_tensor(x_sub[:, dc * SUBW:dc * SUBW + w],
                                                               gsil[:, gl:gl + w],
                                                               x_sub[:, dc * SUBW:dc * SUBW + w],
                                                               ALU.add),
             r=[T("gsil", gl, gl + w), T("x_sub", dc * SUBW, dc * SUBW + w)],
             w=[T("x_sub", dc * SUBW, dc * SUBW + w)])
        S.op("gpsimd", lambda E, dc=dc: E.tensor_copy(xn[:, dc * SUBW:dc * SUBW + w],
                                                      x_sub[:, dc * SUBW:dc * SUBW + w]),
             r=[T("x_sub", dc * SUBW, dc * SUBW + w)], w=[T("xn", dc * SUBW, dc * SUBW + w)])
    dbg_chunked("gy", yf, 8)
    for hf in range(4):
        dsty = x1_hbm.rearrange("(c p) t -> p c t", p=128)[:, hf * 2:(hf + 1) * 2, ls:ls + w]
        srcy = xn.h.ap().rearrange("p (c t) -> p c t", c=8)[:, hf * 2:(hf + 1) * 2, 0:w]
        S.op("scalar", lambda E, d=dsty, s=srcy: E.dma_start(d, s),
             r=[("xn", hf * 2 * SUBW, (hf + 1) * 2 * SUBW)], dma=True, tag="x1_dma")


SC2T = 1.0 / 0.56
NEG = -1.0e30


L1A_HI_SPECS = [
        ("wq", [1024, 1024], F32), ("wk", [1024, 1024], F32),
        ("wv", [1024, 1024], F32), ("wo", [1024, 1024], F32),
        ("idb", [128, 128], BF16), ("mask0_h", [128, 136], F32),
        ("maskr_h", [128, 136], F32),
]


def emit_l1a(S, nc, B, x1_hbm, s2_hbm, HI):

    def T(name, lo, hi):
        return (name, lo, hi)

    def dma(dst_ap, src_ap, r=(), w=(), tag=None):
        return S.op("sync", lambda E, d=dst_ap, s=src_ap: E.dma_start(d, s),
                    r=r, w=w, dma=True, tag=tag)

    def gdma(dst_ap, src_ap, r=(), w=(), tag=None):
        return S.op("gpsimd", lambda E, d=dst_ap, s=src_ap: E.dma_start(d, s),
                    r=r, w=w, dma=True, tag=tag)

    for nm_src, nm_dst in [("wq", "wq_b"), ("wk", "wk_b"), ("wv", "wv_b"), ("wo", "wo_b")]:
        for c in range(8):
            gdma(B[nm_dst][:, c * 1024:(c + 1) * 1024], HI[nm_src][c * 128:(c + 1) * 128, :],
                 w=[T(nm_dst, c * 1024, (c + 1) * 1024)])
    if "wd_b" in B and "wdt" in HI:
        for c in range(24):
            gdma(B["wd_b"][:, c * 1024:(c + 1) * 1024], HI["wdt"][c * 128:(c + 1) * 128, :],
                 w=[T("wd_b", c * 1024, (c + 1) * 1024)])
    for dst, srcn in [("idb_s", "idb"), ("mask0", "mask0_h"), ("maskr", "maskr_h")]:
        dma(B[dst][:, :], HI[srcn][:, :], w=[B[dst].full])
    for nm, val in [("onesb", 1.0), ("ones1b", 1.0), ("epsc", 1e-5), ("mhalf", -0.5),
                    ("sc2t", SC2T)]:
        S.op("vector", lambda E, nm=nm, val=val: E.memset(B[nm][:, :], val), w=[B[nm].full])

    P4, P5 = B["P4"], B["P5"]
    row_m, row_v, row_r = B["row_m"], B["row_v"], B["row_r"]
    row_rb, row_mb = B["row_rb"], B["row_mb"]
    x_sub, xbf, xpt, xn1 = B["x_sub"], B["xbf"], B["xpt"], B["xn1"]

    def ln_block(w, dst_fn):
        for c in range(8):
            sl = (c % 2) * SUBW
            S.op("scalar", lambda E, c=c, sl=sl, w=w: E.copy(xbf[:, sl:sl + w],
                                                            x_sub[:, c * SUBW:c * SUBW + w]),
                 r=[T("x_sub", c * SUBW, c * SUBW + w)], w=[T("xbf", sl, sl + w)])
            S.op("vector", lambda E, sl=sl, w=w: E.tensor_tensor(xpt[:, sl:sl + w], xbf[:, sl:sl + w],
                                                                 xbf[:, sl:sl + w], ALU.mult),
                 r=[T("xbf", sl, sl + w)], w=[T("xpt", sl, sl + w)])
            S.op("tensor", lambda E, c=c, sl=sl, w=w: E.matmul(P4[0:1, 0:w], B["onesb"][:, :],
                                                               xbf[:, sl:sl + w],
                                                               start=(c == 0), stop=(c == 7)),
                 r=[T("xbf", sl, sl + w), B["onesb"].full], w=[T("P4", 0, w)])
            S.op("tensor", lambda E, c=c, sl=sl, w=w: E.matmul(P5[0:1, 0:w], B["onesb"][:, :],
                                                               xpt[:, sl:sl + w],
                                                               start=(c == 0), stop=(c == 7)),
                 r=[T("xpt", sl, sl + w), B["onesb"].full], w=[T("P5", 0, w)])
        S.op("vector", lambda E, w=w: E.tensor_scalar(row_m[0:1, 0:w], P4[0:1, 0:w], 1.0 / DM,
                                                      None, ALU.mult),
             r=[T("P4", 0, w)], w=[T("row_m", 0, w)])
        S.op("vector", lambda E, w=w: E.tensor_scalar(row_v[0:1, 0:w], P5[0:1, 0:w], 1.0 / DM,
                                                      None, ALU.mult),
             r=[T("P5", 0, w)], w=[T("row_v", 0, w)])
        S.op("vector", lambda E, w=w: E.tensor_tensor(row_r[0:1, 0:w], row_m[0:1, 0:w],
                                                      row_m[0:1, 0:w], ALU.mult),
             r=[T("row_m", 0, w)], w=[T("row_r", 0, w)])
        S.op("vector", lambda E, w=w: E.tensor_tensor(row_v[0:1, 0:w], row_v[0:1, 0:w],
                                                      row_r[0:1, 0:w], ALU.subtract),
             r=[T("row_v", 0, w), T("row_r", 0, w)], w=[T("row_v", 0, w)])
        S.op("scalar", lambda E, w=w: E.activation(row_r[0:1, 0:w], row_v[0:1, 0:w], AF.Ln,
                                                   bias=B["epsc"][0:1, 0:1]),
             r=[T("row_v", 0, w), B["epsc"].full], w=[T("row_r", 0, w)])
        S.op("scalar", lambda E, w=w: E.activation(row_r[0:1, 0:w], row_r[0:1, 0:w], AF.Exp,
                                                   scale=B["mhalf"][0:1, 0:1]),
             r=[T("row_r", 0, w), B["mhalf"].full], w=[T("row_r", 0, w)])
        S.op("scalar", lambda E, w=w: E.copy(row_rb[0:1, 0:w], row_r[0:1, 0:w]),
             r=[T("row_r", 0, w)], w=[T("row_rb", 0, w)])
        S.op("scalar", lambda E, w=w: E.copy(row_mb[0:1, 0:w], row_m[0:1, 0:w]),
             r=[T("row_m", 0, w)], w=[T("row_mb", 0, w)])
        S.op("tensor", lambda E, w=w: E.matmul(P4[:, 0:w], B["ones1b"][0:1, :], row_mb[0:1, 0:w],
                                               start=True, stop=True),
             r=[T("row_mb", 0, w), B["ones1b"].full], w=[T("P4", 0, w)])
        S.op("tensor", lambda E, w=w: E.matmul(P5[:, 0:w], B["ones1b"][0:1, :], row_rb[0:1, 0:w],
                                               start=True, stop=True),
             r=[T("row_rb", 0, w), B["ones1b"].full], w=[T("P5", 0, w)])
        for c in range(8):
            ap, wtag = dst_fn(c)
            S.op("vector", lambda E, c=c, w=w: E.tensor_tensor(
                B["stage"][:, 0:w] if False else x_sub[:, c * SUBW:c * SUBW + w],
                x_sub[:, c * SUBW:c * SUBW + w], P4[:, 0:w], ALU.subtract),
                r=[T("x_sub", c * SUBW, c * SUBW + w), T("P4", 0, w)],
                w=[T("x_sub", c * SUBW, c * SUBW + w)])
            S.op("vector", lambda E, c=c, ap=ap, wtag=wtag, w=w: E.tensor_tensor(
                ap, x_sub[:, c * SUBW:c * SUBW + w], P5[:, 0:w], ALU.mult),
                r=[T("x_sub", c * SUBW, c * SUBW + w), T("P5", 0, w)], w=[wtag])

    src = x1_hbm.rearrange("(c p) t -> p c t", p=128)[:, :, 0:8]
    dst = x_sub.h.ap().rearrange("p (c t) -> p c t", c=8)[:, :, 0:8]
    S.op("sync", lambda E, d=dst, s=src: E.dma_start(d, s), w=[x_sub.full], dma=True)
    ln_block(8, lambda c: (xn1[:, c * 520:c * 520 + 8], T("xn1", c * 520, c * 520 + 8)))

    B["_ln_block"] = ln_block
    for si in range(4):
        emit_l1a_sub(S, nc, B, x1_hbm, s2_hbm, si)




def emit_l1a_sub(S, nc, B, x1_hbm, s2_hbm, si):
    w = SUBW

    def T(name, lo, hi):
        return (name, lo, hi)

    P0, P1, P2, P3, P4, P5 = (B[k] for k in ["P0", "P1", "P2", "P3", "P4", "P5"])
    PTb = B["PTb"]
    x_sub, xn1 = B["x_sub"], B["xn1"]
    qf, kf = B["qf"], B["kf"]
    vts = [B["vt0"], B["vt1"], B["vt2"], B["vt3"]]
    vnx, att, atf, s2 = B["vnx"], B["att"], B["atf"], B["s2"]

    if si > 0:
        for c in range(8):
            S.op("vector", lambda E, c=c: E.tensor_copy(xn1[:, c * 520:c * 520 + 8],
                                                        xn1[:, c * 520 + 512:c * 520 + 520]),
                 r=[T("xn1", c * 520 + 512, c * 520 + 520)], w=[T("xn1", c * 520, c * 520 + 8)])

    ls = 512 * si + 8
    for hf in range(2):
        srch = x1_hbm.rearrange("(c p) t -> p c t", p=128)[:, hf * 4:(hf + 1) * 4, ls:ls + w]
        dsth = x_sub.h.ap().rearrange("p (c t) -> p c t", c=8)[:, hf * 4:(hf + 1) * 4, 0:w]
        S.op("sync", lambda E, d=dsth, s=srch: E.dma_start(d, s),
             w=[("x_sub", hf * 4 * SUBW, (hf + 1) * 4 * SUBW)], dma=True)
    ln_block = B["_ln_block"]
    ln_block(w, lambda c: (xn1[:, c * 520 + 8:c * 520 + 520], T("xn1", c * 520 + 8, c * 520 + 520)))

    for jc in range(8):
        pz = ("P0", "P1")[jc % 2]
        PZ = B[pz]
        for k in range(8):
            S.op("tensor", lambda E, jc=jc, k=k, PZ=PZ: E.matmul(
                PZ[:, 0:w], B["wq_b"][:, k * 1024 + jc * 128:k * 1024 + (jc + 1) * 128],
                xn1[:, k * 520 + 8:k * 520 + 520], start=(k == 0), stop=(k == 7)),
                r=[T("wq_b", k * 1024 + jc * 128, k * 1024 + (jc + 1) * 128),
                   T("xn1", k * 520 + 8, k * 520 + 520)], w=[T(pz, 0, w)])
        S.op("scalar", lambda E, jc=jc, pz=pz, PZ=PZ: E.copy(qf[:, jc * SUBW:(jc + 1) * SUBW],
                                                             PZ[:, 0:w]),
             r=[T(pz, 0, w)], w=[T("qf", jc * SUBW, (jc + 1) * SUBW)])
    for jc in range(8):
        pz = ("P0", "P1")[jc % 2]
        PZ = B[pz]
        for k in range(8):
            S.op("tensor", lambda E, jc=jc, k=k, PZ=PZ: E.matmul(
                PZ[:, 0:8], B["wk_b"][:, k * 1024 + jc * 128:k * 1024 + (jc + 1) * 128],
                xn1[:, k * 520:k * 520 + 8], start=(k == 0), stop=(k == 7)),
                r=[T("wk_b", k * 1024 + jc * 128, k * 1024 + (jc + 1) * 128),
                   T("xn1", k * 520, k * 520 + 8)], w=[T(pz, 0, 8)])
        S.op("scalar", lambda E, jc=jc, pz=pz, PZ=PZ: E.copy(kf[:, jc * 520:jc * 520 + 8],
                                                             PZ[0:128, 0:8]),
             r=[T(pz, 0, 8)], w=[T("kf", jc * 520, jc * 520 + 8)])
        pz2 = ("P2", "P3")[jc % 2]
        PZ2 = B[pz2]
        for k in range(8):
            S.op("tensor", lambda E, jc=jc, k=k, PZ2=PZ2: E.matmul(
                PZ2[:, 0:w], B["wk_b"][:, k * 1024 + jc * 128:k * 1024 + (jc + 1) * 128],
                xn1[:, k * 520 + 8:k * 520 + 520], start=(k == 0), stop=(k == 7)),
                r=[T("wk_b", k * 1024 + jc * 128, k * 1024 + (jc + 1) * 128),
                   T("xn1", k * 520 + 8, k * 520 + 520)], w=[T(pz2, 0, w)])
        S.op("scalar", lambda E, jc=jc, pz2=pz2, PZ2=PZ2: E.copy(kf[:, jc * 520 + 8:(jc + 1) * 520],
                                                                 PZ2[:, 0:w]),
             r=[T(pz2, 0, w)], w=[T("kf", jc * 520 + 8, (jc + 1) * 520)])

    for m in range(4):
        for jh in range(2):
            pz = ("P0", "P1")[jh]
            PZ = B[pz]
            for k in range(8):
                S.op("tensor", lambda E, m=m, jh=jh, k=k, PZ=PZ: E.matmul(
                    PZ[:, 0:512], xn1[:, k * 520 + m * 128:k * 520 + (m + 1) * 128],
                    B["wv_b"][:, k * 1024 + jh * 512:k * 1024 + (jh + 1) * 512],
                    start=(k == 0), stop=(k == 7)),
                    r=[T("xn1", k * 520 + m * 128, k * 520 + (m + 1) * 128),
                       T("wv_b", k * 1024 + jh * 512, k * 1024 + (jh + 1) * 512)],
                    w=[T(pz, 0, 512)])
            S.op("scalar", lambda E, m=m, jh=jh, pz=pz, PZ=PZ: E.copy(
                vts[m][:, jh * 512:(jh + 1) * 512], PZ[:, 0:512]),
                r=[T(pz, 0, 512)], w=[(vts[m].name, jh * 512, (jh + 1) * 512)])
    for jh in range(2):
        pz = ("P0", "P1")[jh]
        PZ = B[pz]
        for k in range(8):
            S.op("tensor", lambda E, jh=jh, k=k, PZ=PZ: E.matmul(
                PZ[0:8, 0:512], xn1[:, k * 520 + 512:k * 520 + 520],
                B["wv_b"][:, k * 1024 + jh * 512:k * 1024 + (jh + 1) * 512],
                start=(k == 0), stop=(k == 7)),
                r=[T("xn1", k * 520 + 512, k * 520 + 520),
                   T("wv_b", k * 1024 + jh * 512, k * 1024 + (jh + 1) * 512)],
                w=[T(pz, 0, 512)])
        S.op("scalar", lambda E, jh=jh, pz=pz, PZ=PZ: E.copy(vnx[0:8, jh * 512:(jh + 1) * 512],
                                                             PZ[0:8, 0:512]),
             r=[T(pz, 0, 512)], w=[T("vnx", jh * 512, (jh + 1) * 512)])

    sm, rmx, nmx, ebf = B["sm"], B["rmx"], B["nmx"], B["ebf"]
    esum, erec, eta, etb = B["esum"], B["erec"], B["eta"], B["etb"]
    for tb in range(4):
        for h in range(16):
            jc = h // 2
            rb = (h % 2) * 64
            slot = h % 4
            so = slot * 136
            eo = slot * 128
            po = (slot // 2) * 272
            tp = slot * 128
            mk = B["mask0"] if (si == 0 and tb == 0) else B["maskr"]
            mk_tag = B["mask0"].full if (si == 0 and tb == 0) else B["maskr"].full
            pz = ("P2", "P3")[h % 2]
            PZ = B[pz]
            S.op("tensor", lambda E, tb=tb, jc=jc, rb=rb, PZ=PZ, po=po: E.matmul(
                PZ[0:128, po:po + 136], qf[rb:rb + 64, jc * SUBW + tb * 128:jc * SUBW + (tb + 1) * 128],
                kf[rb:rb + 64, jc * 520 + tb * 128:jc * 520 + tb * 128 + 136],
                start=True, stop=True),
                r=[T("qf", jc * SUBW, (jc + 1) * SUBW), T("kf", jc * 520, (jc + 1) * 520)],
                w=[T(pz, po, po + 136)])
            S.op("vector", lambda E, PZ=PZ, mk=mk, so=so, po=po: E.tensor_tensor(
                sm[:, so:so + 136], PZ[0:128, po:po + 136], mk[:, :], ALU.add),
                r=[T(pz, po, po + 136), mk_tag], w=[T("sm", so, so + 136)])
            S.op("vector", lambda E, so=so, slot=slot: E.reduce_max(rmx[:, slot:slot + 1],
                                                                    sm[:, so:so + 136], axis=AX.X),
                 r=[T("sm", so, so + 136)], w=[T("rmx", slot, slot + 1)])
            S.op("vector", lambda E, slot=slot: E.tensor_scalar(nmx[:, slot:slot + 1],
                                                                rmx[:, slot:slot + 1], -SC2T,
                                                                None, ALU.mult),
                 r=[T("rmx", slot, slot + 1)], w=[T("nmx", slot, slot + 1)])
            S.op("scalar", lambda E, so=so, slot=slot: E.activation(
                ebf[:, so:so + 136], sm[:, so:so + 136], AF.Exp,
                scale=B["sc2t"][:, 0:1], bias=nmx[:, slot:slot + 1]),
                r=[T("sm", so, so + 136), T("nmx", slot, slot + 1), B["sc2t"].full],
                w=[T("ebf", so, so + 136)])
            S.op("vector", lambda E, so=so, slot=slot: E.reduce_sum(esum[:, slot:slot + 1],
                                                                    ebf[:, so:so + 136], axis=AX.X),
                 r=[T("ebf", so, so + 136)], w=[T("esum", slot, slot + 1)])
            S.op("vector", lambda E, slot=slot: E.reciprocal(erec[:, slot:slot + 1],
                                                             esum[:, slot:slot + 1]),
                 r=[T("esum", slot, slot + 1)], w=[T("erec", slot, slot + 1)])
            te = slot * 256
            tb2 = slot * 256 + 128
            S.op("tensor", lambda E, so=so, te=te: E.transpose(PTb[0:8, te:te + 128],
                                                               ebf[:, so + 128:so + 136],
                                                               B["idb_s"][:, :]),
                 r=[T("ebf", so, so + 136), T("idb_s", 0, 128)], w=[T("PTb", te, te + 128)])
            S.op("vector", lambda E, eo=eo, te=te: E.tensor_copy(eta[0:8, eo:eo + 128],
                                                                 PTb[0:8, te:te + 128]),
                 r=[T("PTb", te, te + 128)], w=[T("eta", eo, eo + 128)])
            S.op("tensor", lambda E, so=so, tb2=tb2: E.transpose(PTb[:, tb2:tb2 + 128], ebf[:, so:so + 128],
                                                                 B["idb_s"][:, :]),
                 r=[T("ebf", so, so + 136), T("idb_s", 0, 128)], w=[T("PTb", tb2, tb2 + 128)])
            S.op("vector", lambda E, eo=eo, tb2=tb2: E.tensor_copy(etb[:, eo:eo + 128],
                                                                   PTb[:, tb2:tb2 + 128]),
                 r=[T("PTb", tb2, tb2 + 128)], w=[T("etb", eo, eo + 128)])
            vhalo = vts[tb]
            vnext_ap = (vts[tb + 1][0:8, h * 64:(h + 1) * 64] if tb < 3
                        else vnx[0:8, h * 64:(h + 1) * 64])
            vnext_tag = ((vts[tb + 1].name, h * 64, (h + 1) * 64) if tb < 3
                         else T("vnx", h * 64, (h + 1) * 64))
            pz2 = ("P4", "P5")[h % 2]
            PZ2 = B[pz2]
            ao = (slot // 2) * 64
            S.op("tensor", lambda E, eo=eo, tb=tb, h=h, PZ2=PZ2, ao=ao: E.matmul(
                PZ2[0:128, ao:ao + 64], etb[:, eo:eo + 128], vts[tb][:, h * 64:(h + 1) * 64],
                start=True, stop=False),
                r=[T("etb", eo, eo + 128), (vts[tb].name, h * 64, (h + 1) * 64)],
                w=[T(pz2, ao, ao + 64)])
            S.op("tensor", lambda E, eo=eo, va=vnext_ap, vt=vnext_tag, PZ2=PZ2, ao=ao: E.matmul(
                PZ2[0:128, ao:ao + 64], eta[0:8, eo:eo + 128], va, start=False, stop=True),
                r=[T("eta", eo, eo + 128), vnext_tag], w=[T(pz2, ao, ao + 64)])
            S.op("vector", lambda E, tb=tb, h=h, slot=slot, PZ2=PZ2, pz2=pz2, ao=ao: E.tensor_scalar(
                att[:, tb * 1024 + h * 64:tb * 1024 + (h + 1) * 64], PZ2[0:128, ao:ao + 64],
                erec[:, slot:slot + 1], None, ALU.mult),
                r=[T(pz2, ao, ao + 64), T("erec", slot, slot + 1)],
                w=[T("att", tb * 1024 + h * 64, tb * 1024 + (h + 1) * 64)])

    for tb in range(4):
        for jc in range(8):
            S.op("tensor", lambda E, tb=tb, jc=jc: E.transpose(
                PTb[:, 0:128], att[:, tb * 1024 + jc * 128:tb * 1024 + (jc + 1) * 128],
                B["idb_s"][:, :]),
                r=[T("att", tb * 1024 + jc * 128, tb * 1024 + (jc + 1) * 128),
                   T("idb_s", 0, 128)], w=[T("PTb", 0, 128)])
            S.op("vector", lambda E, tb=tb, jc=jc: E.tensor_copy(
                atf[:, jc * SUBW + tb * 128:jc * SUBW + (tb + 1) * 128], PTb[:, 0:128]),
                r=[T("PTb", 0, 128)], w=[T("atf", jc * SUBW + tb * 128, jc * SUBW + (tb + 1) * 128)])
    for dc in range(8):
        pz = ("P0", "P1")[dc % 2]
        PZ = B[pz]
        for jc in range(8):
            S.op("tensor", lambda E, dc=dc, jc=jc, PZ=PZ: E.matmul(
                PZ[:, 0:w], B["wo_b"][:, jc * 1024 + dc * 128:jc * 1024 + (dc + 1) * 128],
                atf[:, jc * SUBW:(jc + 1) * SUBW], start=(jc == 0), stop=(jc == 7)),
                r=[T("wo_b", jc * 1024 + dc * 128, jc * 1024 + (dc + 1) * 128),
                   T("atf", jc * SUBW, (jc + 1) * SUBW)], w=[T(pz, 0, w)])
        S.op("scalar", lambda E, dc=dc, pz=pz, PZ=PZ: E.copy(s2[:, dc * SUBW:(dc + 1) * SUBW],
                                                             PZ[:, 0:w]),
             r=[T(pz, 0, w)], w=[T("s2", dc * SUBW, (dc + 1) * SUBW)])
    for hf in range(2):
        dsty = s2_hbm.rearrange("(c p) t -> p c t", p=128)[:, hf * 4:(hf + 1) * 4,
                                                           si * SUBW:(si + 1) * SUBW]
        srcy = s2.h.ap().rearrange("p (c t) -> p c t", c=8)[:, hf * 4:(hf + 1) * 4, 0:w]
        S.op("scalar", lambda E, d=dsty, s=srcy: E.dma_start(d, s),
             r=[("s2", hf * 4 * SUBW, (hf + 1) * 4 * SUBW)], dma=True, tag="s2_dma")


K127 = 1.0 / 127.0
K3 = 1.0 / (127.0 ** 3)


def emit_l1b(S, nc, B, x1_hbm, s2_hbm, out_hbm, HI, nsubs=4, use_s2s=True, W=SUBW):

    def T(name, lo, hi):
        return (name, lo, hi)

    def dma(dst_ap, src_ap, r=(), w=(), tag=None):
        return S.op("sync", lambda E, d=dst_ap, s=src_ap: E.dma_start(d, s),
                    r=r, w=w, dma=True, tag=tag)

    def gdma(dst_ap, src_ap, r=(), w=(), tag=None):
        return S.op("gpsimd", lambda E, d=dst_ap, s=src_ap: E.dma_start(d, s),
                    r=r, w=w, dma=True, tag=tag)

    for nm_src, nm_dst in [("wgt", "wg_b"), ("wut", "wu_b")]:
        for c in range(8):
            for hf in range(2):
                gdma(B[nm_dst][:, c * 3072 + hf * 1536:c * 3072 + (hf + 1) * 1536],
                     HI[nm_src][c * 128:(c + 1) * 128, hf * 1536:(hf + 1) * 1536],
                     w=[T(nm_dst, c * 3072 + hf * 1536, c * 3072 + (hf + 1) * 1536)])
    for nm, val in [("onesb", 1.0), ("ones1b", 1.0), ("epsc", 1e-5), ("mhalf", -0.5),
                    ("c127", K127), ("k3c", K3)]:
        S.op("vector", lambda E, nm=nm, val=val: E.memset(B[nm][:, :], val), w=[B[nm].full])

    for si in range(nsubs):
        emit_l1b_sub(S, nc, B, x1_hbm, s2_hbm, out_hbm, si, use_s2s, W)




def emit_l1b_sub(S, nc, B, x1_hbm, s2_hbm, out_hbm, si, use_s2s=True, W=SUBW):
    w = W
    ls = si * W

    def T(name, lo, hi):
        return (name, lo, hi)

    P0, P1, P2, P3, P4, P5 = (B[k] for k in ["P0", "P1", "P2", "P3", "P4", "P5"])
    x_sub, xbf, xn1 = B["x_sub"], B["xbf"], B["xn1"]
    xpt = B["sg"]
    row_v = B["row_v"]
    row_rb, row_mb = B["row_rb"], B["row_mb"]
    s2s, hb, sg, outf = B["s2s"], B["hb"], B["sg"], B["outf"]
    tt1 = B["xbf"]

    for hf in range(2):
        srch = x1_hbm.rearrange("(c p) t -> p c t", p=128)[:, hf * 4:(hf + 1) * 4, HALO + ls:HALO + ls + w]
        dsth = x_sub.h.ap().rearrange("p (c t) -> p c t", c=8)[:, hf * 4:(hf + 1) * 4, 0:w]
        S.op("sync", lambda E, d=dsth, s=srch: E.dma_start(d, s),
             w=[("x_sub", hf * 4 * W, (hf + 1) * 4 * W)], dma=True)

    for c in range(8):
        sl = (c % 2) * W
        S.op("scalar", lambda E, c=c, sl=sl: E.copy(xbf[:, sl:sl + w], x_sub[:, c * W:c * W + w]),
             r=[T("x_sub", c * W, c * W + w)], w=[T("xbf", sl, sl + w)])
        S.op("vector", lambda E, sl=sl: E.tensor_tensor(xpt[:, sl:sl + w], xbf[:, sl:sl + w],
                                                        xbf[:, sl:sl + w], ALU.mult),
             r=[T("xbf", sl, sl + w)], w=[T("sg", sl, sl + w)])
        S.op("tensor", lambda E, c=c, sl=sl: E.matmul(P4[0:1, 0:w], B["onesb"][:, :], xbf[:, sl:sl + w],
                                                      start=(c == 0), stop=(c == 7)),
             r=[T("xbf", sl, sl + w), B["onesb"].full], w=[T("P4", 0, w)])
        S.op("tensor", lambda E, c=c, sl=sl: E.matmul(P5[0:1, 0:w], B["onesb"][:, :], xpt[:, sl:sl + w],
                                                      start=(c == 0), stop=(c == 7)),
             r=[T("sg", sl, sl + w), B["onesb"].full], w=[T("P5", 0, w)])
    S.op("vector", lambda E: E.tensor_scalar(row_v[0:1, 0:w], P4[0:1, 0:w], 1.0 / DM, None, ALU.mult),
         r=[T("P4", 0, w)], w=[T("row_v", 0, w)])
    S.op("scalar", lambda E: E.copy(row_mb[0:1, 0:w], row_v[0:1, 0:w]),
         r=[T("row_v", 0, w)], w=[T("row_mb", 0, w)])
    S.op("vector", lambda E: E.tensor_tensor(row_rb[0:1, 0:w], row_v[0:1, 0:w], row_v[0:1, 0:w],
                                             ALU.mult),
         r=[T("row_v", 0, w)], w=[T("row_rb", 0, w)])
    S.op("vector", lambda E: E.tensor_scalar(row_v[0:1, 0:w], P5[0:1, 0:w], 1.0 / DM, None, ALU.mult),
         r=[T("P5", 0, w)], w=[T("row_v", 0, w)])
    S.op("vector", lambda E: E.tensor_tensor(row_v[0:1, 0:w], row_v[0:1, 0:w], row_rb[0:1, 0:w],
                                             ALU.subtract),
         r=[T("row_v", 0, w), T("row_rb", 0, w)], w=[T("row_v", 0, w)])
    S.op("scalar", lambda E: E.activation(row_v[0:1, 0:w], row_v[0:1, 0:w], AF.Ln,
                                          bias=B["epsc"][0:1, 0:1]),
         r=[T("row_v", 0, w), B["epsc"].full], w=[T("row_v", 0, w)])
    S.op("scalar", lambda E: E.activation(row_v[0:1, 0:w], row_v[0:1, 0:w], AF.Exp,
                                          scale=B["mhalf"][0:1, 0:1]),
         r=[T("row_v", 0, w), B["mhalf"].full], w=[T("row_v", 0, w)])
    S.op("scalar", lambda E: E.copy(row_rb[0:1, 0:w], row_v[0:1, 0:w]),
         r=[T("row_v", 0, w)], w=[T("row_rb", 0, w)])
    S.op("tensor", lambda E: E.matmul(P4[:, 0:w], B["ones1b"][0:1, :], row_mb[0:1, 0:w],
                                      start=True, stop=True),
         r=[T("row_mb", 0, w), B["ones1b"].full], w=[T("P4", 0, w)])
    S.op("tensor", lambda E: E.matmul(P5[:, 0:w], B["ones1b"][0:1, :], row_rb[0:1, 0:w],
                                      start=True, stop=True),
         r=[T("row_rb", 0, w), B["ones1b"].full], w=[T("P5", 0, w)])
    for c in range(8):
        sl = (c % 2) * W
        S.op("vector", lambda E, c=c, sl=sl: E.tensor_tensor(xbf[:, sl:sl + w],
                                                             x_sub[:, c * W:c * W + w],
                                                             P4[:, 0:w], ALU.subtract),
             r=[T("x_sub", c * W, c * W + w), T("P4", 0, w)], w=[T("xbf", sl, sl + w)])
        S.op("vector", lambda E, c=c, sl=sl: E.tensor_tensor(xn1[:, c * W:c * W + w],
                                                             xbf[:, sl:sl + w],
                                                             P5[:, 0:w], ALU.mult),
             r=[T("xbf", sl, sl + w), T("P5", 0, w)], w=[T("xn1", c * W, c * W + w)])

    for hc in range(24):
        sl = (hc % 2) * W
        for k in range(8):
            S.op("tensor", lambda E, hc=hc, k=k: E.matmul(
                P0[:, 0:w], B["wg_b"][:, k * 3072 + hc * 128:k * 3072 + (hc + 1) * 128],
                xn1[:, k * W:k * W + w], start=(k == 0), stop=(k == 7)),
                r=[T("wg_b", k * 3072 + hc * 128, k * 3072 + (hc + 1) * 128),
                   T("xn1", k * W, k * W + w)], w=[T("P0", 0, w)])
        for k in range(8):
            S.op("tensor", lambda E, hc=hc, k=k: E.matmul(
                P1[:, 0:w], B["wu_b"][:, k * 3072 + hc * 128:k * 3072 + (hc + 1) * 128],
                xn1[:, k * W:k * W + w], start=(k == 0), stop=(k == 7)),
                r=[T("wu_b", k * 3072 + hc * 128, k * 3072 + (hc + 1) * 128),
                   T("xn1", k * W, k * W + w)], w=[T("P1", 0, w)])
        S.op("scalar", lambda E, sl=sl: E.activation(sg[:, sl:sl + w], P0[:, 0:w], AF.Sigmoid,
                                                     scale=B["c127"][:, 0:1]),
             r=[T("P0", 0, w), B["c127"].full], w=[T("sg", sl, sl + w)])
        S.op("vector", lambda E, sl=sl: E.tensor_tensor(tt1[:, sl:sl + w], P0[:, 0:w],
                                                        sg[:, sl:sl + w], ALU.mult),
             r=[T("P0", 0, w), T("sg", sl, sl + w)], w=[T("xbf", sl, sl + w)])
        S.op("vector", lambda E, hc=hc, sl=sl: E.tensor_tensor(hb[:, hc * W:hc * W + w],
                                                               tt1[:, sl:sl + w],
                                                               P1[:, 0:w], ALU.mult),
             r=[T("xbf", sl, sl + w), T("P1", 0, w)], w=[T("hb", hc * W, hc * W + w)])

    for dc in range(8):
        pz = ("P2", "P3")[dc % 2]
        PZ = B[pz]
        sl = (dc % 2) * W
        if use_s2s:
            S.op("sync", lambda E, dc=dc, sl=sl: E.dma_start(
                s2s[:, sl:sl + w], s2_hbm[dc * 128:(dc + 1) * 128, ls:ls + w]),
                w=[T("s2s", sl, sl + w)], dma=True)
        for hc in range(24):
            S.op("tensor", lambda E, dc=dc, hc=hc, PZ=PZ: E.matmul(
                PZ[:, 0:w], B["wd_b"][:, hc * 1024 + dc * 128:hc * 1024 + (dc + 1) * 128],
                hb[:, hc * W:hc * W + w], start=(hc == 0), stop=(hc == 23)),
                r=[T("wd_b", hc * 1024 + dc * 128, hc * 1024 + (dc + 1) * 128),
                   T("hb", hc * W, hc * W + w)], w=[T(pz, 0, w)])
        S.op("vector", lambda E, pz=pz, PZ=PZ, sl=sl: E.tensor_scalar(tt1[:, sl:sl + w], PZ[:, 0:w],
                                                                      B["k3c"][:, 0:1], None, ALU.mult),
             r=[T(pz, 0, w), B["k3c"].full], w=[T("xbf", sl, sl + w)])
        S.op("scalar", lambda E, sl=sl: E.activation(tt1[:, sl:sl + w], tt1[:, sl:sl + w], AF.Silu),
             r=[T("xbf", sl, sl + w)], w=[T("xbf", sl, sl + w)])
        if use_s2s:
            S.op("vector", lambda E, dc=dc, sl=sl: E.tensor_tensor(tt1[:, sl:sl + w], tt1[:, sl:sl + w],
                                                                   s2s[:, sl:sl + w],
                                                                   ALU.mult),
                 r=[T("xbf", sl, sl + w), T("s2s", sl, sl + w)], w=[T("xbf", sl, sl + w)])
        oslot = (dc % 4) * W
        S.op("vector", lambda E, dc=dc, sl=sl, oslot=oslot: E.tensor_tensor(
            outf[:, oslot:oslot + w], tt1[:, sl:sl + w],
            x_sub[:, dc * W:dc * W + w], ALU.add),
            r=[T("xbf", sl, sl + w), T("x_sub", dc * W, dc * W + w)],
            w=[T("outf", oslot, oslot + w)])
        if dc % 4 == 3:
            ob = dc // 4
            dsty = out_hbm.rearrange("(c p) t -> p c t", p=128)[:, ob * 4:(ob + 1) * 4, ls:ls + w]
            srcy = outf.h.ap().rearrange("p (c t) -> p c t", c=4)[:, :, 0:w]
            S.op("scalar", lambda E, d=dsty, s=srcy: E.dma_start(d, s), r=[outf.full],
                 dma=True, tag="out_dma")



W2 = 512
NSUB2 = TCORE // W2

L1B_HI_SPECS = [
    ("wgt", [1024, 3072], I32), ("wut", [1024, 3072], I32), ("wdt", [3072, 1024], I32),
]


def build_fused():
    nc = bass.Bass("TRN2", target_bir_lowering=False)
    S = Sched()
    stack = contextlib.ExitStack()
    with stack:
        x_hbm = nc.dram_tensor("x_fm", [DM, TLOC], F32, kind="ExternalInput")
        x1_hbm = nc.dram_tensor("x1_fm", [DM, TLOC], BF16, kind="Internal")
        s2_hbm = nc.dram_tensor("s2_fm", [DM, TCORE], BF16, kind="Internal")
        out_hbm = nc.dram_tensor("out_fm", [DM, TCORE], F32, kind="ExternalOutput")
        HI0 = {n: nc.dram_tensor(n, sh, dty, kind="ExternalInput") for n, sh, dty in L0_HI_SPECS}
        HI1 = {n: nc.dram_tensor(n, sh, dty, kind="ExternalInput")
               for n, sh, dty in L1A_HI_SPECS if n != "idb"}
        HI1["idb"] = HI0["idb"]
        HI2 = {n: nc.dram_tensor(n, sh, dty, kind="ExternalInput") for n, sh, dty in L1B_HI_SPECS}

        PS = {}
        for nm in ["P0", "P1", "P2", "P3", "P4", "P5"]:
            PS[nm] = TBuf(stack, nc, nm, [128, 512], F32, psum=True)
        PS["PTf"] = TBuf(stack, nc, "PTf", [128, 128], F32, psum=True)
        PS["PTb"] = TBuf(stack, nc, "PTb", [128, 1024], BF16, psum=True)

        ar = Arena(stack, nc, 212736)

        B = dict(PS)

        def add(name, shape, dtype):
            B[name] = ar.carve(name, shape, dtype)

        for name, shape, dtype in L0_BUFS:
            add(name, shape, dtype)
        B["yf"] = B["stage"]
        B["xnf"] = B["stage"]
        emit_l0_inner(S, nc, B, x_hbm, x1_hbm, HI0)
        S.fence()

        ar.reset()
        wd_tv = ar.carve_top("wd_b", [128, 24 * 1024], BF16)
        B = dict(PS)
        for name, shape, dtype in L1A_BUFS:
            add(name, shape, dtype)
        B["wd_b"] = wd_tv
        H1 = dict(HI1)
        H1["wdt"] = HI2["wdt"]
        emit_l1a(S, nc, B, x1_hbm, s2_hbm, H1)
        S.fence()

        ar.reset()
        B = dict(PS)
        for name, shape, dtype in L1B_BUFS:
            add(name, shape, dtype)
        B["wd_b"] = wd_tv
        emit_l1b(S, nc, B, x1_hbm, s2_hbm, out_hbm, HI2, nsubs=NSUB2, W=W2)

        outs = [op["id"] for op in S.ops if op.get("tag") == "out_dma"]
        S.mark_inc(outs)
        S.op("sync", lambda E: E.nop(), deps=outs)
        S.build(nc, stack)
    return nc, S.stats


L0_BUFS = [
    ("wzt_b", [128, 8 * 2048], BF16), ("wot_b", [128, 16 * DM], BF16),
    ("cw1_b", [128, 8 * 256], BF16), ("cw2_b", [128, 2 * 2], BF16),
    ("stage", [128, 2 * 2048], F32),
    ("bdt_s", [128, 16], F32), ("dskip_s", [128, 16], F32),
    ("cb1_sb", [128, 2], F32), ("cb2_s", [2, 1], F32),
    ("sigq_b", [64, DM], BF16), ("sigsT", [128, 8 * 64], F32),
    ("csig_b", [8, DM], BF16), ("csigT", [128, 8 * 8], F32),
    ("dirs_b", [64, DM], BF16),
    ("sw1_b", [64, 32], BF16), ("sb1_b", [64, 16], BF16),
    ("sw2_b", [64, 16], BF16), ("sb2_b", [64, 1], BF16),
    ("gam_s", [128, 8], F32), ("bet_s", [128, 8], F32),
    ("idf_s", [128, 128], F32), ("idb_s", [128, 128], BF16),
    ("sel01_s", [2, 64], BF16),
    ("onesb", [128, 1], BF16), ("ones1f", [1, 128], F32), ("ones1b", [1, 128], BF16),
    ("ones16b", [16, 1], BF16),
    ("knots_s", [1, 8], F32), ("dk_row", [1, 7], F32),
    ("kn_b", [128, 8], F32), ("dk_b", [128, 7], F32),
    ("itmp", [1, 1], F32), ("it_b", [128, 1], F32), ("osc_b", [128, 1], F32),
    ("epsc", [128, 1], F32), ("mhalf", [128, 1], F32), ("mone", [128, 1], F32),
    ("rsq2", [128, 1], F32), ("onef", [128, 1], F32),
    ("osc_s", [1, 1], F32), ("toff_s", [1, 1], F32), ("toff_b", [128, 1], F32),
    ("iota8", [128, 8], F32), ("iota8n", [128, 8], F32), ("iota64", [64, 1], F32),
    ("iotaw", [128, SUBW], F32),
    ("grpwk", [64, 8], F32), ("grpwk2", [64, 8], F32), ("grp_b", [64, 8], BF16),
    ("bcrow", [1, 32], F32),
    ("a_all", [128, 16 * 16], F32), ("cw_all", [128, 16 * 16], F32),
    ("base_all", [128, 16], F32), ("dcol", [128, 16], F32),
    ("bb", [128, 16], F32), ("cb", [128, 16], F32),
    ("tmp16", [128, 16], F32), ("tmp16c", [128, 16], F32),
    ("dbc", [96, 1], F32), ("dltb", [64, 1], BF16),
    ("x_sub", [128, 8 * SUBW], F32), ("xbf", [128, 2 * SUBW], BF16),
    ("xn", [128, 8 * SUBW], BF16),
    ("it2", [1, 2], F32),
    ("row_m", [1, SUBW], F32), ("row_rb", [1, SUBW], BF16), ("row_mb", [1, SUBW], BF16),
    ("ball", [128, 16], F32), ("tv", [128, 1], F32),
    ("pt", [128, 32 * 264], BF16),
    ("lnK", [128, 256], F32), ("sgn", [128, 256], F32), ("sgd", [128, 256], F32),
    ("dA256", [128, 256], F32), ("c256", [128, 1], F32), ("eye8", [128, 512], BF16),
    ("diagb", [128, 16 * 128], BF16),
    ("ball2", [128, 32], F32),
    ("y_all", [128, 16 * SUBW], BF16), ("zs2", [128, 2 * SUBW], BF16),
    ("s_all", [128, 8 * SUBW], BF16),
    ("gates", [72, SUBW], F32),
    ("gcT", [128, 4 * 8], F32), ("tgT", [128, 4 * 64], F32), ("rmax", [128, 4], F32),
    ("eq8", [128, 8], F32), ("midx", [128, 8], F32), ("cidx", [128, 4], F32),
    ("ohc", [128, 8], F32), ("selg", [128, 8], F32), ("lidx", [128, 1], F32),
    ("tidx", [128, 4], F32), ("trowb", [1, SUBW], BF16), ("ohT", [64, SUBW], BF16),
    ("w1s", [16, 2 * SUBW], BF16), ("b1s", [16, SUBW], BF16),
    ("w2s", [16, SUBW], BF16), ("b2s", [1, SUBW], BF16),
    ("comp2b", [2, SUBW], BF16),
    ("hh1", [16, SUBW], BF16), ("hh2", [16, SUBW], BF16), ("hhp", [16, SUBW], BF16),
    ("magb", [1, SUBW], BF16),
    ("h1x", [128, 2 * SUBW], BF16), ("h1e", [128, SUBW], BF16),
    ("dirsel", [128, 2 * SUBW], F32), ("ybf", [128, 2 * SUBW], BF16),
    ("gsil", [128, 2 * SUBW], BF16),
]

L1A_BUFS = [
    ("wq_b", [128, 8 * 1024], BF16), ("wk_b", [128, 8 * 1024], BF16),
    ("wv_b", [128, 8 * 1024], BF16), ("wo_b", [128, 8 * 1024], BF16),
    ("x_sub", [128, 8 * SUBW], BF16),
    ("xbf", [128, 2 * SUBW], BF16), ("xpt", [128, 2 * SUBW], BF16),
    ("xn1", [128, 8 * 520], BF16),
    ("row_m", [1, SUBW], F32), ("row_v", [1, SUBW], F32), ("row_r", [1, SUBW], F32),
    ("row_rb", [1, SUBW], BF16), ("row_mb", [1, SUBW], BF16),
    ("onesb", [128, 1], BF16), ("ones1b", [1, 128], BF16),
    ("epsc", [128, 1], F32), ("mhalf", [128, 1], F32),
    ("idb_s", [128, 128], BF16),
    ("mask0", [128, 136], F32), ("sc2t", [128, 1], F32), ("maskr", [128, 136], F32),
    ("qf", [128, 8 * SUBW], BF16), ("kf", [128, 8 * 520], BF16),
    ("vt0", [128, 1024], BF16), ("vt1", [128, 1024], BF16),
    ("vt2", [128, 1024], BF16), ("vt3", [128, 1024], BF16),
    ("vnx", [8, 1024], BF16),
    ("sm", [128, 4 * 136], F32), ("rmx", [128, 4], F32), ("nmx", [128, 4], F32),
    ("ebf", [128, 4 * 136], BF16), ("esum", [128, 4], F32), ("erec", [128, 4], F32),
    ("eta", [8, 4 * 128], BF16), ("etb", [128, 4 * 128], BF16),
    ("att", [128, 4 * 1024], BF16), ("atf", [128, 8 * SUBW], BF16),
    ("s2", [128, 8 * SUBW], BF16),
]

L1B_BUFS = [
    ("wg_b", [128, 8 * 3072], BF16), ("wu_b", [128, 8 * 3072], BF16),
    ("x_sub", [128, 8 * W2], BF16),
    ("xbf", [128, 2 * W2], BF16),
    ("xn1", [128, 8 * W2], BF16),
    ("s2s", [128, 2 * W2], BF16),
    ("hb", [128, 24 * W2], BF16),
    ("sg", [128, 2 * W2], BF16),
    ("outf", [128, 4 * W2], F32),
    ("row_v", [1, W2], F32), ("row_rb", [1, W2], BF16), ("row_mb", [1, W2], BF16),
    ("onesb", [128, 1], BF16), ("ones1b", [1, 128], BF16),
    ("epsc", [128, 1], F32), ("mhalf", [128, 1], F32),
    ("c127", [128, 1], F32), ("k3c", [128, 1], F32),
]


def l0_in_maps(inputs):
    x = np.ascontiguousarray(inputs["L0_attn_x"].reshape(SEQ, DM).T)
    iota_plain = np.broadcast_to(np.arange(SUBW, dtype=np.float32), (128, SUBW)).copy()
    iota_c0 = np.broadcast_to(np.maximum(np.arange(SUBW), 7).astype(np.float32), (128, SUBW)).copy()
    shared = dict(
        wzt=np.ascontiguousarray(inputs["L0_attn_W_in"][2048:4096, :].T),
        wx=np.ascontiguousarray(inputs["L0_attn_W_x"]),
        wdtt=np.ascontiguousarray(inputs["L0_attn_W_dt"].T),
        convw=np.ascontiguousarray(inputs["L0_attn_conv_w"].reshape(16, 128, 4).transpose(1, 0, 2).reshape(128, 64)),
        convb=np.ascontiguousarray(inputs["L0_attn_conv_b"].reshape(16, 128).T),
        bdt=np.ascontiguousarray(inputs["L0_attn_b_dt"].reshape(16, 128).T),
        alog=np.ascontiguousarray(inputs["L0_attn_A_log"]),
        dskip=np.ascontiguousarray(inputs["L0_attn_D_skip"].reshape(16, 128).T),
        wot=np.ascontiguousarray(inputs["L0_attn_W_out"].T),
        sigr=np.ascontiguousarray(inputs["L0_ffn_signatures_raw"]),
        knots=np.ascontiguousarray(inputs["L0_ffn_knot_values"].reshape(1, 8)),
        temp=np.ascontiguousarray(inputs["L0_ffn_temperature"].reshape(1, 1)),
        cw1=np.ascontiguousarray(inputs["L0_ffn_compress_w1"]),
        cb1=np.ascontiguousarray(inputs["L0_ffn_compress_b1"].reshape(2, 128).T),
        cw2=np.ascontiguousarray(inputs["L0_ffn_compress_w2"]),
        cb2=np.ascontiguousarray(inputs["L0_ffn_compress_b2"].reshape(2, 1)),
        sw1=np.ascontiguousarray(inputs["L0_ffn_spline_w1"].reshape(64, 32)),
        sb1=np.ascontiguousarray(inputs["L0_ffn_spline_b1"]),
        sw2=np.ascontiguousarray(inputs["L0_ffn_spline_w2"].reshape(64, 16)),
        sb2=np.ascontiguousarray(inputs["L0_ffn_spline_b2"]),
        dirs=np.ascontiguousarray(inputs["L0_ffn_directions"]),
        gam=np.ascontiguousarray(inputs["L0_ffn_norm_gamma"].reshape(8, 128).T),
        bet=np.ascontiguousarray(inputs["L0_ffn_norm_beta"].reshape(8, 128).T),
        osc=np.ascontiguousarray(inputs["L0_ffn_output_scale"].reshape(1, 1)),
        idf=np.eye(128, dtype=np.float32),
        idb=np.eye(128).astype(BF),
    )
    sel01 = np.zeros((2, 64), dtype=BF)
    sel01[0, 0:16] = 1
    sel01[1, 32:48] = 1
    shared["sel01"] = sel01
    maps = []
    for c in range(N_CORES):
        t0 = c * TCORE
        xs = np.zeros((DM, TLOC), dtype=np.float32)
        lo = t0 - HALO
        if lo < 0:
            xs[:, HALO:] = x[:, t0:t0 + TCORE]
        else:
            xs[:, :] = x[:, lo:t0 + TCORE]
        m = dict(shared)
        m["x_fm"] = xs
        m["toff"] = np.full((1, 1), float(t0 - HALO), dtype=np.float32)
        m["iotaw0"] = iota_c0 if c == 0 else iota_plain
        maps.append(m)
    return maps


def kernel(**inputs):
    nc, _ = build_fused()
    maps = l0_in_maps(inputs)

    band = np.full((128, 136), NEG, dtype=np.float32)
    i = np.arange(128)[:, None]
    c = np.arange(136)[None, :]
    band[(c - i >= 0) & (c - i <= 8)] = 0.0
    band0 = band.copy()
    band0[:, :8] = NEG
    l1 = dict(
        wq=np.ascontiguousarray(inputs["L1_attn_Wq"]),
        wk=np.ascontiguousarray(inputs["L1_attn_Wk"]),
        wv=np.ascontiguousarray(inputs["L1_attn_Wv"]),
        wo=np.ascontiguousarray(inputs["L1_attn_Wo"]),
        maskr_h=band,
        wgt=np.ascontiguousarray(inputs["L1_ffn_w_gate"].T),
        wut=np.ascontiguousarray(inputs["L1_ffn_w_up"].T),
        wdt=np.ascontiguousarray(inputs["L1_ffn_w_down"].T),
    )
    for cix in range(N_CORES):
        maps[cix].update(l1)
        maps[cix]["mask0_h"] = band0 if cix == 0 else band

    res = run_bass_kernel_spmd(nc, maps, core_ids=list(range(N_CORES)))
    out = np.concatenate([np.asarray(r["out_fm"]) for r in res.results], axis=1)
    return np.ascontiguousarray(out.T).reshape(1, SEQ, DM).astype(np.float32)


# version 46 (final)
# speedup vs baseline: 1.0022x; 1.0022x over previous
import contextlib
import numpy as np
import ml_dtypes

import concourse.bass as bass
import concourse.mybir as mybir
from concourse.bass_utils import run_bass_kernel_spmd

F32 = mybir.dt.float32
BF16 = mybir.dt.bfloat16
I32 = mybir.dt.int32
AF = mybir.ActivationFunctionType
ALU = mybir.AluOpType
AX = mybir.AxisListType

BF = ml_dtypes.bfloat16

N_CORES = 8
SEQ = 16384
DM = 1024
TCORE = SEQ // N_CORES
HALO = 8
TLOC = TCORE + HALO
SUBW = 512
SUBS = [(0, SUBW), (SUBW, SUBW), (2 * SUBW, SUBW), (3 * SUBW, SUBW), (4 * SUBW, HALO)]

ENGINES = ("tensor", "vector", "scalar", "gpsimd", "sync")


class Sched:

    def __init__(self):
        self.ops = []
        self.eops = {e: [] for e in ENGINES}
        self.acc = {}
        self._fence = []

    def fence(self):
        lasts = {}
        qd = {}
        for op in self.ops:
            if op["dma"]:
                qd.setdefault(op["eng"], []).append(op["id"])
            else:
                lasts[op["eng"]] = op["id"]
        ids = set(lasts.values())
        for lst in qd.values():
            ids.update(lst[-8:])
        self._fence = sorted(ids)
        self.acc = {}

    def op(self, eng, fn, r=(), w=(), dma=False, deps=(), tag=None):
        opid = len(self.ops)
        dep = set(deps)
        dep.update(self._fence)
        for (b, lo, hi) in r:
            for (k, l2, h2, oid) in self.acc.get(b, ()):
                if k == "w" and l2 < hi and lo < h2:
                    dep.add(oid)
        for (b, lo, hi) in w:
            for (k, l2, h2, oid) in self.acc.get(b, ()):
                if l2 < hi and lo < h2:
                    dep.add(oid)
        for (b, lo, hi) in r:
            self.acc.setdefault(b, []).append(("r", lo, hi, opid))
        for (b, lo, hi) in w:
            recs = self.acc.setdefault(b, [])
            recs[:] = [rec for rec in recs if not (lo <= rec[1] and rec[2] <= hi)]
            recs.append(("w", lo, hi, opid))
        op = dict(id=opid, eng=eng, fn=fn, dma=dma, deps=sorted(dep),
                  inc=False, tag=tag)
        self.ops.append(op)
        self.eops[eng].append(op)
        return opid

    def mark_inc(self, opids):
        for oid in opids:
            self.ops[oid]["inc"] = True

    def build(self, nc, stack):
        for op in self.ops:
            if op["dma"]:
                op["inc"] = True
            for d in op["deps"]:
                dop = self.ops[d]
                if dop["eng"] == op["eng"] and not dop["dma"] and op["eng"] == "tensor":
                    continue
                dop["inc"] = True
        NQ = 8
        cum = {}
        qcnt = {}
        domains = set()
        for op in self.ops:
            if op["dma"]:
                qi = qcnt.get(op["eng"], 0)
                qcnt[op["eng"]] = qi + 1
                dom = "q_%s_%d" % (op["eng"], qi % NQ)
            else:
                dom = op["eng"]
            op["dom"] = dom
            domains.add(dom)
            if op["inc"]:
                cum[dom] = cum.get(dom, 0) + (16 if op["dma"] else 1)
            op["cum"] = cum.get(dom, 0)
        sems = {d: stack.enter_context(nc.semaphore(f"s_{d}")) for d in sorted(domains)}

        know = {}
        ek = {e: {} for e in ENGINES}
        waitplan = {}
        nwaits = 0
        for op in self.ops:
            e = op["eng"]
            needs = {}
            for d in op["deps"]:
                dop = self.ops[d]
                dom, val = dop["dom"], dop["cum"]
                if dom == e and not dop["dma"] and e == "tensor":
                    continue
                if ek[e].get(dom, 0) < val:
                    needs[dom] = max(needs.get(dom, 0), val)
            if op["dma"] and op["inc"]:
                prev = op["cum"] - 16
                if prev > 0 and ek[e].get(op["dom"], 0) < prev:
                    needs[op["dom"]] = max(needs.get(op["dom"], 0), prev)
            for dom, val in needs.items():
                ek[e][dom] = max(ek[e].get(dom, 0), val)
            for d in op["deps"]:
                for dom2, v2 in know.get(d, {}).items():
                    if ek[e].get(dom2, 0) < v2:
                        ek[e][dom2] = v2
            waitplan[op["id"]] = sorted(needs.items())
            nwaits += len(needs)
            k = dict(ek[e])
            k[op["dom"]] = max(k.get(op["dom"], 0), op["cum"])
            know[op["id"]] = k
        self.stats = dict(nops=len(self.ops), nwaits=nwaits,
                          nincs=sum(1 for o in self.ops if o["inc"]))

        block = stack.enter_context(nc.Block())

        def mk(e):
            ops = self.eops[e]

            def body(eng_obj):
                for op in ops:
                    for dom, val in waitplan[op["id"]]:
                        eng_obj.wait_ge(sems[dom], val)
                    inst = op["fn"](eng_obj)
                    if op["inc"]:
                        inst.then_inc(sems[op["dom"]], 16 if op["dma"] else 1)
            return body

        for e in ENGINES:
            if self.eops[e]:
                getattr(block, e)(mk(e))
        return sems


class TBuf:
    def __init__(self, stack, nc, name, shape, dtype, psum=False):
        mk = nc.psum_tensor if psum else nc.sbuf_tensor
        self.h = stack.enter_context(mk(name, list(shape), dtype))
        self.name = name
        self.shape = shape

    def __getitem__(self, sl):
        return self.h[sl]

    def t(self, lo, hi):
        return (self.name, lo, hi)

    @property
    def full(self):
        return (self.name, 0, self.shape[-1])


class _ViewH:

    def __init__(self, typed, col0, ncols):
        self._typed, self._c0, self._n = typed, col0, ncols

    def ap(self):
        return self._typed[0:128, self._c0:self._c0 + self._n]


class TView:

    def __init__(self, typed, col0, name, shape):
        self.name = name
        self.shape = shape
        self._typed = typed
        self._c0 = col0
        self.h = _ViewH(typed, col0, shape[-1])

    def __getitem__(self, sl):
        p, c = sl
        p0 = p.start or 0
        p1 = self.shape[0] if p.stop is None else p.stop
        c0 = (c.start or 0) + self._c0
        c1 = (self.shape[-1] if c.stop is None else c.stop) + self._c0
        return self._typed[p0:p1, c0:c1]

    def t(self, lo, hi):
        return (self.name, lo, hi)

    @property
    def full(self):
        return (self.name, 0, self.shape[-1])


class Arena:

    def __init__(self, stack, nc, nbytes, name="ARENA"):
        self.nbytes = nbytes
        self.h8 = stack.enter_context(nc.sbuf_tensor(name, [128, nbytes], mybir.dt.uint8))
        self._typed = {}
        self.off = 0
        self.top = nbytes

    def reset(self):
        self.off = 0

    def carve_top(self, name, shape, dtype):
        esz = mybir.dt.size(dtype)
        per_part = 1
        for s in shape[1:]:
            per_part *= s
        per_part *= esz
        off = (self.top - per_part) & ~31
        assert off >= self.off, f"arena OOM (top) at {name}"
        self.top = off
        return TView(self.typed(dtype), off // esz, name, shape)

    def typed(self, dtype):
        if dtype not in self._typed:
            self._typed[dtype] = self.h8.bitcast(dtype)
        return self._typed[dtype]

    def carve(self, name, shape, dtype):
        esz = mybir.dt.size(dtype)
        per_part = 1
        for s in shape[1:]:
            per_part *= s
        per_part *= esz
        off = (self.off + 31) & ~31
        assert off + per_part <= self.top, \
            f"arena OOM at {name}: need {per_part}, have {self.top - off}"
        tv = TView(self.typed(dtype), off // esz, name, shape)
        self.off = off + per_part
        return tv


DEBUG_L0 = False


L0_HI_SPECS = [
        ("toff", [1, 1], F32), ("iotaw0", [128, SUBW], F32),
        ("wzt", [DM, 2048], F32), ("wx", [96, 2048], F32), ("wdtt", [64, 2048], F32),
        ("convw", [128, 64], F32), ("convb", [128, 16], F32),
        ("bdt", [128, 16], F32), ("alog", [2048, 16], F32), ("dskip", [128, 16], F32),
        ("wot", [2048, DM], F32), ("sigr", [64, DM], F32), ("knots", [1, 8], F32),
        ("temp", [1, 1], F32), ("cw1", [DM, 256], F32), ("cb1", [128, 2], F32),
        ("cw2", [256, 2], F32), ("cb2", [2, 1], F32), ("sw1", [64, 32], F32),
        ("sb1", [64, 16], F32), ("sw2", [64, 16], F32), ("sb2", [64, 1], F32),
        ("dirs", [64, DM], F32), ("gam", [128, 8], F32), ("bet", [128, 8], F32),
        ("osc", [1, 1], F32), ("idf", [128, 128], F32), ("idb", [128, 128], BF16),
        ("sel01", [2, 64], BF16),
]


def emit_l0_inner(S, nc, B, x_hbm, x1_hbm, HI):

    def T(name, lo, hi):
        return (name, lo, hi)

    def dma(dst_ap, src_ap, r=(), w=(), tag=None):
        return S.op("sync", lambda E, d=dst_ap, s=src_ap: E.dma_start(d, s),
                    r=r, w=w, dma=True, tag=tag)

    def gdma(dst_ap, src_ap, r=(), w=(), tag=None):
        return S.op("gpsimd", lambda E, d=dst_ap, s=src_ap: E.dma_start(d, s),
                    r=r, w=w, dma=True, tag=tag)

    stage = B["stage"]
    dbc = B["dbc"]
    dma(stage[0:96, 0:2048], HI["wx"][:, :], w=[stage.t(0, 2048)])
    S.op("vector", lambda E: E.reduce_sum(dbc[0:96, :], stage[0:96, 0:2048], axis=AX.X),
         r=[stage.t(0, 2048)], w=[dbc.full])
    gdma(B["xn"][0:64, 0:2048], HI["wdtt"][:, :], w=[("xn", 0, 2048)])
    for c in range(8):
        gdma(B["wzt_b"][:, c * 2048:(c + 1) * 2048], HI["wzt"][c * 128:(c + 1) * 128, :],
             w=[T("wzt_b", c * 2048, (c + 1) * 2048)])
    for c in range(16):
        gdma(B["wot_b"][:, c * 1024:(c + 1) * 1024], HI["wot"][c * 128:(c + 1) * 128, :],
             w=[T("wot_b", c * 1024, (c + 1) * 1024)])
    for c in range(8):
        gdma(B["cw1_b"][:, c * 256:(c + 1) * 256], HI["cw1"][c * 128:(c + 1) * 128, :],
             w=[T("cw1_b", c * 256, (c + 1) * 256)])
    for c in range(2):
        gdma(B["cw2_b"][:, c * 2:(c + 1) * 2], HI["cw2"][c * 128:(c + 1) * 128, :],
             w=[T("cw2_b", c * 2, (c + 1) * 2)])

    simple = [("bdt_s", "bdt"),  ("dskip_s", "dskip"),
              ("knots_s", "knots"), ("itmp", "temp"), ("gam_s", "gam"), ("bet_s", "bet"),
              ("idf_s", "idf"), ("idb_s", "idb"), ("osc_s", "osc"), ("toff_s", "toff"),
              ("sel01_s", "sel01"), ("iotaw", "iotaw0"),
              ("cb1_sb", "cb1"), ("cb2_s", "cb2")]
    for dst, src in simple:
        d = B[dst]
        dma(d[:, :], HI[src][:, :], w=[d.full])
    x_sub = B["x_sub"]
    dma(x_sub[0:64, 0:DM], HI["sigr"][:, :], w=[("x_sub", 0, DM)])
    dma(x_sub[:, 1280:1344], HI["convw"][:, :], w=[("x_sub", 1280, 1344)])
    dma(x_sub[:, 1344:1360], HI["convb"][:, :], w=[("x_sub", 1344, 1360)])
    for c in range(16):
        dma(x_sub[:, DM + c * 16:DM + (c + 1) * 16], HI["alog"][c * 128:(c + 1) * 128, :],
            w=[("x_sub", DM + c * 16, DM + (c + 1) * 16)])
    for nm_src, nm_dst, wdt in [("sw1", "sw1_b", 32), ("sb1", "sb1_b", 16),
                                ("sw2", "sw2_b", 16), ("sb2", "sb2_b", 1),
                                ("dirs", "dirs_b", DM)]:
        dstb = B[nm_dst]
        gdma(dstb[0:64, 0:wdt], HI[nm_src][:, :], w=[dstb.full])

    for nm, val in [("onesb", 1.0), ("ones1f", 1.0), ("ones1b", 1.0), ("ones16b", 1.0),
                    ("epsc", 1e-5), ("mhalf", -0.5), ("mone", -1.0),
                    ("rsq2", 0.7071067811865476), ("onef", 1.0), ("c256", 256.0)]:
        S.op("vector", lambda E, nm=nm, val=val: E.memset(B[nm][:, :], val), w=[B[nm].full])
    S.op("gpsimd", lambda E: E.iota(B["iota8"][:, :], [[1, 8]], base=0, channel_multiplier=0,
                                    allow_small_or_imprecise_dtypes=True), w=[B["iota8"].full])
    S.op("gpsimd", lambda E: E.iota(B["iota64"][:, :], [[1, 1]], base=0, channel_multiplier=1,
                                    allow_small_or_imprecise_dtypes=True), w=[B["iota64"].full])

    S.op("vector", lambda E: E.tensor_scalar(B["iota8n"][:, :], B["iota8"][:, :], 1.0, -1000.0,
                                             ALU.mult, ALU.add),
         r=[B["iota8"].full], w=[B["iota8n"].full])

    PTf, PTb = B["PTf"], B["PTb"]

    def bcast_row(dst_ap, dst_w, src_ap, src_tag, width):
        S.op("tensor", lambda E, d=dst_ap, s=src_ap, width=width: E.matmul(
            PTf[:, 0:width], B["ones1f"][0:1, :], s, start=True, stop=True),
            r=[B["ones1f"].full, src_tag], w=[T("PTf", 0, width)])
        S.op("vector", lambda E, d=dst_ap, width=width: E.tensor_copy(d, PTf[:, 0:width]),
             r=[T("PTf", 0, width)], w=[dst_w])

    knots, dk_row = B["knots_s"], B["dk_row"]
    S.op("vector", lambda E: E.tensor_tensor(dk_row[0:1, :], knots[0:1, 1:8], knots[0:1, 0:7],
                                             ALU.subtract),
         r=[knots.full], w=[dk_row.full])
    bcast_row(B["kn_b"][:, :], B["kn_b"].full, knots[0:1, :], knots.full, 8)
    bcast_row(B["dk_b"][:, :], B["dk_b"].full, dk_row[0:1, :], dk_row.full, 7)
    itmp, it2 = B["itmp"], B["it2"]
    S.op("vector", lambda E: E.tensor_scalar(itmp[0:1, :], itmp[0:1, :], 1e-6, None, ALU.add),
         r=[itmp.full], w=[itmp.full])
    S.op("vector", lambda E: E.reciprocal(it2[0:1, 0:1], itmp[0:1, :]), r=[itmp.full], w=[it2.t(0, 1)])
    S.op("vector", lambda E: E.tensor_tensor(it2[0:1, 1:2], itmp[0:1, 0:1], it2[0:1, 0:1], ALU.mult),
         r=[itmp.full, it2.t(0, 1)], w=[it2.t(1, 2)])
    S.op("vector", lambda E: E.tensor_scalar(it2[0:1, 1:2], it2[0:1, 1:2], -1.0, 2.0, ALU.mult, ALU.add),
         r=[it2.t(1, 2)], w=[it2.t(1, 2)])
    S.op("vector", lambda E: E.tensor_tensor(itmp[0:1, 0:1], it2[0:1, 0:1], it2[0:1, 1:2], ALU.mult),
         r=[it2.full], w=[itmp.full])
    bcast_row(B["it_b"][:, :], B["it_b"].full, itmp[0:1, :], itmp.full, 1)
    bcast_row(B["toff_b"][:, :], B["toff_b"].full, B["toff_s"][0:1, :], B["toff_s"].full, 1)
    bcast_row(B["osc_b"][:, :], B["osc_b"].full, B["osc_s"][0:1, :], B["osc_s"].full, 1)

    S.op("tensor", lambda E: E.transpose(PTf[0:1, 0:32], dbc[64:96, 0:1], B["idf_s"][64:96, 64:96]),
         r=[dbc.full, T("idf_s", 0, 128)], w=[T("PTf", 0, 32)])
    S.op("vector", lambda E: E.tensor_copy(B["bcrow"][0:1, :], PTf[0:1, 0:32]),
         r=[T("PTf", 0, 32)], w=[B["bcrow"].full])
    bcast_row(B["bb"][:, :], B["bb"].full, B["bcrow"][0:1, 0:16], B["bcrow"].t(0, 16), 16)
    bcast_row(B["cb"][:, :], B["cb"].full, B["bcrow"][0:1, 16:32], B["bcrow"].t(16, 32), 16)

    S.op("scalar", lambda E: E.copy(B["dltb"][0:64, 0:1], dbc[0:64, 0:1]),
         r=[dbc.full], w=[B["dltb"].full])
    P4 = B["P4"]
    for c in range(16):
        S.op("tensor", lambda E, c=c: E.matmul(P4[:, c:c + 1], B["xn"][0:64, c * 128:(c + 1) * 128],
                                               B["dltb"][0:64, 0:1], start=True, stop=True),
             r=[T("xn", c * 128, (c + 1) * 128), B["dltb"].full], w=[T("P4", c, c + 1)])
    tmp16, tmp16c, dcol = B["tmp16"], B["tmp16c"], B["dcol"]
    S.op("vector", lambda E: E.tensor_tensor(tmp16[:, :], P4[:, 0:16], B["bdt_s"][:, :], ALU.add),
         r=[T("P4", 0, 16), B["bdt_s"].full], w=[tmp16.full])
    S.op("scalar", lambda E: E.activation(tmp16c[:, :], tmp16[:, :], AF.Abs),
         r=[tmp16.full], w=[tmp16c.full])
    S.op("scalar", lambda E: E.activation(tmp16c[:, :], tmp16c[:, :], AF.Exp, scale=B["mone"][:, 0:1]),
         r=[tmp16c.full, B["mone"].full], w=[tmp16c.full])
    S.op("scalar", lambda E: E.activation(tmp16c[:, :], tmp16c[:, :], AF.Ln, bias=B["onef"][:, 0:1]),
         r=[tmp16c.full, B["onef"].full], w=[tmp16c.full])
    S.op("vector", lambda E: E.tensor_scalar(tmp16[:, :], tmp16[:, :], 0.0, None, ALU.max),
         r=[tmp16.full], w=[tmp16.full])
    S.op("vector", lambda E: E.tensor_tensor(dcol[:, :], tmp16[:, :], tmp16c[:, :], ALU.add),
         r=[tmp16.full, tmp16c.full], w=[dcol.full])

    a_all, cw_all, base_all = B["a_all"], B["cw_all"], B["base_all"]
    for c in range(16):
        asl = T("a_all", c * 16, (c + 1) * 16)
        S.op("scalar", lambda E, c=c: E.activation(a_all[:, c * 16:(c + 1) * 16],
                                                   x_sub[:, DM + c * 16:DM + (c + 1) * 16], AF.Exp),
             r=[T("x_sub", DM + c * 16, DM + (c + 1) * 16)], w=[asl])
        S.op("vector", lambda E, c=c: E.tensor_scalar(a_all[:, c * 16:(c + 1) * 16],
                                                      a_all[:, c * 16:(c + 1) * 16],
                                                      dcol[:, c:c + 1], -1.0, ALU.mult, ALU.mult),
             r=[asl, dcol.full], w=[asl])
        S.op("scalar", lambda E, c=c: E.activation(tmp16[:, :], a_all[:, c * 16:(c + 1) * 16], AF.Exp),
             r=[asl], w=[tmp16.full])
        S.op("vector", lambda E: E.tensor_scalar(tmp16[:, :], tmp16[:, :], -1.0, 1.0, ALU.mult, ALU.add),
             r=[tmp16.full], w=[tmp16.full])
        S.op("vector", lambda E: E.reciprocal(tmp16[:, :], tmp16[:, :]), r=[tmp16.full], w=[tmp16.full])
        S.op("vector", lambda E, c=c: E.tensor_scalar(tmp16c[:, :], B["bb"][:, :], dcol[:, c:c + 1],
                                                      None, ALU.mult),
             r=[B["bb"].full, dcol.full], w=[tmp16c.full])
        S.op("vector", lambda E: E.tensor_tensor(tmp16c[:, :], tmp16c[:, :], tmp16[:, :], ALU.mult),
             r=[tmp16c.full, tmp16.full], w=[tmp16c.full])
        csl = T("cw_all", c * 16, (c + 1) * 16)
        S.op("vector", lambda E, c=c: E.tensor_tensor(cw_all[:, c * 16:(c + 1) * 16], tmp16c[:, :],
                                                      B["cb"][:, :], ALU.mult),
             r=[tmp16c.full, B["cb"].full], w=[csl])
        S.op("vector", lambda E, c=c: E.reduce_sum(base_all[:, c:c + 1],
                                                   cw_all[:, c * 16:(c + 1) * 16], axis=AX.X),
             r=[csl], w=[T("base_all", c, c + 1)])
        S.op("vector", lambda E, c=c: E.tensor_tensor(base_all[:, c:c + 1], base_all[:, c:c + 1],
                                                      B["dskip_s"][:, c:c + 1], ALU.add),
             r=[T("base_all", c, c + 1), B["dskip_s"].full], w=[T("base_all", c, c + 1)])
    S.op("scalar", lambda E: E.activation(B["sgn"][:, 0:256], cw_all[:, 0:256], AF.Abs),
         r=[T("cw_all", 0, 256)], w=[B["sgn"].full])
    S.op("scalar", lambda E: E.activation(B["lnK"][:, 0:256], B["sgn"][:, 0:256], AF.Ln),
         r=[B["sgn"].full], w=[B["lnK"].full])
    S.op("vector", lambda E: E.tensor_scalar(B["sgn"][:, 0:256], cw_all[:, 0:256], 0.0, None, ALU.is_gt),
         r=[T("cw_all", 0, 256)], w=[B["sgn"].full])
    S.op("vector", lambda E: E.tensor_scalar(B["sgn"][:, 0:256], B["sgn"][:, 0:256], 2.0, -1.0,
                                             ALU.mult, ALU.add),
         r=[B["sgn"].full], w=[B["sgn"].full])
    S.op("scalar", lambda E: E.activation(B["dA256"][:, 0:256], a_all[:, 0:256], AF.Exp,
                                          scale=B["c256"][:, 0:1]),
         r=[T("a_all", 0, 256), B["c256"].full], w=[B["dA256"].full])
    S.op("vector", lambda E: E.tensor_tensor(B["sgd"][:, 0:256], B["sgn"][:, 0:256],
                                             B["dA256"][:, 0:256], ALU.mult),
         r=[B["sgn"].full, B["dA256"].full], w=[B["sgd"].full])
    for k8 in range(4):
        S.op("gpsimd", lambda E, k8=k8: E.tensor_copy(B["eye8"][:, k8 * 128:(k8 + 1) * 128],
                                                      B["idb_s"][:, :]),
             r=[T("idb_s", 0, 128)], w=[T("eye8", k8 * 128, (k8 + 1) * 128)])

    sigr, sigq_b = B["x_sub"], B["sigq_b"]
    grpwk, grpwk2 = B["grpwk"], B["grpwk2"]
    S.op("vector", lambda E: E.tensor_scalar(B["h1x"][0:64, 0:DM], sigr[0:64, 0:DM], 0.3, None, ALU.is_gt),
         r=[("x_sub", 0, DM)], w=[T("h1x", 0, DM)])
    S.op("vector", lambda E: E.tensor_scalar(B["h1e"][0:64, 0:SUBW], sigr[0:64, 0:SUBW], -0.3, None, ALU.is_lt),
         r=[("x_sub", 0, DM)], w=[T("h1e", 0, SUBW)])
    S.op("vector", lambda E: E.tensor_scalar(B["ybf"][0:64, 0:SUBW], sigr[0:64, SUBW:DM], -0.3, None, ALU.is_lt),
         r=[("x_sub", 0, DM)], w=[T("ybf", 0, SUBW)])
    S.op("vector", lambda E: E.tensor_tensor(sigq_b[0:64, 0:SUBW], B["h1x"][0:64, 0:SUBW],
                                             B["h1e"][0:64, 0:SUBW], ALU.subtract),
         r=[T("h1x", 0, SUBW), T("h1e", 0, SUBW)], w=[sigq_b.t(0, SUBW)])
    S.op("vector", lambda E: E.tensor_tensor(sigq_b[0:64, SUBW:DM], B["h1x"][0:64, SUBW:DM],
                                             B["ybf"][0:64, 0:SUBW], ALU.subtract),
         r=[T("h1x", SUBW, DM), T("ybf", 0, SUBW)], w=[sigq_b.t(SUBW, DM)])
    for c in range(8):
        S.op("tensor", lambda E, c=c: E.transpose(PTb[:, 0:64], sigq_b[0:64, c * 128:(c + 1) * 128],
                                                  B["idb_s"][0:64, 0:64]),
             r=[sigq_b.full, T("idb_s", 0, 128)], w=[T("PTb", 0, 64)])
        S.op("vector", lambda E, c=c: E.tensor_copy(B["sigsT"][:, c * 64:(c + 1) * 64], PTb[:, 0:64]),
             r=[T("PTb", 0, 64)], w=[T("sigsT", c * 64, (c + 1) * 64)])
    S.op("gpsimd", lambda E: E.iota(grpwk[0:64, 0:8], [[-8, 8]], base=0, channel_multiplier=1,
                                    allow_small_or_imprecise_dtypes=True), w=[grpwk.full])
    S.op("vector", lambda E: E.tensor_scalar(grpwk2[0:64, :], grpwk[0:64, :], -0.5, None, ALU.is_ge),
         r=[grpwk.full], w=[grpwk2.full])
    S.op("vector", lambda E: E.tensor_scalar(grpwk[0:64, :], grpwk[0:64, :], 7.5, None, ALU.is_le),
         r=[grpwk.full], w=[grpwk.full])
    S.op("vector", lambda E: E.tensor_tensor(grpwk[0:64, :], grpwk[0:64, :], grpwk2[0:64, :], ALU.mult),
         r=[grpwk.full, grpwk2.full], w=[grpwk.full])
    S.op("scalar", lambda E: E.copy(B["grp_b"][0:64, :], grpwk[0:64, :]),
         r=[grpwk.full], w=[B["grp_b"].full])
    csig_b = B["csig_b"]
    P5 = B["P5"]
    for half in range(2):
        lo = half * 512
        S.op("tensor", lambda E, lo=lo: E.matmul(P5[0:8, 0:512], B["grp_b"][0:64, 0:8],
                                                 sigq_b[0:64, lo:lo + 512], start=True, stop=True),
             r=[B["grp_b"].full, sigq_b.full], w=[T("P5", 0, 512)])
        S.op("vector", lambda E, lo=lo: E.tensor_scalar(B["h1x"][0:8, 0:512], P5[0:8, 0:512],
                                                        0.0, None, ALU.is_gt),
             r=[T("P5", 0, 512)], w=[T("h1x", 0, 512)])
        S.op("vector", lambda E, lo=lo: E.tensor_scalar(B["h1e"][0:8, 0:512], P5[0:8, 0:512],
                                                        0.0, None, ALU.is_lt),
             r=[T("P5", 0, 512)], w=[T("h1e", 0, 512)])
        S.op("vector", lambda E, lo=lo: E.tensor_tensor(csig_b[0:8, lo:lo + 512], B["h1x"][0:8, 0:512],
                                                        B["h1e"][0:8, 0:512], ALU.subtract),
             r=[T("h1x", 0, 512), T("h1e", 0, 512)], w=[csig_b.t(lo, lo + 512)])
    for c in range(8):
        S.op("tensor", lambda E, c=c: E.transpose(PTb[:, 0:8], csig_b[0:8, c * 128:(c + 1) * 128],
                                                  B["idb_s"][0:8, 0:8]),
             r=[csig_b.full, T("idb_s", 0, 128)], w=[T("PTb", 0, 8)])
        S.op("vector", lambda E, c=c: E.tensor_copy(B["csigT"][:, c * 8:(c + 1) * 8], PTb[:, 0:8]),
             r=[T("PTb", 0, 8)], w=[T("csigT", c * 8, (c + 1) * 8)])

    DB = {}
    if DEBUG_L0:
        DB["xn"] = nc.dram_tensor("xn_dbg", [DM, TLOC], BF16, kind="ExternalOutput")
        DB["y"] = nc.dram_tensor("y_dbg", [2048, TLOC], BF16, kind="ExternalOutput")
        DB["s"] = nc.dram_tensor("s_dbg", [DM, TLOC], BF16, kind="ExternalOutput")
        DB["tidx"] = nc.dram_tensor("tidx_dbg", [1, TLOC], BF16, kind="ExternalOutput")
        DB["gy"] = nc.dram_tensor("gy_dbg", [DM, TLOC], F32, kind="ExternalOutput")
        DB["gates"] = nc.dram_tensor("gates_dbg", [72, TLOC], F32, kind="ExternalOutput")
    for si, (ls, w) in enumerate(SUBS):
        emit_l0_sub(S, nc, B, x_hbm, x1_hbm, si, ls, w, DB)
        if si == 0:
            S.op("gpsimd", lambda E: E.iota(B["iotaw"][:, :], [[1, SUBW]], base=0,
                                            channel_multiplier=0,
                                            allow_small_or_imprecise_dtypes=True),
                 w=[B["iotaw"].full])




def emit_l0_sub(S, nc, B, x_hbm, x1_hbm, si, ls, w, DB=None):
    DB = DB or {}

    def T(name, lo, hi):
        return (name, lo, hi)

    def dbg_chunked(key, buf, nch):
        if key not in DB:
            return
        d = DB[key].rearrange("(c p) t -> p c t", p=128)[:, :, ls:ls + w]
        s = buf.h.ap().rearrange("p (c t) -> p c t", c=nch)[:, :, 0:w]
        S.op("sync", lambda E, d=d, s=s: E.dma_start(d, s), r=[buf.full], dma=True, tag="out_dma")

    def dbg_flat(key, src_ap, r_tag, rows):
        if key not in DB:
            return
        d = DB[key][0:rows, ls:ls + w]
        S.op("sync", lambda E, d=d, s=src_ap: E.dma_start(d, s), r=[r_tag], dma=True, tag="out_dma")

    P0, P1, P2, P3, P4, P5 = (B[k] for k in ["P0", "P1", "P2", "P3", "P4", "P5"])
    PTf = B["PTf"]
    x_sub, xbf, xn, yf = B["x_sub"], B["xbf"], B["xn"], B["yf"]
    pt = B["pt"]
    row_m = B["row_m"]
    row_v = B["dirsel"]
    row_r = B["dirsel"]
    row_rb, row_mb = B["row_rb"], B["row_mb"]
    iotaw = B["iotaw"]
    iotaw_tag = B["iotaw"].full

    for hf in range(4):
        srch = x_hbm.rearrange("(c p) t -> p c t", p=128)[:, hf * 2:(hf + 1) * 2, ls:ls + w]
        dsth = x_sub.h.ap().rearrange("p (c t) -> p c t", c=8)[:, hf * 2:(hf + 1) * 2, 0:w]
        S.op("sync", lambda E, d=dsth, s=srch: E.dma_start(d, s),
             w=[("x_sub", hf * 2 * SUBW, (hf + 1) * 2 * SUBW)], dma=True)

    a_all, cw_all, base_all = B["a_all"], B["cw_all"], B["base_all"]
    ball2, tv, y_all = B["ball2"], B["tv"], B["y_all"]
    S.op("vector", lambda E, ls=ls: E.tensor_scalar(tv[:, :], B["toff_b"][:, :], float(ls + 1),
                                                    None, ALU.add),
         r=[B["toff_b"].full], w=[tv.full])
    ew = min(w, 264)
    lnK, sgn, sgd, diagb, eye8 = B["lnK"], B["sgn"], B["sgd"], B["diagb"], B["eye8"]
    S.op("vector", lambda E, ls=ls: E.tensor_scalar(tv[:, :], B["toff_b"][:, :], float(ls + 1),
                                                    None, ALU.add),
         r=[B["toff_b"].full], w=[tv.full])
    bdeps = B.setdefault("_mamba_bdep", {0: [], 1: []})
    ball_all = B["cw_all"]
    S.op("vector", lambda E: E.tensor_scalar(ball_all[:, 0:256], a_all[:, 0:256],
                                             tv[:, 0:1], None, ALU.mult),
         r=[T("a_all", 0, 256), tv.full], w=[T("cw_all", 0, 256)])
    S.op("vector", lambda E: E.tensor_tensor(ball_all[:, 0:256], ball_all[:, 0:256],
                                             lnK[:, 0:256], ALU.add),
         r=[T("cw_all", 0, 256), lnK.full], w=[T("cw_all", 0, 256)])

    def emit_mamba(c0, c1):
        for c in range(c0, c1):
            par = c % 2
            bs = par * 16
            PSY = B["P4"] if par == 0 else B["P5"]
            psy = "P4" if par == 0 else "P5"
            S.op("vector", lambda E, c=c, bs=bs: E.tensor_scalar(ball2[:, bs:bs + 16],
                                                                 a_all[:, c * 16:(c + 1) * 16],
                                                                 tv[:, 0:1], None, ALU.mult),
                 r=[T("a_all", c * 16, (c + 1) * 16), tv.full], w=[T("ball2", bs, bs + 16)])
            S.op("vector", lambda E, c=c, bs=bs: E.tensor_tensor(ball2[:, bs:bs + 16],
                                                                 ball2[:, bs:bs + 16],
                                                                 lnK[:, c * 16:(c + 1) * 16], ALU.add),
                 r=[T("ball2", bs, bs + 16), lnK.full], w=[T("ball2", bs, bs + 16)])
            lastmm = None
            for hh in (0, 1):
                n0 = hh * 8
                hdep = list(bdeps[par])
                eids = []
                for n in range(n0, n0 + 8):
                    col = c * 16 + n
                    sl = (par * 16 + n) * 264
                    eid = S.op("scalar", lambda E, col=col, sl=sl: E.activation(
                        pt[:, sl:sl + ew], iotaw[:, 0:ew], AF.Exp,
                        scale=a_all[:, col:col + 1], bias=ball_all[:, col:col + 1]),
                        r=[iotaw_tag, T("a_all", col, col + 1), T("cw_all", col, col + 1)],
                        w=[T("pt", sl, sl + ew)], deps=hdep)
                    eids.append(eid)
                edep = [eids[-1]]
                do = hh * 1024
                for q4 in range(2):
                    S.op("vector", lambda E, c=c, n0=n0, do=do, q4=q4: E.tensor_tensor(
                        diagb[:, do + q4 * 512:do + (q4 + 1) * 512].rearrange("p (n k) -> p n k", n=4),
                        eye8[:, 0:512].rearrange("p (n k) -> p n k", n=4),
                        sgn[:, c * 16 + n0 + q4 * 4:c * 16 + n0 + q4 * 4 + 4]
                           .rearrange("p (n one) -> p n one", one=1)
                           .broadcast_to([128, 4, 128]),
                        ALU.mult),
                        r=[T("eye8", 0, 512), sgn.full], w=[T("diagb", do + q4 * 512, do + (q4 + 1) * 512)])
                for n in range(n0, n0 + 8):
                    sl = (par * 16 + n) * 264
                    dgo = do + (n % 8) * 128
                    lastmm = S.op("tensor", lambda E, sl=sl, dgo=dgo, n=n, PSY=PSY: E.matmul(
                        PSY[:, 0:ew], diagb[:, dgo:dgo + 128], pt[:, sl:sl + ew],
                        start=(n == 0), stop=(n == 15)),
                        r=[T("diagb", dgo, dgo + 128), T("pt", sl, sl + ew)],
                        w=[T(psy, 0, ew)], deps=edep)
            if w > 264:
                for hh in (0, 1):
                    n0 = hh * 8
                    do = hh * 1024
                    for q4 in range(2):
                        S.op("vector", lambda E, c=c, n0=n0, do=do, q4=q4: E.tensor_tensor(
                            diagb[:, do + q4 * 512:do + (q4 + 1) * 512].rearrange("p (n k) -> p n k", n=4),
                            eye8[:, 0:512].rearrange("p (n k) -> p n k", n=4),
                            sgd[:, c * 16 + n0 + q4 * 4:c * 16 + n0 + q4 * 4 + 4]
                               .rearrange("p (n one) -> p n one", one=1)
                               .broadcast_to([128, 4, 128]),
                            ALU.mult),
                            r=[T("eye8", 0, 512), sgd.full], w=[T("diagb", do + q4 * 512, do + (q4 + 1) * 512)])
                    for n in range(n0, n0 + 8):
                        sl = (par * 16 + n) * 264
                        dgo = do + (n % 8) * 128
                        lastmm = S.op("tensor", lambda E, sl=sl, dgo=dgo, n=n, PSY=PSY: E.matmul(
                            PSY[:, 264:w], diagb[:, dgo:dgo + 128], pt[:, sl + 8:sl + 256],
                            start=(n == 0), stop=(n == 15)),
                            r=[T("diagb", dgo, dgo + 128), T("pt", sl + 8, sl + 256)],
                            w=[T(psy, 264, w)])
            S.op("vector", lambda E, c=c, PSY=PSY, psy=psy: E.tensor_scalar(
                y_all[:, c * SUBW:c * SUBW + w], PSY[:, 0:w],
                -1.0, base_all[:, c:c + 1], ALU.mult, ALU.add),
                r=[T(psy, 0, w), T("base_all", c, c + 1)],
                w=[T("y_all", c * SUBW, c * SUBW + w)])
            bdeps[par] = [lastmm]

    emit_mamba(0, 8)

    for c in range(8):
        sl = (c % 2) * SUBW
        S.op("scalar", lambda E, c=c, sl=sl: E.copy(xbf[:, sl:sl + w], x_sub[:, c * SUBW:c * SUBW + w]),
             r=[T("x_sub", c * SUBW, c * SUBW + w)], w=[T("xbf", sl, sl + w)])
        S.op("gpsimd", lambda E, sl=sl: E.tensor_tensor(pt[:, sl:sl + w], xbf[:, sl:sl + w],
                                                        xbf[:, sl:sl + w], ALU.mult),
             r=[T("xbf", sl, sl + w)], w=[T("pt", sl, sl + w)])
        S.op("tensor", lambda E, c=c, sl=sl: E.matmul(P4[0:1, 0:w], B["onesb"][:, :], xbf[:, sl:sl + w],
                                                      start=(c == 0), stop=(c == 7)),
             r=[T("xbf", sl, sl + w), B["onesb"].full], w=[T("P4", 0, w)])
        S.op("tensor", lambda E, c=c, sl=sl: E.matmul(P5[0:1, 0:w], B["onesb"][:, :], pt[:, sl:sl + w],
                                                      start=(c == 0), stop=(c == 7)),
             r=[T("pt", sl, sl + w), B["onesb"].full], w=[T("P5", 0, w)])

    def ln_rows():
        S.op("vector", lambda E: E.tensor_scalar(row_m[0:1, 0:w], P4[0:1, 0:w], 1.0 / DM, None, ALU.mult),
             r=[T("P4", 0, w)], w=[T("row_m", 0, w)])
        S.op("vector", lambda E: E.tensor_scalar(row_v[0:1, SUBW:SUBW + w], P5[0:1, 0:w], 1.0 / DM, None, ALU.mult),
             r=[T("P5", 0, w)], w=[T("dirsel", SUBW, SUBW + w)])
        S.op("vector", lambda E: E.tensor_tensor(row_r[0:1, 0:w], row_m[0:1, 0:w], row_m[0:1, 0:w],
                                                 ALU.mult),
             r=[T("row_m", 0, w)], w=[T("dirsel", 0, w)])
        S.op("vector", lambda E: E.tensor_tensor(row_v[0:1, SUBW:SUBW + w], row_v[0:1, SUBW:SUBW + w], row_r[0:1, 0:w],
                                                 ALU.subtract),
             r=[T("dirsel", SUBW, SUBW + w), T("dirsel", 0, w)], w=[T("dirsel", SUBW, SUBW + w)])
        S.op("scalar", lambda E: E.activation(row_r[0:1, 0:w], row_v[0:1, SUBW:SUBW + w], AF.Ln,
                                              bias=B["epsc"][0:1, 0:1]),
             r=[T("dirsel", SUBW, SUBW + w), B["epsc"].full], w=[T("dirsel", 0, w)])
        S.op("scalar", lambda E: E.activation(row_r[0:1, 0:w], row_r[0:1, 0:w], AF.Exp,
                                              scale=B["mhalf"][0:1, 0:1]),
             r=[T("dirsel", 0, w), B["mhalf"].full], w=[T("dirsel", 0, w)])
        S.op("tensor", lambda E: E.matmul(P4[:, 0:w], B["ones1f"][0:1, :], row_m[0:1, 0:w],
                                          start=True, stop=True),
             r=[T("row_m", 0, w), B["ones1f"].full], w=[T("P4", 0, w)])
        S.op("tensor", lambda E: E.matmul(P5[:, 0:w], B["ones1f"][0:1, :], row_r[0:1, 0:w],
                                          start=True, stop=True),
             r=[T("dirsel", 0, w), B["ones1f"].full], w=[T("P5", 0, w)])

    ln_rows()
    xnf = B["xnf"]
    for c in range(8):
        S.op("vector", lambda E, c=c: E.tensor_tensor(xnf[:, c * SUBW:c * SUBW + w],
                                                      x_sub[:, c * SUBW:c * SUBW + w],
                                                      P4[:, 0:w], ALU.subtract),
             r=[T("x_sub", c * SUBW, c * SUBW + w), T("P4", 0, w)], w=[T("stage", c * SUBW, c * SUBW + w)])
        S.op("vector", lambda E, c=c: E.tensor_tensor(xnf[:, c * SUBW:c * SUBW + w],
                                                      xnf[:, c * SUBW:c * SUBW + w],
                                                      P5[:, 0:w], ALU.mult),
             r=[T("stage", c * SUBW, c * SUBW + w), T("P5", 0, w)], w=[T("stage", c * SUBW, c * SUBW + w)])
        S.op("gpsimd", lambda E, c=c: E.tensor_copy(xn[:, c * SUBW:c * SUBW + w],
                                                    xnf[:, c * SUBW:c * SUBW + w]),
             r=[T("stage", c * SUBW, c * SUBW + w)], w=[T("xn", c * SUBW, c * SUBW + w)])
    dbg_chunked("xn", xn, 8)

    emit_mamba(8, 16)
    dbg_chunked("y", y_all, 16)

    zs2, u_all = B["zs2"], B["y_all"]
    for c in range(16):
        pz = ("P0", "P1")[c % 2]
        PZ = B[pz]
        for k in range(8):
            S.op("tensor", lambda E, c=c, k=k, PZ=PZ: E.matmul(
                PZ[:, 0:w], B["wzt_b"][:, k * 2048 + c * 128:k * 2048 + (c + 1) * 128],
                xn[:, k * SUBW:k * SUBW + w], start=(k == 0), stop=(k == 7)),
                r=[T("wzt_b", k * 2048 + c * 128, k * 2048 + (c + 1) * 128),
                   T("xn", k * SUBW, k * SUBW + w)],
                w=[T(pz, 0, w)])
        ssl = (c % 2) * SUBW
        S.op("scalar", lambda E, pz=pz, PZ=PZ, ssl=ssl: E.activation(zs2[:, ssl:ssl + w],
                                                                      PZ[:, 0:w], AF.Silu),
             r=[T(pz, 0, w)], w=[T("zs2", ssl, ssl + w)])
        S.op("gpsimd", lambda E, c=c, ssl=ssl: E.tensor_tensor(u_all[:, c * SUBW:c * SUBW + w],
                                                               y_all[:, c * SUBW:c * SUBW + w],
                                                               zs2[:, ssl:ssl + w], ALU.mult),
             r=[T("y_all", c * SUBW, c * SUBW + w), T("zs2", ssl, ssl + w)],
             w=[T("y_all", c * SUBW, c * SUBW + w)])

    s_all = B["s_all"]
    for dc in range(8):
        pz = ("P2", "P3")[dc % 2]
        PZ = B[pz]
        for ec in range(16):
            S.op("tensor", lambda E, dc=dc, ec=ec, PZ=PZ: E.matmul(
                PZ[:, 0:w], B["wot_b"][:, ec * DM + dc * 128:ec * DM + (dc + 1) * 128],
                u_all[:, ec * SUBW:ec * SUBW + w], start=(ec == 0), stop=(ec == 15)),
                r=[T("wot_b", ec * DM + dc * 128, ec * DM + (dc + 1) * 128),
                   T("y_all", ec * SUBW, ec * SUBW + w)],
                w=[T(pz, 0, w)])
        S.op("scalar", lambda E, dc=dc, pz=pz, PZ=PZ: E.copy(s_all[:, dc * SUBW:dc * SUBW + w],
                                                             PZ[:, 0:w]),
             r=[T(pz, 0, w)], w=[T("s_all", dc * SUBW, dc * SUBW + w)])
    dbg_chunked("s", s_all, 8)

    u7, seg, gates = B["dirsel"], B["dirsel"], B["gates"]
    for k in range(8):
        S.op("tensor", lambda E, k=k: E.matmul(
            P2[0:64, 0:w], B["sigsT"][:, k * 64:(k + 1) * 64],
            xnf[:, k * SUBW:k * SUBW + w],
            start=(k == 0), stop=(k == 7)),
            r=[T("sigsT", k * 64, (k + 1) * 64), T("stage", k * SUBW, k * SUBW + w)],
            w=[T("P2", 0, w)])
    for k in range(8):
        S.op("tensor", lambda E, k=k: E.matmul(
            P2[64:72, 0:w], B["csigT"][:, k * 8:(k + 1) * 8],
            xnf[:, k * SUBW:k * SUBW + w],
            start=(k == 0), stop=(k == 7)),
            r=[T("csigT", k * 8, (k + 1) * 8), T("stage", k * SUBW, k * SUBW + w)],
            w=[T("P2", 0, w)])
    S.op("scalar", lambda E: E.activation(u7[0:72, SUBW:SUBW + w], P2[0:72, 0:w], AF.Sigmoid,
                                          scale=B["it_b"][0:72, 0:1]),
         r=[T("P2", 0, w), B["it_b"].full], w=[T("dirsel", SUBW, SUBW + w)])
    S.op("vector", lambda E: E.tensor_scalar(u7[0:72, SUBW:SUBW + w], u7[0:72, SUBW:SUBW + w], 7.0, None, ALU.mult),
         r=[T("dirsel", SUBW, SUBW + w)], w=[T("dirsel", SUBW, SUBW + w)])
    for k in range(7):
        if k == 0:
            S.op("vector", lambda E: E.tensor_scalar(seg[0:72, 0:w], u7[0:72, SUBW:SUBW + w], 0.0, 1.0,
                                                     ALU.max, ALU.min),
                 r=[T("dirsel", SUBW, SUBW + w)], w=[T("dirsel", 0, w)])
            S.op("vector", lambda E: E.tensor_scalar(gates[0:72, 0:w], seg[0:72, 0:w],
                                                     B["dk_b"][0:72, 0:1], B["kn_b"][0:72, 0:1],
                                                     ALU.mult, ALU.add),
                 r=[T("dirsel", 0, w), B["dk_b"].full, B["kn_b"].full], w=[T("gates", 0, w)])
        else:
            S.op("vector", lambda E, k=k: E.tensor_scalar(seg[0:72, 0:w], u7[0:72, SUBW:SUBW + w],
                                                          float(-k), None, ALU.add),
                 r=[T("dirsel", SUBW, SUBW + w)], w=[T("dirsel", 0, w)])
            S.op("vector", lambda E: E.tensor_scalar(seg[0:72, 0:w], seg[0:72, 0:w], 0.0, 1.0,
                                                     ALU.max, ALU.min),
                 r=[T("dirsel", 0, w)], w=[T("dirsel", 0, w)])
            S.op("vector", lambda E, k=k: E.scalar_tensor_tensor(gates[0:72, 0:w], seg[0:72, 0:w],
                                                                 B["dk_b"][0:72, k:k + 1],
                                                                 gates[0:72, 0:w], ALU.mult, ALU.add),
                 r=[T("dirsel", 0, w), T("gates", 0, w), B["dk_b"].full], w=[T("gates", 0, w)])
    dbg_flat("gates", gates[0:72, 0:w], ("gates", 0, w), 72)

    nblk = (w + 127) // 128
    gcT, tgT, rmax, eq8, midx = B["gcT"], B["tgT"], B["rmax"], B["eq8"], B["midx"]
    cidx, ohc, selg, lidx, tidx = B["cidx"], B["ohc"], B["selg"], B["lidx"], B["tidx"]
    trowb = B["trowb"]
    if nblk == 4:
        for b in range(4):
            S.op("tensor", lambda E, b=b: E.transpose(PTf[0:128, 0:8],
                                                      gates[64:72, b * 128:(b + 1) * 128],
                                                      B["idf_s"][64:72, 64:72]),
                 r=[T("gates", 0, w), T("idf_s", 0, 128)], w=[T("PTf", 0, 8)])
            S.op("vector", lambda E, b=b: E.tensor_copy(gcT[0:128, b * 8:(b + 1) * 8], PTf[0:128, 0:8]),
                 r=[T("PTf", 0, 8)], w=[T("gcT", b * 8, (b + 1) * 8)])
            S.op("tensor", lambda E, b=b: E.transpose(PTf[0:128, 0:64],
                                                      gates[0:64, b * 128:(b + 1) * 128],
                                                      B["idf_s"][0:64, 0:64]),
                 r=[T("gates", 0, w), T("idf_s", 0, 128)], w=[T("PTf", 0, 64)])
            S.op("vector", lambda E, b=b: E.tensor_copy(tgT[0:128, b * 64:(b + 1) * 64],
                                                        PTf[0:128, 0:64]),
                 r=[T("PTf", 0, 64)], w=[T("tgT", b * 64, (b + 1) * 64)])
            S.op("vector", lambda E, b=b: E.tensor_copy(yf[:, 388 + b * 8:388 + (b + 1) * 8],
                                                        B["iota8"][0:128, 0:8]),
                 r=[B["iota8"].full], w=[T("stage", 388 + b * 8, 388 + (b + 1) * 8)])
            S.op("vector", lambda E, b=b: E.tensor_copy(yf[:, 420 + b * 8:420 + (b + 1) * 8],
                                                        B["iota8n"][0:128, 0:8]),
                 r=[B["iota8n"].full], w=[T("stage", 420 + b * 8, 420 + (b + 1) * 8)])
        r3 = lambda ap, b=4, g=8: ap.rearrange("p (b g) -> p b g", b=b)
        c3 = lambda ap: ap.rearrange("p (b one) -> p b one", one=1)
        S.op("vector", lambda E: E.reduce_max(c3(rmax[0:128, 0:4]), r3(gcT[0:128, 0:32]), axis=AX.X),
             r=[T("gcT", 0, 32)], w=[T("rmax", 0, 4)])
        S.op("vector", lambda E: E.tensor_tensor(
            r3(yf[:, 288:320]), r3(gcT[0:128, 0:32]),
            c3(rmax[0:128, 0:4]).broadcast_to([128, 4, 8]), ALU.is_ge),
            r=[T("gcT", 0, 32), T("rmax", 0, 4)], w=[T("stage", 288, 320)])
        S.op("vector", lambda E: E.tensor_tensor(r3(yf[:, 320:352]), r3(yf[:, 288:320]),
                                                 r3(yf[:, 420:452]), ALU.mult),
             r=[T("stage", 288, 320), T("stage", 420, 452)], w=[T("stage", 320, 352)])
        S.op("vector", lambda E: E.tensor_scalar(yf[:, 320:352], yf[:, 320:352], 1000.0, None, ALU.add),
             r=[T("stage", 320, 352)], w=[T("stage", 320, 352)])
        S.op("vector", lambda E: E.tensor_reduce(c3(cidx[0:128, 0:4]), r3(yf[:, 320:352]),
                                                 axis=AX.X, op=ALU.min),
             r=[T("stage", 320, 352)], w=[T("cidx", 0, 4)])
        S.op("vector", lambda E: E.tensor_tensor(
            r3(yf[:, 288:320]), r3(yf[:, 388:420]),
            c3(cidx[0:128, 0:4]).broadcast_to([128, 4, 8]), ALU.is_equal),
            r=[T("stage", 388, 420), T("cidx", 0, 4)], w=[T("stage", 288, 320)])
        S.op("vector", lambda E: E.tensor_tensor(
            yf[:, 0:256].rearrange("p (b g j) -> p b g j", b=4, g=8),
            tgT[0:128, 0:256].rearrange("p (b g j) -> p b g j", b=4, g=8),
            yf[:, 288:320].rearrange("p (b g one) -> p b g one", b=4, g=8)
              .broadcast_to([128, 4, 8, 8]),
            ALU.mult),
            r=[T("tgT", 0, 256), T("stage", 288, 320)], w=[T("stage", 0, 256)])
        S.op("vector", lambda E: E.tensor_reduce(
            yf[:, 352:384].rearrange("p (b j one) -> p b j one", b=4, j=8),
            yf[:, 0:256].rearrange("p (b g j) -> p b j g", b=4, g=8),
            axis=AX.X, op=ALU.add),
            r=[T("stage", 0, 256)], w=[T("stage", 352, 384)])
        S.op("vector", lambda E: E.reduce_max(c3(yf[:, 384:388]), r3(yf[:, 352:384]), axis=AX.X),
             r=[T("stage", 352, 384)], w=[T("stage", 384, 388)])
        S.op("vector", lambda E: E.tensor_tensor(
            r3(yf[:, 288:320]), r3(yf[:, 352:384]),
            c3(yf[:, 384:388]).broadcast_to([128, 4, 8]), ALU.is_ge),
            r=[T("stage", 352, 388)], w=[T("stage", 288, 320)])
        S.op("vector", lambda E: E.tensor_tensor(r3(yf[:, 320:352]), r3(yf[:, 288:320]),
                                                 r3(yf[:, 420:452]), ALU.mult),
             r=[T("stage", 288, 320), T("stage", 420, 452)], w=[T("stage", 320, 352)])
        S.op("vector", lambda E: E.tensor_scalar(yf[:, 320:352], yf[:, 320:352], 1000.0, None, ALU.add),
             r=[T("stage", 320, 352)], w=[T("stage", 320, 352)])
        S.op("vector", lambda E: E.tensor_reduce(c3(yf[:, 384:388]), r3(yf[:, 320:352]),
                                                 axis=AX.X, op=ALU.min),
             r=[T("stage", 320, 352)], w=[T("stage", 384, 388)])
        S.op("vector", lambda E: E.scalar_tensor_tensor(tidx[0:128, 0:4], cidx[0:128, 0:4], 8.0,
                                                        yf[:, 384:388], ALU.mult, ALU.add),
             r=[T("cidx", 0, 4), T("stage", 384, 388)], w=[T("tidx", 0, 4)])
        for b in range(4):
            S.op("tensor", lambda E, b=b: E.transpose(PTf[0:1, 0:128], tidx[0:128, b:b + 1],
                                                      B["idf_s"][0:128, 0:128]),
                 r=[T("tidx", 0, 4), T("idf_s", 0, 128)], w=[T("PTf", 0, 128)])
            S.op("vector", lambda E, b=b: E.tensor_copy(trowb[0:1, b * 128:(b + 1) * 128],
                                                        PTf[0:1, 0:128]),
                 r=[T("PTf", 0, 128)], w=[T("trowb", b * 128, (b + 1) * 128)])
    else:
     for b in range(nblk):
        bw = min(128, w - b * 128)
        S.op("tensor", lambda E, b=b, bw=bw: E.transpose(PTf[0:bw, 0:8],
                                                         gates[64:72, b * 128:b * 128 + bw],
                                                         B["idf_s"][64:72, 64:72]),
             r=[T("gates", 0, w), T("idf_s", 0, 128)], w=[T("PTf", 0, 8)])
        S.op("vector", lambda E, b=b, bw=bw: E.tensor_copy(gcT[0:bw, b * 8:(b + 1) * 8], PTf[0:bw, 0:8]),
             r=[T("PTf", 0, 8)], w=[T("gcT", b * 8, (b + 1) * 8)])
        S.op("tensor", lambda E, b=b, bw=bw: E.transpose(PTf[0:bw, 0:64],
                                                         gates[0:64, b * 128:b * 128 + bw],
                                                         B["idf_s"][0:64, 0:64]),
             r=[T("gates", 0, w), T("idf_s", 0, 128)], w=[T("PTf", 0, 64)])
        S.op("vector", lambda E, b=b, bw=bw: E.tensor_copy(tgT[0:bw, b * 64:(b + 1) * 64],
                                                           PTf[0:bw, 0:64]),
             r=[T("PTf", 0, 64)], w=[T("tgT", b * 64, (b + 1) * 64)])
        S.op("vector", lambda E, b=b, bw=bw: E.reduce_max(rmax[0:bw, b:b + 1],
                                                          gcT[0:bw, b * 8:(b + 1) * 8], axis=AX.X),
             r=[T("gcT", b * 8, (b + 1) * 8)], w=[T("rmax", b, b + 1)])
        S.op("vector", lambda E, b=b, bw=bw: E.tensor_scalar(eq8[0:bw, :],
                                                             gcT[0:bw, b * 8:(b + 1) * 8],
                                                             rmax[0:bw, b:b + 1], None, ALU.is_ge),
             r=[T("gcT", b * 8, (b + 1) * 8), T("rmax", b, b + 1)], w=[eq8.full])
        S.op("vector", lambda E, bw=bw: E.tensor_tensor(midx[0:bw, :], eq8[0:bw, :],
                                                        B["iota8n"][0:bw, :], ALU.mult),
             r=[eq8.full, B["iota8n"].full], w=[midx.full])
        S.op("vector", lambda E, bw=bw: E.tensor_scalar(midx[0:bw, :], midx[0:bw, :], 1000.0, None, ALU.add),
             r=[midx.full], w=[midx.full])
        S.op("vector", lambda E, b=b, bw=bw: E.tensor_reduce(cidx[0:bw, b:b + 1], midx[0:bw, :],
                                                             axis=AX.X, op=ALU.min),
             r=[midx.full], w=[T("cidx", b, b + 1)])
        S.op("vector", lambda E, b=b, bw=bw: E.tensor_scalar(ohc[0:bw, :], B["iota8"][0:bw, :],
                                                             cidx[0:bw, b:b + 1], None, ALU.is_equal),
             r=[B["iota8"].full, T("cidx", b, b + 1)], w=[ohc.full])
        for gi in range(8):
            if gi == 0:
                S.op("vector", lambda E, b=b, bw=bw, gi=gi: E.tensor_scalar(
                    selg[0:bw, :], tgT[0:bw, b * 64 + gi * 8:b * 64 + (gi + 1) * 8],
                    ohc[0:bw, gi:gi + 1], None, ALU.mult),
                    r=[T("tgT", b * 64, (b + 1) * 64), ohc.full], w=[selg.full])
            else:
                S.op("vector", lambda E, b=b, bw=bw, gi=gi: E.scalar_tensor_tensor(
                    selg[0:bw, :], tgT[0:bw, b * 64 + gi * 8:b * 64 + (gi + 1) * 8],
                    ohc[0:bw, gi:gi + 1], selg[0:bw, :], ALU.mult, ALU.add),
                    r=[T("tgT", b * 64, (b + 1) * 64), ohc.full, selg.full], w=[selg.full])
        S.op("vector", lambda E, bw=bw: E.reduce_max(lidx[0:bw, 0:1], selg[0:bw, :], axis=AX.X),
             r=[selg.full], w=[lidx.full])
        S.op("vector", lambda E, bw=bw: E.tensor_scalar(eq8[0:bw, :], selg[0:bw, :],
                                                        lidx[0:bw, 0:1], None, ALU.is_ge),
             r=[selg.full, lidx.full], w=[eq8.full])
        S.op("vector", lambda E, bw=bw: E.tensor_tensor(midx[0:bw, :], eq8[0:bw, :],
                                                        B["iota8n"][0:bw, :], ALU.mult),
             r=[eq8.full, B["iota8n"].full], w=[midx.full])
        S.op("vector", lambda E, bw=bw: E.tensor_scalar(midx[0:bw, :], midx[0:bw, :], 1000.0, None, ALU.add),
             r=[midx.full], w=[midx.full])
        S.op("vector", lambda E, bw=bw: E.tensor_reduce(lidx[0:bw, 0:1], midx[0:bw, :],
                                                        axis=AX.X, op=ALU.min),
             r=[midx.full], w=[lidx.full])
        S.op("vector", lambda E, b=b, bw=bw: E.scalar_tensor_tensor(tidx[0:bw, b:b + 1],
                                                                    cidx[0:bw, b:b + 1], 8.0,
                                                                    lidx[0:bw, 0:1], ALU.mult, ALU.add),
             r=[T("cidx", b, b + 1), lidx.full], w=[T("tidx", b, b + 1)])
        S.op("tensor", lambda E, b=b, bw=bw: E.transpose(PTf[0:1, 0:bw], tidx[0:bw, b:b + 1],
                                                         B["idf_s"][0:bw, 0:bw]),
             r=[T("tidx", b, b + 1), T("idf_s", 0, 128)], w=[T("PTf", 0, 128)])
        S.op("vector", lambda E, b=b, bw=bw: E.tensor_copy(trowb[0:1, b * 128:b * 128 + bw],
                                                           PTf[0:1, 0:bw]),
             r=[T("PTf", 0, 128)], w=[T("trowb", b * 128, b * 128 + bw)])
    dbg_flat("tidx", trowb[0:1, 0:w], ("trowb", 0, w), 1)

    ohT = B["ohT"]
    S.op("tensor", lambda E: E.matmul(P3[0:64, 0:w], B["ones1b"][0:1, 0:64], trowb[0:1, 0:w],
                                      start=True, stop=True),
         r=[T("trowb", 0, w), B["ones1b"].full], w=[T("P3", 0, w)])
    S.op("vector", lambda E: E.tensor_scalar(ohT[0:64, 0:w], P3[0:64, 0:w], B["iota64"][0:64, 0:1],
                                             None, ALU.is_equal),
         r=[T("P3", 0, w), B["iota64"].full], w=[T("ohT", 0, w)])

    w1s, b1s, w2s, b2s = B["w1s"], B["b1s"], B["w2s"], B["b2s"]
    for (lhs_ap, lhs_tag, dstb, dlo, rows) in [
        (B["sw1_b"][0:64, 0:16], B["sw1_b"].full, w1s, 0, 16),
        (B["sw1_b"][0:64, 16:32], B["sw1_b"].full, w1s, SUBW, 16),
        (B["sb1_b"][0:64, 0:16], B["sb1_b"].full, b1s, 0, 16),
        (B["sw2_b"][0:64, 0:16], B["sw2_b"].full, w2s, 0, 16),
        (B["sb2_b"][0:64, 0:1], B["sb2_b"].full, b2s, 0, 1),
    ]:
        S.op("tensor", lambda E, lhs=lhs_ap, rows=rows: E.matmul(P3[0:rows, 0:w], lhs,
                                                                 ohT[0:64, 0:w], start=True, stop=True),
             r=[lhs_tag, T("ohT", 0, w)], w=[T("P3", 0, w)])
        if dstb in (w2s, b2s):
            S.op("vector", lambda E, dstb=dstb, dlo=dlo, rows=rows: E.tensor_scalar(
                dstb[0:rows, dlo:dlo + w], P3[0:rows, 0:w], B["osc_b"][0:rows, 0:1], None, ALU.mult),
                r=[T("P3", 0, w), B["osc_b"].full], w=[dstb.t(dlo, dlo + w)])
        else:
            S.op("scalar", lambda E, dstb=dstb, dlo=dlo, rows=rows: E.copy(dstb[0:rows, dlo:dlo + w],
                                                                          P3[0:rows, 0:w]),
                 r=[T("P3", 0, w)], w=[dstb.t(dlo, dlo + w)])

    h1x, h1e = B["h1x"], B["h1e"]
    h1b = h1x
    for hc in range(2):
        for k in range(8):
            S.op("tensor", lambda E, hc=hc, k=k: E.matmul(
                P3[:, 0:w], B["cw1_b"][:, k * 256 + hc * 128:k * 256 + (hc + 1) * 128],
                xn[:, k * SUBW:k * SUBW + w], start=(k == 0), stop=(k == 7)),
                r=[T("cw1_b", k * 256 + hc * 128, k * 256 + (hc + 1) * 128),
                   T("xn", k * SUBW, k * SUBW + w)], w=[T("P3", 0, w)])
        S.op("vector", lambda E, hc=hc: E.tensor_scalar(h1x[:, hc * SUBW:hc * SUBW + w], P3[:, 0:w],
                                                        1.0, B["cb1_sb"][:, hc:hc + 1],
                                                        ALU.mult, ALU.add),
             r=[T("P3", 0, w), B["cb1_sb"].full], w=[T("h1x", hc * SUBW, hc * SUBW + w)])
        S.op("scalar", lambda E, hc=hc: E.activation(h1e[:, 0:w], h1x[:, hc * SUBW:hc * SUBW + w],
                                                     AF.Erf, scale=B["rsq2"][:, 0:1]),
             r=[T("h1x", hc * SUBW, hc * SUBW + w), B["rsq2"].full], w=[h1e.full])
        S.op("vector", lambda E: E.tensor_scalar(h1e[:, 0:w], h1e[:, 0:w], 1.0, 0.5, ALU.add, ALU.mult),
             r=[h1e.full], w=[h1e.full])
        S.op("vector", lambda E, hc=hc: E.tensor_tensor(h1b[:, hc * SUBW:hc * SUBW + w],
                                                        h1x[:, hc * SUBW:hc * SUBW + w],
                                                        h1e[:, 0:w], ALU.mult),
             r=[T("h1x", hc * SUBW, hc * SUBW + w), h1e.full], w=[T("h1x", hc * SUBW, hc * SUBW + w)])
    for hc in range(2):
        S.op("tensor", lambda E, hc=hc: E.matmul(P3[0:2, 0:w], B["cw2_b"][:, hc * 2:(hc + 1) * 2],
                                                 h1b[:, hc * SUBW:hc * SUBW + w],
                                                 start=(hc == 0), stop=(hc == 1)),
             r=[T("cw2_b", hc * 2, (hc + 1) * 2), T("h1x", hc * SUBW, hc * SUBW + w)],
             w=[T("P3", 0, w)])
    comp2b = B["comp2b"]
    S.op("scalar", lambda E: E.activation(comp2b[0:2, 0:w], P3[0:2, 0:w], AF.Tanh,
                                          bias=B["cb2_s"][0:2, 0:1]),
         r=[T("P3", 0, w), B["cb2_s"].full], w=[T("comp2b", 0, w)])
    S.op("tensor", lambda E: E.matmul(P0[0:16, 0:w], B["sel01_s"][0:2, 0:16], comp2b[0:2, 0:w],
                                      start=True, stop=True),
         r=[B["sel01_s"].full, T("comp2b", 0, w)], w=[T("P0", 0, w)])
    S.op("tensor", lambda E: E.matmul(P1[0:16, 0:w], B["sel01_s"][0:2, 32:48], comp2b[0:2, 0:w],
                                      start=True, stop=True),
         r=[B["sel01_s"].full, T("comp2b", 0, w)], w=[T("P1", 0, w)])
    hh1, hh2, hhp = B["hh1"], B["hh2"], B["hhp"]
    S.op("vector", lambda E: E.tensor_tensor(hh1[0:16, 0:w], P0[0:16, 0:w], w1s[0:16, 0:w], ALU.mult),
         r=[T("P0", 0, w), T("w1s", 0, w)], w=[T("hh1", 0, w)])
    S.op("vector", lambda E: E.tensor_tensor(hh2[0:16, 0:w], P1[0:16, 0:w],
                                             w1s[0:16, SUBW:SUBW + w], ALU.mult),
         r=[T("P1", 0, w), T("w1s", SUBW, SUBW + w)], w=[T("hh2", 0, w)])
    S.op("vector", lambda E: E.tensor_tensor(hh1[0:16, 0:w], hh1[0:16, 0:w], hh2[0:16, 0:w], ALU.add),
         r=[T("hh1", 0, w), T("hh2", 0, w)], w=[T("hh1", 0, w)])
    S.op("vector", lambda E: E.tensor_tensor(hh1[0:16, 0:w], hh1[0:16, 0:w], b1s[0:16, 0:w], ALU.add),
         r=[T("hh1", 0, w), T("b1s", 0, w)], w=[T("hh1", 0, w)])
    S.op("vector", lambda E: E.tensor_scalar(hh1[0:16, 0:w], hh1[0:16, 0:w], 0.0, None, ALU.max),
         r=[T("hh1", 0, w)], w=[T("hh1", 0, w)])
    S.op("vector", lambda E: E.tensor_tensor(hhp[0:16, 0:w], hh1[0:16, 0:w], w2s[0:16, 0:w], ALU.mult),
         r=[T("hh1", 0, w), T("w2s", 0, w)], w=[T("hhp", 0, w)])
    S.op("tensor", lambda E: E.matmul(P3[0:1, 0:w], B["ones16b"][0:16, 0:1], hhp[0:16, 0:w],
                                      start=True, stop=True),
         r=[T("hhp", 0, w), B["ones16b"].full], w=[T("P3", 0, w)])
    magb = B["magb"]
    S.op("vector", lambda E: E.tensor_tensor(magb[0:1, 0:w], P3[0:1, 0:w], b2s[0:1, 0:w], ALU.add),
         r=[T("P3", 0, w), T("b2s", 0, w)], w=[T("magb", 0, w)])
    S.op("tensor", lambda E: E.matmul(P3[:, 0:w], B["ones1b"][0:1, :], magb[0:1, 0:w],
                                      start=True, stop=True),
         r=[T("magb", 0, w), B["ones1b"].full], w=[T("P3", 0, w)])

    dirsel, ybf = B["dirsel"], B["ybf"]
    for dc in range(8):
        sl = (dc % 2) * SUBW
        S.op("tensor", lambda E, dc=dc: E.matmul(P2[:, 0:w], B["dirs_b"][0:64, dc * 128:(dc + 1) * 128],
                                                 ohT[0:64, 0:w], start=True, stop=True),
             r=[B["dirs_b"].full, T("ohT", 0, w)], w=[T("P2", 0, w)])
        S.op("scalar", lambda E, sl=sl: E.copy(dirsel[:, sl:sl + w], P2[:, 0:w]),
             r=[T("P2", 0, w)], w=[T("dirsel", sl, sl + w)])
        S.op("vector", lambda E, dc=dc, sl=sl: E.tensor_tensor(yf[:, dc * SUBW:dc * SUBW + w],
                                                               dirsel[:, sl:sl + w],
                                                               P3[:, 0:w], ALU.mult),
             r=[T("dirsel", sl, sl + w), T("P3", 0, w)], w=[T("stage", dc * SUBW, dc * SUBW + w)])
        S.op("vector", lambda E, dc=dc: E.tensor_tensor(yf[:, dc * SUBW:dc * SUBW + w],
                                                        yf[:, dc * SUBW:dc * SUBW + w],
                                                        xn[:, dc * SUBW:dc * SUBW + w], ALU.add),
             r=[T("stage", dc * SUBW, dc * SUBW + w), T("xn", dc * SUBW, dc * SUBW + w)],
             w=[T("stage", dc * SUBW, dc * SUBW + w)])
        S.op("scalar", lambda E, dc=dc, sl=sl: E.copy(ybf[:, sl:sl + w],
                                                      yf[:, dc * SUBW:dc * SUBW + w]),
             r=[T("stage", dc * SUBW, dc * SUBW + w)], w=[T("ybf", sl, sl + w)])
        S.op("vector", lambda E, sl=sl: E.tensor_tensor(pt[:, sl:sl + w], ybf[:, sl:sl + w],
                                                        ybf[:, sl:sl + w], ALU.mult),
             r=[T("ybf", sl, sl + w)], w=[T("pt", sl, sl + w)])
        S.op("tensor", lambda E, dc=dc, sl=sl: E.matmul(P4[0:1, 0:w], B["onesb"][:, :],
                                                        ybf[:, sl:sl + w],
                                                        start=(dc == 0), stop=(dc == 7)),
             r=[T("ybf", sl, sl + w), B["onesb"].full], w=[T("P4", 0, w)])
        S.op("tensor", lambda E, dc=dc, sl=sl: E.matmul(P5[0:1, 0:w], B["onesb"][:, :],
                                                        pt[:, sl:sl + w],
                                                        start=(dc == 0), stop=(dc == 7)),
             r=[T("pt", sl, sl + w), B["onesb"].full], w=[T("P5", 0, w)])
    ln_rows()

    gam, bet, gsil, s_all = B["gam_s"], B["bet_s"], B["gsil"], B["s_all"]
    for dc in range(8):
        gl = (dc % 2) * SUBW
        S.op("vector", lambda E, dc=dc: E.tensor_tensor(yf[:, dc * SUBW:dc * SUBW + w],
                                                        yf[:, dc * SUBW:dc * SUBW + w],
                                                        P4[:, 0:w], ALU.subtract),
             r=[T("stage", dc * SUBW, dc * SUBW + w), T("P4", 0, w)],
             w=[T("stage", dc * SUBW, dc * SUBW + w)])
        S.op("vector", lambda E, dc=dc: E.tensor_tensor(yf[:, dc * SUBW:dc * SUBW + w],
                                                        yf[:, dc * SUBW:dc * SUBW + w],
                                                        P5[:, 0:w], ALU.mult),
             r=[T("stage", dc * SUBW, dc * SUBW + w), T("P5", 0, w)],
             w=[T("stage", dc * SUBW, dc * SUBW + w)])
        S.op("vector", lambda E, dc=dc: E.tensor_scalar(yf[:, dc * SUBW:dc * SUBW + w],
                                                        yf[:, dc * SUBW:dc * SUBW + w],
                                                        gam[:, dc:dc + 1], bet[:, dc:dc + 1],
                                                        ALU.mult, ALU.add),
             r=[T("stage", dc * SUBW, dc * SUBW + w), gam.full, bet.full],
             w=[T("stage", dc * SUBW, dc * SUBW + w)])
        S.op("scalar", lambda E, dc=dc, gl=gl: E.activation(gsil[:, gl:gl + w],
                                                            yf[:, dc * SUBW:dc * SUBW + w], AF.Silu),
             r=[T("stage", dc * SUBW, dc * SUBW + w)], w=[T("gsil", gl, gl + w)])
        S.op("vector", lambda E, dc=dc, gl=gl: E.tensor_tensor(gsil[:, gl:gl + w], gsil[:, gl:gl + w],
                                                               s_all[:, dc * SUBW:dc * SUBW + w],
                                                               ALU.mult),
             r=[T("gsil", gl, gl + w), T("s_all", dc * SUBW, dc * SUBW + w)], w=[T("gsil", gl, gl + w)])
        S.op("vector", lambda E, dc=dc, gl=gl: E.tensor_tensor(x_sub[:, dc * SUBW:dc * SUBW + w],
                                                               gsil[:, gl:gl + w],
                                                               x_sub[:, dc * SUBW:dc * SUBW + w],
                                                               ALU.add),
             r=[T("gsil", gl, gl + w), T("x_sub", dc * SUBW, dc * SUBW + w)],
             w=[T("x_sub", dc * SUBW, dc * SUBW + w)])
        S.op("gpsimd", lambda E, dc=dc: E.tensor_copy(xn[:, dc * SUBW:dc * SUBW + w],
                                                      x_sub[:, dc * SUBW:dc * SUBW + w]),
             r=[T("x_sub", dc * SUBW, dc * SUBW + w)], w=[T("xn", dc * SUBW, dc * SUBW + w)])
    dbg_chunked("gy", yf, 8)
    for hf in range(4):
        dsty = x1_hbm.rearrange("(c p) t -> p c t", p=128)[:, hf * 2:(hf + 1) * 2, ls:ls + w]
        srcy = xn.h.ap().rearrange("p (c t) -> p c t", c=8)[:, hf * 2:(hf + 1) * 2, 0:w]
        S.op("scalar", lambda E, d=dsty, s=srcy: E.dma_start(d, s),
             r=[("xn", hf * 2 * SUBW, (hf + 1) * 2 * SUBW)], dma=True, tag="x1_dma")


SC2T = 1.0 / 0.56
NEG = -1.0e30


L1A_HI_SPECS = [
        ("wq", [1024, 1024], F32), ("wk", [1024, 1024], F32),
        ("wv", [1024, 1024], F32), ("wo", [1024, 1024], F32),
        ("idb", [128, 128], BF16), ("mask0_h", [128, 136], F32),
        ("maskr_h", [128, 136], F32),
]


def emit_l1a(S, nc, B, x1_hbm, s2_hbm, HI):

    def T(name, lo, hi):
        return (name, lo, hi)

    def dma(dst_ap, src_ap, r=(), w=(), tag=None):
        return S.op("sync", lambda E, d=dst_ap, s=src_ap: E.dma_start(d, s),
                    r=r, w=w, dma=True, tag=tag)

    def gdma(dst_ap, src_ap, r=(), w=(), tag=None):
        return S.op("gpsimd", lambda E, d=dst_ap, s=src_ap: E.dma_start(d, s),
                    r=r, w=w, dma=True, tag=tag)

    for nm_src, nm_dst in [("wq", "wq_b"), ("wk", "wk_b"), ("wv", "wv_b"), ("wo", "wo_b")]:
        for c in range(8):
            gdma(B[nm_dst][:, c * 1024:(c + 1) * 1024], HI[nm_src][c * 128:(c + 1) * 128, :],
                 w=[T(nm_dst, c * 1024, (c + 1) * 1024)])
    if "wd_b" in B and "wdt" in HI:
        for c in range(24):
            gdma(B["wd_b"][:, c * 1024:(c + 1) * 1024], HI["wdt"][c * 128:(c + 1) * 128, :],
                 w=[T("wd_b", c * 1024, (c + 1) * 1024)])
    for dst, srcn in [("idb_s", "idb"), ("mask0", "mask0_h"), ("maskr", "maskr_h")]:
        dma(B[dst][:, :], HI[srcn][:, :], w=[B[dst].full])
    for nm, val in [("onesb", 1.0), ("ones1b", 1.0), ("epsc", 1e-5), ("mhalf", -0.5),
                    ("sc2t", SC2T)]:
        S.op("vector", lambda E, nm=nm, val=val: E.memset(B[nm][:, :], val), w=[B[nm].full])

    P4, P5 = B["P4"], B["P5"]
    row_m, row_v, row_r = B["row_m"], B["row_v"], B["row_r"]
    row_rb, row_mb = B["row_rb"], B["row_mb"]
    x_sub, xbf, xpt, xn1 = B["x_sub"], B["xbf"], B["xpt"], B["xn1"]

    def ln_block(w, dst_fn):
        for c in range(8):
            sl = (c % 2) * SUBW
            S.op("scalar", lambda E, c=c, sl=sl, w=w: E.copy(xbf[:, sl:sl + w],
                                                            x_sub[:, c * SUBW:c * SUBW + w]),
                 r=[T("x_sub", c * SUBW, c * SUBW + w)], w=[T("xbf", sl, sl + w)])
            S.op("vector", lambda E, sl=sl, w=w: E.tensor_tensor(xpt[:, sl:sl + w], xbf[:, sl:sl + w],
                                                                 xbf[:, sl:sl + w], ALU.mult),
                 r=[T("xbf", sl, sl + w)], w=[T("xpt", sl, sl + w)])
            S.op("tensor", lambda E, c=c, sl=sl, w=w: E.matmul(P4[0:1, 0:w], B["onesb"][:, :],
                                                               xbf[:, sl:sl + w],
                                                               start=(c == 0), stop=(c == 7)),
                 r=[T("xbf", sl, sl + w), B["onesb"].full], w=[T("P4", 0, w)])
            S.op("tensor", lambda E, c=c, sl=sl, w=w: E.matmul(P5[0:1, 0:w], B["onesb"][:, :],
                                                               xpt[:, sl:sl + w],
                                                               start=(c == 0), stop=(c == 7)),
                 r=[T("xpt", sl, sl + w), B["onesb"].full], w=[T("P5", 0, w)])
        S.op("vector", lambda E, w=w: E.tensor_scalar(row_m[0:1, 0:w], P4[0:1, 0:w], 1.0 / DM,
                                                      None, ALU.mult),
             r=[T("P4", 0, w)], w=[T("row_m", 0, w)])
        S.op("vector", lambda E, w=w: E.tensor_scalar(row_v[0:1, 0:w], P5[0:1, 0:w], 1.0 / DM,
                                                      None, ALU.mult),
             r=[T("P5", 0, w)], w=[T("row_v", 0, w)])
        S.op("vector", lambda E, w=w: E.tensor_tensor(row_r[0:1, 0:w], row_m[0:1, 0:w],
                                                      row_m[0:1, 0:w], ALU.mult),
             r=[T("row_m", 0, w)], w=[T("row_r", 0, w)])
        S.op("vector", lambda E, w=w: E.tensor_tensor(row_v[0:1, 0:w], row_v[0:1, 0:w],
                                                      row_r[0:1, 0:w], ALU.subtract),
             r=[T("row_v", 0, w), T("row_r", 0, w)], w=[T("row_v", 0, w)])
        S.op("scalar", lambda E, w=w: E.activation(row_r[0:1, 0:w], row_v[0:1, 0:w], AF.Ln,
                                                   bias=B["epsc"][0:1, 0:1]),
             r=[T("row_v", 0, w), B["epsc"].full], w=[T("row_r", 0, w)])
        S.op("scalar", lambda E, w=w: E.activation(row_r[0:1, 0:w], row_r[0:1, 0:w], AF.Exp,
                                                   scale=B["mhalf"][0:1, 0:1]),
             r=[T("row_r", 0, w), B["mhalf"].full], w=[T("row_r", 0, w)])
        S.op("scalar", lambda E, w=w: E.copy(row_rb[0:1, 0:w], row_r[0:1, 0:w]),
             r=[T("row_r", 0, w)], w=[T("row_rb", 0, w)])
        S.op("scalar", lambda E, w=w: E.copy(row_mb[0:1, 0:w], row_m[0:1, 0:w]),
             r=[T("row_m", 0, w)], w=[T("row_mb", 0, w)])
        S.op("tensor", lambda E, w=w: E.matmul(P4[:, 0:w], B["ones1b"][0:1, :], row_mb[0:1, 0:w],
                                               start=True, stop=True),
             r=[T("row_mb", 0, w), B["ones1b"].full], w=[T("P4", 0, w)])
        S.op("tensor", lambda E, w=w: E.matmul(P5[:, 0:w], B["ones1b"][0:1, :], row_rb[0:1, 0:w],
                                               start=True, stop=True),
             r=[T("row_rb", 0, w), B["ones1b"].full], w=[T("P5", 0, w)])
        for c in range(8):
            ap, wtag = dst_fn(c)
            S.op("vector", lambda E, c=c, w=w: E.tensor_tensor(
                B["stage"][:, 0:w] if False else x_sub[:, c * SUBW:c * SUBW + w],
                x_sub[:, c * SUBW:c * SUBW + w], P4[:, 0:w], ALU.subtract),
                r=[T("x_sub", c * SUBW, c * SUBW + w), T("P4", 0, w)],
                w=[T("x_sub", c * SUBW, c * SUBW + w)])
            S.op("vector", lambda E, c=c, ap=ap, wtag=wtag, w=w: E.tensor_tensor(
                ap, x_sub[:, c * SUBW:c * SUBW + w], P5[:, 0:w], ALU.mult),
                r=[T("x_sub", c * SUBW, c * SUBW + w), T("P5", 0, w)], w=[wtag])

    src = x1_hbm.rearrange("(c p) t -> p c t", p=128)[:, :, 0:8]
    dst = x_sub.h.ap().rearrange("p (c t) -> p c t", c=8)[:, :, 0:8]
    S.op("sync", lambda E, d=dst, s=src: E.dma_start(d, s), w=[x_sub.full], dma=True)
    ln_block(8, lambda c: (xn1[:, c * 520:c * 520 + 8], T("xn1", c * 520, c * 520 + 8)))

    B["_ln_block"] = ln_block
    for si in range(4):
        emit_l1a_sub(S, nc, B, x1_hbm, s2_hbm, si)




def emit_l1a_sub(S, nc, B, x1_hbm, s2_hbm, si):
    w = SUBW

    def T(name, lo, hi):
        return (name, lo, hi)

    P0, P1, P2, P3, P4, P5 = (B[k] for k in ["P0", "P1", "P2", "P3", "P4", "P5"])
    PTb = B["PTb"]
    x_sub, xn1 = B["x_sub"], B["xn1"]
    qf, kf = B["qf"], B["kf"]
    vts = [B["vt0"], B["vt1"], B["vt2"], B["vt3"]]
    vnx, att, atf, s2 = B["vnx"], B["att"], B["atf"], B["s2"]

    if si > 0:
        for c in range(8):
            S.op("vector", lambda E, c=c: E.tensor_copy(xn1[:, c * 520:c * 520 + 8],
                                                        xn1[:, c * 520 + 512:c * 520 + 520]),
                 r=[T("xn1", c * 520 + 512, c * 520 + 520)], w=[T("xn1", c * 520, c * 520 + 8)])

    ls = 512 * si + 8
    for hf in range(2):
        srch = x1_hbm.rearrange("(c p) t -> p c t", p=128)[:, hf * 4:(hf + 1) * 4, ls:ls + w]
        dsth = x_sub.h.ap().rearrange("p (c t) -> p c t", c=8)[:, hf * 4:(hf + 1) * 4, 0:w]
        S.op("sync", lambda E, d=dsth, s=srch: E.dma_start(d, s),
             w=[("x_sub", hf * 4 * SUBW, (hf + 1) * 4 * SUBW)], dma=True)
    ln_block = B["_ln_block"]
    ln_block(w, lambda c: (xn1[:, c * 520 + 8:c * 520 + 520], T("xn1", c * 520 + 8, c * 520 + 520)))

    for jc in range(8):
        pz = ("P0", "P1")[jc % 2]
        PZ = B[pz]
        for k in range(8):
            S.op("tensor", lambda E, jc=jc, k=k, PZ=PZ: E.matmul(
                PZ[:, 0:w], B["wq_b"][:, k * 1024 + jc * 128:k * 1024 + (jc + 1) * 128],
                xn1[:, k * 520 + 8:k * 520 + 520], start=(k == 0), stop=(k == 7)),
                r=[T("wq_b", k * 1024 + jc * 128, k * 1024 + (jc + 1) * 128),
                   T("xn1", k * 520 + 8, k * 520 + 520)], w=[T(pz, 0, w)])
        S.op("scalar", lambda E, jc=jc, pz=pz, PZ=PZ: E.copy(qf[:, jc * SUBW:(jc + 1) * SUBW],
                                                             PZ[:, 0:w]),
             r=[T(pz, 0, w)], w=[T("qf", jc * SUBW, (jc + 1) * SUBW)])
    for jc in range(8):
        pz = ("P0", "P1")[jc % 2]
        PZ = B[pz]
        for k in range(8):
            S.op("tensor", lambda E, jc=jc, k=k, PZ=PZ: E.matmul(
                PZ[:, 0:8], B["wk_b"][:, k * 1024 + jc * 128:k * 1024 + (jc + 1) * 128],
                xn1[:, k * 520:k * 520 + 8], start=(k == 0), stop=(k == 7)),
                r=[T("wk_b", k * 1024 + jc * 128, k * 1024 + (jc + 1) * 128),
                   T("xn1", k * 520, k * 520 + 8)], w=[T(pz, 0, 8)])
        S.op("scalar", lambda E, jc=jc, pz=pz, PZ=PZ: E.copy(kf[:, jc * 520:jc * 520 + 8],
                                                             PZ[0:128, 0:8]),
             r=[T(pz, 0, 8)], w=[T("kf", jc * 520, jc * 520 + 8)])
        pz2 = ("P2", "P3")[jc % 2]
        PZ2 = B[pz2]
        for k in range(8):
            S.op("tensor", lambda E, jc=jc, k=k, PZ2=PZ2: E.matmul(
                PZ2[:, 0:w], B["wk_b"][:, k * 1024 + jc * 128:k * 1024 + (jc + 1) * 128],
                xn1[:, k * 520 + 8:k * 520 + 520], start=(k == 0), stop=(k == 7)),
                r=[T("wk_b", k * 1024 + jc * 128, k * 1024 + (jc + 1) * 128),
                   T("xn1", k * 520 + 8, k * 520 + 520)], w=[T(pz2, 0, w)])
        S.op("scalar", lambda E, jc=jc, pz2=pz2, PZ2=PZ2: E.copy(kf[:, jc * 520 + 8:(jc + 1) * 520],
                                                                 PZ2[:, 0:w]),
             r=[T(pz2, 0, w)], w=[T("kf", jc * 520 + 8, (jc + 1) * 520)])

    for m in range(4):
        for jh in range(2):
            pz = ("P0", "P1")[jh]
            PZ = B[pz]
            for k in range(8):
                S.op("tensor", lambda E, m=m, jh=jh, k=k, PZ=PZ: E.matmul(
                    PZ[:, 0:512], xn1[:, k * 520 + m * 128:k * 520 + (m + 1) * 128],
                    B["wv_b"][:, k * 1024 + jh * 512:k * 1024 + (jh + 1) * 512],
                    start=(k == 0), stop=(k == 7)),
                    r=[T("xn1", k * 520 + m * 128, k * 520 + (m + 1) * 128),
                       T("wv_b", k * 1024 + jh * 512, k * 1024 + (jh + 1) * 512)],
                    w=[T(pz, 0, 512)])
            S.op("scalar", lambda E, m=m, jh=jh, pz=pz, PZ=PZ: E.copy(
                vts[m][:, jh * 512:(jh + 1) * 512], PZ[:, 0:512]),
                r=[T(pz, 0, 512)], w=[(vts[m].name, jh * 512, (jh + 1) * 512)])
    for jh in range(2):
        pz = ("P0", "P1")[jh]
        PZ = B[pz]
        for k in range(8):
            S.op("tensor", lambda E, jh=jh, k=k, PZ=PZ: E.matmul(
                PZ[0:8, 0:512], xn1[:, k * 520 + 512:k * 520 + 520],
                B["wv_b"][:, k * 1024 + jh * 512:k * 1024 + (jh + 1) * 512],
                start=(k == 0), stop=(k == 7)),
                r=[T("xn1", k * 520 + 512, k * 520 + 520),
                   T("wv_b", k * 1024 + jh * 512, k * 1024 + (jh + 1) * 512)],
                w=[T(pz, 0, 512)])
        S.op("scalar", lambda E, jh=jh, pz=pz, PZ=PZ: E.copy(vnx[0:8, jh * 512:(jh + 1) * 512],
                                                             PZ[0:8, 0:512]),
             r=[T(pz, 0, 512)], w=[T("vnx", jh * 512, (jh + 1) * 512)])

    sm, rmx, nmx, ebf = B["sm"], B["rmx"], B["nmx"], B["ebf"]
    esum, erec, eta, etb = B["esum"], B["erec"], B["eta"], B["etb"]
    for tb in range(4):
        for h in range(16):
            jc = h // 2
            rb = (h % 2) * 64
            slot = h % 4
            so = slot * 136
            eo = slot * 128
            po = (slot // 2) * 272
            tp = slot * 128
            mk = B["mask0"] if (si == 0 and tb == 0) else B["maskr"]
            mk_tag = B["mask0"].full if (si == 0 and tb == 0) else B["maskr"].full
            pz = ("P2", "P3")[h % 2]
            PZ = B[pz]
            S.op("tensor", lambda E, tb=tb, jc=jc, rb=rb, PZ=PZ, po=po: E.matmul(
                PZ[0:128, po:po + 136], qf[rb:rb + 64, jc * SUBW + tb * 128:jc * SUBW + (tb + 1) * 128],
                kf[rb:rb + 64, jc * 520 + tb * 128:jc * 520 + tb * 128 + 136],
                start=True, stop=True),
                r=[T("qf", jc * SUBW, (jc + 1) * SUBW), T("kf", jc * 520, (jc + 1) * 520)],
                w=[T(pz, po, po + 136)])
            S.op("vector", lambda E, PZ=PZ, mk=mk, so=so, po=po: E.tensor_tensor(
                sm[:, so:so + 136], PZ[0:128, po:po + 136], mk[:, :], ALU.add),
                r=[T(pz, po, po + 136), mk_tag], w=[T("sm", so, so + 136)])
            S.op("vector", lambda E, so=so, slot=slot: E.reduce_max(rmx[:, slot:slot + 1],
                                                                    sm[:, so:so + 136], axis=AX.X),
                 r=[T("sm", so, so + 136)], w=[T("rmx", slot, slot + 1)])
            S.op("vector", lambda E, slot=slot: E.tensor_scalar(nmx[:, slot:slot + 1],
                                                                rmx[:, slot:slot + 1], -SC2T,
                                                                None, ALU.mult),
                 r=[T("rmx", slot, slot + 1)], w=[T("nmx", slot, slot + 1)])
            S.op("scalar", lambda E, so=so, slot=slot: E.activation(
                ebf[:, so:so + 136], sm[:, so:so + 136], AF.Exp,
                scale=B["sc2t"][:, 0:1], bias=nmx[:, slot:slot + 1]),
                r=[T("sm", so, so + 136), T("nmx", slot, slot + 1), B["sc2t"].full],
                w=[T("ebf", so, so + 136)])
            S.op("vector", lambda E, so=so, slot=slot: E.reduce_sum(esum[:, slot:slot + 1],
                                                                    ebf[:, so:so + 136], axis=AX.X),
                 r=[T("ebf", so, so + 136)], w=[T("esum", slot, slot + 1)])
            S.op("vector", lambda E, slot=slot: E.reciprocal(erec[:, slot:slot + 1],
                                                             esum[:, slot:slot + 1]),
                 r=[T("esum", slot, slot + 1)], w=[T("erec", slot, slot + 1)])
            te = slot * 256
            tb2 = slot * 256 + 128
            S.op("tensor", lambda E, so=so, te=te: E.transpose(PTb[0:8, te:te + 128],
                                                               ebf[:, so + 128:so + 136],
                                                               B["idb_s"][:, :]),
                 r=[T("ebf", so, so + 136), T("idb_s", 0, 128)], w=[T("PTb", te, te + 128)])
            S.op("vector", lambda E, eo=eo, te=te: E.tensor_copy(eta[0:8, eo:eo + 128],
                                                                 PTb[0:8, te:te + 128]),
                 r=[T("PTb", te, te + 128)], w=[T("eta", eo, eo + 128)])
            S.op("tensor", lambda E, so=so, tb2=tb2: E.transpose(PTb[:, tb2:tb2 + 128], ebf[:, so:so + 128],
                                                                 B["idb_s"][:, :]),
                 r=[T("ebf", so, so + 136), T("idb_s", 0, 128)], w=[T("PTb", tb2, tb2 + 128)])
            S.op("vector", lambda E, eo=eo, tb2=tb2: E.tensor_copy(etb[:, eo:eo + 128],
                                                                   PTb[:, tb2:tb2 + 128]),
                 r=[T("PTb", tb2, tb2 + 128)], w=[T("etb", eo, eo + 128)])
            vhalo = vts[tb]
            vnext_ap = (vts[tb + 1][0:8, h * 64:(h + 1) * 64] if tb < 3
                        else vnx[0:8, h * 64:(h + 1) * 64])
            vnext_tag = ((vts[tb + 1].name, h * 64, (h + 1) * 64) if tb < 3
                         else T("vnx", h * 64, (h + 1) * 64))
            pz2 = ("P4", "P5")[h % 2]
            PZ2 = B[pz2]
            ao = (slot // 2) * 64
            S.op("tensor", lambda E, eo=eo, tb=tb, h=h, PZ2=PZ2, ao=ao: E.matmul(
                PZ2[0:128, ao:ao + 64], etb[:, eo:eo + 128], vts[tb][:, h * 64:(h + 1) * 64],
                start=True, stop=False),
                r=[T("etb", eo, eo + 128), (vts[tb].name, h * 64, (h + 1) * 64)],
                w=[T(pz2, ao, ao + 64)])
            S.op("tensor", lambda E, eo=eo, va=vnext_ap, vt=vnext_tag, PZ2=PZ2, ao=ao: E.matmul(
                PZ2[0:128, ao:ao + 64], eta[0:8, eo:eo + 128], va, start=False, stop=True),
                r=[T("eta", eo, eo + 128), vnext_tag], w=[T(pz2, ao, ao + 64)])
            S.op("scalar", lambda E, tb=tb, h=h, slot=slot, PZ2=PZ2, pz2=pz2, ao=ao: E.activation(
                att[:, tb * 1024 + h * 64:tb * 1024 + (h + 1) * 64], PZ2[0:128, ao:ao + 64],
                AF.Copy, scale=erec[:, slot:slot + 1]),
                r=[T(pz2, ao, ao + 64), T("erec", slot, slot + 1)],
                w=[T("att", tb * 1024 + h * 64, tb * 1024 + (h + 1) * 64)])

    for tb in range(4):
        for jc in range(8):
            S.op("tensor", lambda E, tb=tb, jc=jc: E.transpose(
                PTb[:, 0:128], att[:, tb * 1024 + jc * 128:tb * 1024 + (jc + 1) * 128],
                B["idb_s"][:, :]),
                r=[T("att", tb * 1024 + jc * 128, tb * 1024 + (jc + 1) * 128),
                   T("idb_s", 0, 128)], w=[T("PTb", 0, 128)])
            S.op("vector", lambda E, tb=tb, jc=jc: E.tensor_copy(
                atf[:, jc * SUBW + tb * 128:jc * SUBW + (tb + 1) * 128], PTb[:, 0:128]),
                r=[T("PTb", 0, 128)], w=[T("atf", jc * SUBW + tb * 128, jc * SUBW + (tb + 1) * 128)])
    for dc in range(8):
        pz = ("P0", "P1")[dc % 2]
        PZ = B[pz]
        for jc in range(8):
            S.op("tensor", lambda E, dc=dc, jc=jc, PZ=PZ: E.matmul(
                PZ[:, 0:w], B["wo_b"][:, jc * 1024 + dc * 128:jc * 1024 + (dc + 1) * 128],
                atf[:, jc * SUBW:(jc + 1) * SUBW], start=(jc == 0), stop=(jc == 7)),
                r=[T("wo_b", jc * 1024 + dc * 128, jc * 1024 + (dc + 1) * 128),
                   T("atf", jc * SUBW, (jc + 1) * SUBW)], w=[T(pz, 0, w)])
        S.op("scalar", lambda E, dc=dc, pz=pz, PZ=PZ: E.copy(s2[:, dc * SUBW:(dc + 1) * SUBW],
                                                             PZ[:, 0:w]),
             r=[T(pz, 0, w)], w=[T("s2", dc * SUBW, (dc + 1) * SUBW)])
    for hf in range(2):
        dsty = s2_hbm.rearrange("(c p) t -> p c t", p=128)[:, hf * 4:(hf + 1) * 4,
                                                           si * SUBW:(si + 1) * SUBW]
        srcy = s2.h.ap().rearrange("p (c t) -> p c t", c=8)[:, hf * 4:(hf + 1) * 4, 0:w]
        S.op("scalar", lambda E, d=dsty, s=srcy: E.dma_start(d, s),
             r=[("s2", hf * 4 * SUBW, (hf + 1) * 4 * SUBW)], dma=True, tag="s2_dma")


K127 = 1.0 / 127.0
K3 = 1.0 / (127.0 ** 3)


def emit_l1b(S, nc, B, x1_hbm, s2_hbm, out_hbm, HI, nsubs=4, use_s2s=True, W=SUBW):

    def T(name, lo, hi):
        return (name, lo, hi)

    def dma(dst_ap, src_ap, r=(), w=(), tag=None):
        return S.op("sync", lambda E, d=dst_ap, s=src_ap: E.dma_start(d, s),
                    r=r, w=w, dma=True, tag=tag)

    def gdma(dst_ap, src_ap, r=(), w=(), tag=None):
        return S.op("gpsimd", lambda E, d=dst_ap, s=src_ap: E.dma_start(d, s),
                    r=r, w=w, dma=True, tag=tag)

    for nm_src, nm_dst in [("wgt", "wg_b"), ("wut", "wu_b")]:
        for c in range(8):
            for hf in range(2):
                gdma(B[nm_dst][:, c * 3072 + hf * 1536:c * 3072 + (hf + 1) * 1536],
                     HI[nm_src][c * 128:(c + 1) * 128, hf * 1536:(hf + 1) * 1536],
                     w=[T(nm_dst, c * 3072 + hf * 1536, c * 3072 + (hf + 1) * 1536)])
    for nm, val in [("onesb", 1.0), ("ones1b", 1.0), ("epsc", 1e-5), ("mhalf", -0.5),
                    ("c127", K127), ("k3c", K3)]:
        S.op("vector", lambda E, nm=nm, val=val: E.memset(B[nm][:, :], val), w=[B[nm].full])

    for si in range(nsubs):
        emit_l1b_sub(S, nc, B, x1_hbm, s2_hbm, out_hbm, si, use_s2s, W)




def emit_l1b_sub(S, nc, B, x1_hbm, s2_hbm, out_hbm, si, use_s2s=True, W=SUBW):
    w = W
    ls = si * W

    def T(name, lo, hi):
        return (name, lo, hi)

    P0, P1, P2, P3, P4, P5 = (B[k] for k in ["P0", "P1", "P2", "P3", "P4", "P5"])
    x_sub, xbf, xn1 = B["x_sub"], B["xbf"], B["xn1"]
    xpt = B["sg"]
    row_v = B["row_v"]
    row_rb, row_mb = B["row_rb"], B["row_mb"]
    s2s, hb, sg, outf = B["s2s"], B["hb"], B["sg"], B["outf"]
    tt1 = B["xbf"]

    for hf in range(2):
        srch = x1_hbm.rearrange("(c p) t -> p c t", p=128)[:, hf * 4:(hf + 1) * 4, HALO + ls:HALO + ls + w]
        dsth = x_sub.h.ap().rearrange("p (c t) -> p c t", c=8)[:, hf * 4:(hf + 1) * 4, 0:w]
        S.op("sync", lambda E, d=dsth, s=srch: E.dma_start(d, s),
             w=[("x_sub", hf * 4 * W, (hf + 1) * 4 * W)], dma=True)

    for c in range(8):
        sl = (c % 2) * W
        S.op("scalar", lambda E, c=c, sl=sl: E.copy(xbf[:, sl:sl + w], x_sub[:, c * W:c * W + w]),
             r=[T("x_sub", c * W, c * W + w)], w=[T("xbf", sl, sl + w)])
        S.op("vector", lambda E, sl=sl: E.tensor_tensor(xpt[:, sl:sl + w], xbf[:, sl:sl + w],
                                                        xbf[:, sl:sl + w], ALU.mult),
             r=[T("xbf", sl, sl + w)], w=[T("sg", sl, sl + w)])
        S.op("tensor", lambda E, c=c, sl=sl: E.matmul(P4[0:1, 0:w], B["onesb"][:, :], xbf[:, sl:sl + w],
                                                      start=(c == 0), stop=(c == 7)),
             r=[T("xbf", sl, sl + w), B["onesb"].full], w=[T("P4", 0, w)])
        S.op("tensor", lambda E, c=c, sl=sl: E.matmul(P5[0:1, 0:w], B["onesb"][:, :], xpt[:, sl:sl + w],
                                                      start=(c == 0), stop=(c == 7)),
             r=[T("sg", sl, sl + w), B["onesb"].full], w=[T("P5", 0, w)])
    S.op("vector", lambda E: E.tensor_scalar(row_v[0:1, 0:w], P4[0:1, 0:w], 1.0 / DM, None, ALU.mult),
         r=[T("P4", 0, w)], w=[T("row_v", 0, w)])
    S.op("scalar", lambda E: E.copy(row_mb[0:1, 0:w], row_v[0:1, 0:w]),
         r=[T("row_v", 0, w)], w=[T("row_mb", 0, w)])
    S.op("vector", lambda E: E.tensor_tensor(row_rb[0:1, 0:w], row_v[0:1, 0:w], row_v[0:1, 0:w],
                                             ALU.mult),
         r=[T("row_v", 0, w)], w=[T("row_rb", 0, w)])
    S.op("vector", lambda E: E.tensor_scalar(row_v[0:1, 0:w], P5[0:1, 0:w], 1.0 / DM, None, ALU.mult),
         r=[T("P5", 0, w)], w=[T("row_v", 0, w)])
    S.op("vector", lambda E: E.tensor_tensor(row_v[0:1, 0:w], row_v[0:1, 0:w], row_rb[0:1, 0:w],
                                             ALU.subtract),
         r=[T("row_v", 0, w), T("row_rb", 0, w)], w=[T("row_v", 0, w)])
    S.op("scalar", lambda E: E.activation(row_v[0:1, 0:w], row_v[0:1, 0:w], AF.Ln,
                                          bias=B["epsc"][0:1, 0:1]),
         r=[T("row_v", 0, w), B["epsc"].full], w=[T("row_v", 0, w)])
    S.op("scalar", lambda E: E.activation(row_v[0:1, 0:w], row_v[0:1, 0:w], AF.Exp,
                                          scale=B["mhalf"][0:1, 0:1]),
         r=[T("row_v", 0, w), B["mhalf"].full], w=[T("row_v", 0, w)])
    S.op("scalar", lambda E: E.copy(row_rb[0:1, 0:w], row_v[0:1, 0:w]),
         r=[T("row_v", 0, w)], w=[T("row_rb", 0, w)])
    S.op("tensor", lambda E: E.matmul(P4[:, 0:w], B["ones1b"][0:1, :], row_mb[0:1, 0:w],
                                      start=True, stop=True),
         r=[T("row_mb", 0, w), B["ones1b"].full], w=[T("P4", 0, w)])
    S.op("tensor", lambda E: E.matmul(P5[:, 0:w], B["ones1b"][0:1, :], row_rb[0:1, 0:w],
                                      start=True, stop=True),
         r=[T("row_rb", 0, w), B["ones1b"].full], w=[T("P5", 0, w)])
    for c in range(8):
        sl = (c % 2) * W
        S.op("vector", lambda E, c=c, sl=sl: E.tensor_tensor(xbf[:, sl:sl + w],
                                                             x_sub[:, c * W:c * W + w],
                                                             P4[:, 0:w], ALU.subtract),
             r=[T("x_sub", c * W, c * W + w), T("P4", 0, w)], w=[T("xbf", sl, sl + w)])
        S.op("vector", lambda E, c=c, sl=sl: E.tensor_tensor(xn1[:, c * W:c * W + w],
                                                             xbf[:, sl:sl + w],
                                                             P5[:, 0:w], ALU.mult),
             r=[T("xbf", sl, sl + w), T("P5", 0, w)], w=[T("xn1", c * W, c * W + w)])

    for hc in range(24):
        sl = (hc % 2) * W
        for k in range(8):
            S.op("tensor", lambda E, hc=hc, k=k: E.matmul(
                P0[:, 0:w], B["wg_b"][:, k * 3072 + hc * 128:k * 3072 + (hc + 1) * 128],
                xn1[:, k * W:k * W + w], start=(k == 0), stop=(k == 7)),
                r=[T("wg_b", k * 3072 + hc * 128, k * 3072 + (hc + 1) * 128),
                   T("xn1", k * W, k * W + w)], w=[T("P0", 0, w)])
        for k in range(8):
            S.op("tensor", lambda E, hc=hc, k=k: E.matmul(
                P1[:, 0:w], B["wu_b"][:, k * 3072 + hc * 128:k * 3072 + (hc + 1) * 128],
                xn1[:, k * W:k * W + w], start=(k == 0), stop=(k == 7)),
                r=[T("wu_b", k * 3072 + hc * 128, k * 3072 + (hc + 1) * 128),
                   T("xn1", k * W, k * W + w)], w=[T("P1", 0, w)])
        S.op("scalar", lambda E, sl=sl: E.activation(sg[:, sl:sl + w], P0[:, 0:w], AF.Sigmoid,
                                                     scale=B["c127"][:, 0:1]),
             r=[T("P0", 0, w), B["c127"].full], w=[T("sg", sl, sl + w)])
        S.op("vector", lambda E, sl=sl: E.tensor_tensor(tt1[:, sl:sl + w], P0[:, 0:w],
                                                        sg[:, sl:sl + w], ALU.mult),
             r=[T("P0", 0, w), T("sg", sl, sl + w)], w=[T("xbf", sl, sl + w)])
        S.op("vector", lambda E, hc=hc, sl=sl: E.tensor_tensor(hb[:, hc * W:hc * W + w],
                                                               tt1[:, sl:sl + w],
                                                               P1[:, 0:w], ALU.mult),
             r=[T("xbf", sl, sl + w), T("P1", 0, w)], w=[T("hb", hc * W, hc * W + w)])

    for dc in range(8):
        pz = ("P2", "P3")[dc % 2]
        PZ = B[pz]
        sl = (dc % 2) * W
        if use_s2s:
            S.op("sync", lambda E, dc=dc, sl=sl: E.dma_start(
                s2s[:, sl:sl + w], s2_hbm[dc * 128:(dc + 1) * 128, ls:ls + w]),
                w=[T("s2s", sl, sl + w)], dma=True)
        for hc in range(24):
            S.op("tensor", lambda E, dc=dc, hc=hc, PZ=PZ: E.matmul(
                PZ[:, 0:w], B["wd_b"][:, hc * 1024 + dc * 128:hc * 1024 + (dc + 1) * 128],
                hb[:, hc * W:hc * W + w], start=(hc == 0), stop=(hc == 23)),
                r=[T("wd_b", hc * 1024 + dc * 128, hc * 1024 + (dc + 1) * 128),
                   T("hb", hc * W, hc * W + w)], w=[T(pz, 0, w)])
        S.op("vector", lambda E, pz=pz, PZ=PZ, sl=sl: E.tensor_scalar(tt1[:, sl:sl + w], PZ[:, 0:w],
                                                                      B["k3c"][:, 0:1], None, ALU.mult),
             r=[T(pz, 0, w), B["k3c"].full], w=[T("xbf", sl, sl + w)])
        S.op("scalar", lambda E, sl=sl: E.activation(tt1[:, sl:sl + w], tt1[:, sl:sl + w], AF.Silu),
             r=[T("xbf", sl, sl + w)], w=[T("xbf", sl, sl + w)])
        if use_s2s:
            S.op("vector", lambda E, dc=dc, sl=sl: E.tensor_tensor(tt1[:, sl:sl + w], tt1[:, sl:sl + w],
                                                                   s2s[:, sl:sl + w],
                                                                   ALU.mult),
                 r=[T("xbf", sl, sl + w), T("s2s", sl, sl + w)], w=[T("xbf", sl, sl + w)])
        oslot = (dc % 4) * W
        S.op("vector", lambda E, dc=dc, sl=sl, oslot=oslot: E.tensor_tensor(
            outf[:, oslot:oslot + w], tt1[:, sl:sl + w],
            x_sub[:, dc * W:dc * W + w], ALU.add),
            r=[T("xbf", sl, sl + w), T("x_sub", dc * W, dc * W + w)],
            w=[T("outf", oslot, oslot + w)])
        if dc % 4 == 3:
            ob = dc // 4
            dsty = out_hbm.rearrange("(c p) t -> p c t", p=128)[:, ob * 4:(ob + 1) * 4, ls:ls + w]
            srcy = outf.h.ap().rearrange("p (c t) -> p c t", c=4)[:, :, 0:w]
            S.op("scalar", lambda E, d=dsty, s=srcy: E.dma_start(d, s), r=[outf.full],
                 dma=True, tag="out_dma")



W2 = 512
NSUB2 = TCORE // W2

L1B_HI_SPECS = [
    ("wgt", [1024, 3072], I32), ("wut", [1024, 3072], I32), ("wdt", [3072, 1024], I32),
]


def build_fused():
    nc = bass.Bass("TRN2", target_bir_lowering=False)
    S = Sched()
    stack = contextlib.ExitStack()
    with stack:
        x_hbm = nc.dram_tensor("x_fm", [DM, TLOC], F32, kind="ExternalInput")
        x1_hbm = nc.dram_tensor("x1_fm", [DM, TLOC], BF16, kind="Internal")
        s2_hbm = nc.dram_tensor("s2_fm", [DM, TCORE], BF16, kind="Internal")
        out_hbm = nc.dram_tensor("out_fm", [DM, TCORE], F32, kind="ExternalOutput")
        HI0 = {n: nc.dram_tensor(n, sh, dty, kind="ExternalInput") for n, sh, dty in L0_HI_SPECS}
        HI1 = {n: nc.dram_tensor(n, sh, dty, kind="ExternalInput")
               for n, sh, dty in L1A_HI_SPECS if n != "idb"}
        HI1["idb"] = HI0["idb"]
        HI2 = {n: nc.dram_tensor(n, sh, dty, kind="ExternalInput") for n, sh, dty in L1B_HI_SPECS}

        PS = {}
        for nm in ["P0", "P1", "P2", "P3", "P4", "P5"]:
            PS[nm] = TBuf(stack, nc, nm, [128, 512], F32, psum=True)
        PS["PTf"] = TBuf(stack, nc, "PTf", [128, 128], F32, psum=True)
        PS["PTb"] = TBuf(stack, nc, "PTb", [128, 1024], BF16, psum=True)

        ar = Arena(stack, nc, 212736)

        B = dict(PS)

        def add(name, shape, dtype):
            B[name] = ar.carve(name, shape, dtype)

        for name, shape, dtype in L0_BUFS:
            add(name, shape, dtype)
        B["yf"] = B["stage"]
        B["xnf"] = B["stage"]
        emit_l0_inner(S, nc, B, x_hbm, x1_hbm, HI0)
        S.fence()

        ar.reset()
        wd_tv = ar.carve_top("wd_b", [128, 24 * 1024], BF16)
        B = dict(PS)
        for name, shape, dtype in L1A_BUFS:
            add(name, shape, dtype)
        B["wd_b"] = wd_tv
        H1 = dict(HI1)
        H1["wdt"] = HI2["wdt"]
        emit_l1a(S, nc, B, x1_hbm, s2_hbm, H1)
        S.fence()

        ar.reset()
        B = dict(PS)
        for name, shape, dtype in L1B_BUFS:
            add(name, shape, dtype)
        B["wd_b"] = wd_tv
        emit_l1b(S, nc, B, x1_hbm, s2_hbm, out_hbm, HI2, nsubs=NSUB2, W=W2)

        outs = [op["id"] for op in S.ops if op.get("tag") == "out_dma"]
        S.mark_inc(outs)
        S.op("sync", lambda E: E.nop(), deps=outs)
        S.build(nc, stack)
    return nc, S.stats


L0_BUFS = [
    ("wzt_b", [128, 8 * 2048], BF16), ("wot_b", [128, 16 * DM], BF16),
    ("cw1_b", [128, 8 * 256], BF16), ("cw2_b", [128, 2 * 2], BF16),
    ("stage", [128, 2 * 2048], F32),
    ("bdt_s", [128, 16], F32), ("dskip_s", [128, 16], F32),
    ("cb1_sb", [128, 2], F32), ("cb2_s", [2, 1], F32),
    ("sigq_b", [64, DM], BF16), ("sigsT", [128, 8 * 64], F32),
    ("csig_b", [8, DM], BF16), ("csigT", [128, 8 * 8], F32),
    ("dirs_b", [64, DM], BF16),
    ("sw1_b", [64, 32], BF16), ("sb1_b", [64, 16], BF16),
    ("sw2_b", [64, 16], BF16), ("sb2_b", [64, 1], BF16),
    ("gam_s", [128, 8], F32), ("bet_s", [128, 8], F32),
    ("idf_s", [128, 128], F32), ("idb_s", [128, 128], BF16),
    ("sel01_s", [2, 64], BF16),
    ("onesb", [128, 1], BF16), ("ones1f", [1, 128], F32), ("ones1b", [1, 128], BF16),
    ("ones16b", [16, 1], BF16),
    ("knots_s", [1, 8], F32), ("dk_row", [1, 7], F32),
    ("kn_b", [128, 8], F32), ("dk_b", [128, 7], F32),
    ("itmp", [1, 1], F32), ("it_b", [128, 1], F32), ("osc_b", [128, 1], F32),
    ("epsc", [128, 1], F32), ("mhalf", [128, 1], F32), ("mone", [128, 1], F32),
    ("rsq2", [128, 1], F32), ("onef", [128, 1], F32),
    ("osc_s", [1, 1], F32), ("toff_s", [1, 1], F32), ("toff_b", [128, 1], F32),
    ("iota8", [128, 8], F32), ("iota8n", [128, 8], F32), ("iota64", [64, 1], F32),
    ("iotaw", [128, SUBW], F32),
    ("grpwk", [64, 8], F32), ("grpwk2", [64, 8], F32), ("grp_b", [64, 8], BF16),
    ("bcrow", [1, 32], F32),
    ("a_all", [128, 16 * 16], F32), ("cw_all", [128, 16 * 16], F32),
    ("base_all", [128, 16], F32), ("dcol", [128, 16], F32),
    ("bb", [128, 16], F32), ("cb", [128, 16], F32),
    ("tmp16", [128, 16], F32), ("tmp16c", [128, 16], F32),
    ("dbc", [96, 1], F32), ("dltb", [64, 1], BF16),
    ("x_sub", [128, 8 * SUBW], F32), ("xbf", [128, 2 * SUBW], BF16),
    ("xn", [128, 8 * SUBW], BF16),
    ("it2", [1, 2], F32),
    ("row_m", [1, SUBW], F32), ("row_rb", [1, SUBW], BF16), ("row_mb", [1, SUBW], BF16),
    ("ball", [128, 16], F32), ("tv", [128, 1], F32),
    ("pt", [128, 32 * 264], BF16),
    ("lnK", [128, 256], F32), ("sgn", [128, 256], F32), ("sgd", [128, 256], F32),
    ("dA256", [128, 256], F32), ("c256", [128, 1], F32), ("eye8", [128, 512], BF16),
    ("diagb", [128, 16 * 128], BF16),
    ("ball2", [128, 32], F32),
    ("y_all", [128, 16 * SUBW], BF16), ("zs2", [128, 2 * SUBW], BF16),
    ("s_all", [128, 8 * SUBW], BF16),
    ("gates", [72, SUBW], F32),
    ("gcT", [128, 4 * 8], F32), ("tgT", [128, 4 * 64], F32), ("rmax", [128, 4], F32),
    ("eq8", [128, 8], F32), ("midx", [128, 8], F32), ("cidx", [128, 4], F32),
    ("ohc", [128, 8], F32), ("selg", [128, 8], F32), ("lidx", [128, 1], F32),
    ("tidx", [128, 4], F32), ("trowb", [1, SUBW], BF16), ("ohT", [64, SUBW], BF16),
    ("w1s", [16, 2 * SUBW], BF16), ("b1s", [16, SUBW], BF16),
    ("w2s", [16, SUBW], BF16), ("b2s", [1, SUBW], BF16),
    ("comp2b", [2, SUBW], BF16),
    ("hh1", [16, SUBW], BF16), ("hh2", [16, SUBW], BF16), ("hhp", [16, SUBW], BF16),
    ("magb", [1, SUBW], BF16),
    ("h1x", [128, 2 * SUBW], BF16), ("h1e", [128, SUBW], BF16),
    ("dirsel", [128, 2 * SUBW], F32), ("ybf", [128, 2 * SUBW], BF16),
    ("gsil", [128, 2 * SUBW], BF16),
]

L1A_BUFS = [
    ("wq_b", [128, 8 * 1024], BF16), ("wk_b", [128, 8 * 1024], BF16),
    ("wv_b", [128, 8 * 1024], BF16), ("wo_b", [128, 8 * 1024], BF16),
    ("x_sub", [128, 8 * SUBW], BF16),
    ("xbf", [128, 2 * SUBW], BF16), ("xpt", [128, 2 * SUBW], BF16),
    ("xn1", [128, 8 * 520], BF16),
    ("row_m", [1, SUBW], F32), ("row_v", [1, SUBW], F32), ("row_r", [1, SUBW], F32),
    ("row_rb", [1, SUBW], BF16), ("row_mb", [1, SUBW], BF16),
    ("onesb", [128, 1], BF16), ("ones1b", [1, 128], BF16),
    ("epsc", [128, 1], F32), ("mhalf", [128, 1], F32),
    ("idb_s", [128, 128], BF16),
    ("mask0", [128, 136], F32), ("sc2t", [128, 1], F32), ("maskr", [128, 136], F32),
    ("qf", [128, 8 * SUBW], BF16), ("kf", [128, 8 * 520], BF16),
    ("vt0", [128, 1024], BF16), ("vt1", [128, 1024], BF16),
    ("vt2", [128, 1024], BF16), ("vt3", [128, 1024], BF16),
    ("vnx", [8, 1024], BF16),
    ("sm", [128, 4 * 136], F32), ("rmx", [128, 4], F32), ("nmx", [128, 4], F32),
    ("ebf", [128, 4 * 136], BF16), ("esum", [128, 4], F32), ("erec", [128, 4], F32),
    ("eta", [8, 4 * 128], BF16), ("etb", [128, 4 * 128], BF16),
    ("att", [128, 4 * 1024], BF16), ("atf", [128, 8 * SUBW], BF16),
    ("s2", [128, 8 * SUBW], BF16),
]

L1B_BUFS = [
    ("wg_b", [128, 8 * 3072], BF16), ("wu_b", [128, 8 * 3072], BF16),
    ("x_sub", [128, 8 * W2], BF16),
    ("xbf", [128, 2 * W2], BF16),
    ("xn1", [128, 8 * W2], BF16),
    ("s2s", [128, 2 * W2], BF16),
    ("hb", [128, 24 * W2], BF16),
    ("sg", [128, 2 * W2], BF16),
    ("outf", [128, 4 * W2], F32),
    ("row_v", [1, W2], F32), ("row_rb", [1, W2], BF16), ("row_mb", [1, W2], BF16),
    ("onesb", [128, 1], BF16), ("ones1b", [1, 128], BF16),
    ("epsc", [128, 1], F32), ("mhalf", [128, 1], F32),
    ("c127", [128, 1], F32), ("k3c", [128, 1], F32),
]


def l0_in_maps(inputs):
    x = np.ascontiguousarray(inputs["L0_attn_x"].reshape(SEQ, DM).T)
    iota_plain = np.broadcast_to(np.arange(SUBW, dtype=np.float32), (128, SUBW)).copy()
    iota_c0 = np.broadcast_to(np.maximum(np.arange(SUBW), 7).astype(np.float32), (128, SUBW)).copy()
    shared = dict(
        wzt=np.ascontiguousarray(inputs["L0_attn_W_in"][2048:4096, :].T),
        wx=np.ascontiguousarray(inputs["L0_attn_W_x"]),
        wdtt=np.ascontiguousarray(inputs["L0_attn_W_dt"].T),
        convw=np.ascontiguousarray(inputs["L0_attn_conv_w"].reshape(16, 128, 4).transpose(1, 0, 2).reshape(128, 64)),
        convb=np.ascontiguousarray(inputs["L0_attn_conv_b"].reshape(16, 128).T),
        bdt=np.ascontiguousarray(inputs["L0_attn_b_dt"].reshape(16, 128).T),
        alog=np.ascontiguousarray(inputs["L0_attn_A_log"]),
        dskip=np.ascontiguousarray(inputs["L0_attn_D_skip"].reshape(16, 128).T),
        wot=np.ascontiguousarray(inputs["L0_attn_W_out"].T),
        sigr=np.ascontiguousarray(inputs["L0_ffn_signatures_raw"]),
        knots=np.ascontiguousarray(inputs["L0_ffn_knot_values"].reshape(1, 8)),
        temp=np.ascontiguousarray(inputs["L0_ffn_temperature"].reshape(1, 1)),
        cw1=np.ascontiguousarray(inputs["L0_ffn_compress_w1"]),
        cb1=np.ascontiguousarray(inputs["L0_ffn_compress_b1"].reshape(2, 128).T),
        cw2=np.ascontiguousarray(inputs["L0_ffn_compress_w2"]),
        cb2=np.ascontiguousarray(inputs["L0_ffn_compress_b2"].reshape(2, 1)),
        sw1=np.ascontiguousarray(inputs["L0_ffn_spline_w1"].reshape(64, 32)),
        sb1=np.ascontiguousarray(inputs["L0_ffn_spline_b1"]),
        sw2=np.ascontiguousarray(inputs["L0_ffn_spline_w2"].reshape(64, 16)),
        sb2=np.ascontiguousarray(inputs["L0_ffn_spline_b2"]),
        dirs=np.ascontiguousarray(inputs["L0_ffn_directions"]),
        gam=np.ascontiguousarray(inputs["L0_ffn_norm_gamma"].reshape(8, 128).T),
        bet=np.ascontiguousarray(inputs["L0_ffn_norm_beta"].reshape(8, 128).T),
        osc=np.ascontiguousarray(inputs["L0_ffn_output_scale"].reshape(1, 1)),
        idf=np.eye(128, dtype=np.float32),
        idb=np.eye(128).astype(BF),
    )
    sel01 = np.zeros((2, 64), dtype=BF)
    sel01[0, 0:16] = 1
    sel01[1, 32:48] = 1
    shared["sel01"] = sel01
    maps = []
    for c in range(N_CORES):
        t0 = c * TCORE
        xs = np.zeros((DM, TLOC), dtype=np.float32)
        lo = t0 - HALO
        if lo < 0:
            xs[:, HALO:] = x[:, t0:t0 + TCORE]
        else:
            xs[:, :] = x[:, lo:t0 + TCORE]
        m = dict(shared)
        m["x_fm"] = xs
        m["toff"] = np.full((1, 1), float(t0 - HALO), dtype=np.float32)
        m["iotaw0"] = iota_c0 if c == 0 else iota_plain
        maps.append(m)
    return maps


def kernel(**inputs):
    nc, _ = build_fused()
    maps = l0_in_maps(inputs)

    band = np.full((128, 136), NEG, dtype=np.float32)
    i = np.arange(128)[:, None]
    c = np.arange(136)[None, :]
    band[(c - i >= 0) & (c - i <= 8)] = 0.0
    band0 = band.copy()
    band0[:, :8] = NEG
    l1 = dict(
        wq=np.ascontiguousarray(inputs["L1_attn_Wq"]),
        wk=np.ascontiguousarray(inputs["L1_attn_Wk"]),
        wv=np.ascontiguousarray(inputs["L1_attn_Wv"]),
        wo=np.ascontiguousarray(inputs["L1_attn_Wo"]),
        maskr_h=band,
        wgt=np.ascontiguousarray(inputs["L1_ffn_w_gate"].T),
        wut=np.ascontiguousarray(inputs["L1_ffn_w_up"].T),
        wdt=np.ascontiguousarray(inputs["L1_ffn_w_down"].T),
    )
    for cix in range(N_CORES):
        maps[cix].update(l1)
        maps[cix]["mask0_h"] = band0 if cix == 0 else band

    res = run_bass_kernel_spmd(nc, maps, core_ids=list(range(N_CORES)))
    out = np.concatenate([np.asarray(r["out_fm"]) for r in res.results], axis=1)
    return np.ascontiguousarray(out.T).reshape(1, SEQ, DM).astype(np.float32)
